# version 13
# baseline (speedup 1.0000x reference)
import numpy as np
from contextlib import ExitStack

import concourse.bass as bass
import concourse.mybir as mybir
from concourse.bass_utils import run_bass_kernel_spmd

F32 = mybir.dt.float32
BF16 = mybir.dt.bfloat16
AF = mybir.ActivationFunctionType
ALU = mybir.AluOpType
AX = mybir.AxisListType

D = 2048
NT = 2048
CH = 128
TCC = 16
TPT = 8
NH = 16
HD = 128
QH = 4
NQ = 4
NDC = 16
EPS_LN = 1e-5
EPS_RMS = 1e-6
RECIP_D = 1.0 / 2048.0

ENGINES = ("sync", "scalar", "vector", "tensor", "gpsimd")
RACE_DETECT = False
PE_DENSE_INC = False
TRUNCATE = None
EVAC_MODE = 0
SKIP_CC = False
PRELUDE_CUT = 0

_LL = -(2.0 ** (-8.0 * (np.arange(NH) + 1.0) / NH))
_LAMC = np.exp(_LL * CH)


class Sem:
    __slots__ = ("h", "n")

    def __init__(self, h):
        self.h = h
        self.n = 0


def build_program(debug=()):
    nc = bass.Bass("TRN2", target_bir_lowering=False, debug=False,
                   detect_race_conditions=RACE_DETECT)

    x_sl = nc.dram_tensor("x_sl", [NT, D], F32, kind="ExternalInput")
    wq = nc.dram_tensor("wq", [D, D], F32, kind="ExternalInput")
    wk = nc.dram_tensor("wk", [D, D], F32, kind="ExternalInput")
    wv = nc.dram_tensor("wv", [D, D], F32, kind="ExternalInput")
    wo = nc.dram_tensor("wo", [D, D], F32, kind="ExternalInput")
    wg1 = nc.dram_tensor("wg1", [D, HD], F32, kind="ExternalInput")
    wg2 = nc.dram_tensor("wg2", [HD, D], F32, kind="ExternalInput")
    nw_in = nc.dram_tensor("nw_in", [D], F32, kind="ExternalInput")
    bias_in = nc.dram_tensor("bias_in", [1, D], F32, kind="ExternalInput")
    fac_sl = nc.dram_tensor("fac_sl", [8, D, 256], F32, kind="ExternalInput")
    cos_in = nc.dram_tensor("cos_in", [NT, 64], F32, kind="ExternalInput")
    sin_in = nc.dram_tensor("sin_in", [NT, 64], F32, kind="ExternalInput")
    dmask_in = nc.dram_tensor("dmask_in", [CH, D], F32, kind="ExternalInput")
    dkt_in = nc.dram_tensor("dkt_in", [CH, D], F32, kind="ExternalInput")
    dqv_in = nc.dram_tensor("dqv_in", [CH, NH], F32, kind="ExternalInput")
    corr_in = nc.dram_tensor("corr_in", [CH, TCC * NH], F32, kind="ExternalInput")
    idn_in = nc.dram_tensor("idn_in", [128, 128], F32, kind="ExternalInput")
    lamc_in = nc.dram_tensor("lamc_in", [CH, NH], F32, kind="ExternalInput")
    out_sl = nc.dram_tensor("out_sl", [NT, D], F32, kind="ExternalOutput")

    def dscratch(name, shape, dtype):
        kind = "ExternalOutput" if name in debug else "Internal"
        return nc.dram_tensor(name, shape, dtype, kind=kind)

    o_spill = dscratch("o_spill", [TCC, CH, D], BF16)
    q_spill = dscratch("q_spill", [TCC, NH, HD, CH], BF16)
    cc_fs_in = nc.dram_tensor("cc_fs_in", [D, 256], F32)
    cc_fs_out = nc.dram_tensor("cc_fs_out", [8 * D, 256], F32)
    cc_s_in = [nc.dram_tensor(f"cc_s_in{g}", [HD, QH * HD], F32) for g in range(NQ)]
    cc_s_out = [nc.dram_tensor(f"cc_s_out{g}", [2 * HD, QH * HD], F32)
                for g in range(NQ)]
    dbg = {}
    for name, shape, dt in (
        ("d_xlnfm", [128, NDC * 1024], BF16),
        ("d_g1", [128, NT], BF16),
        ("d_s16", [128, NH * HD], F32),
        ("d_gats", [128, TPT * D], BF16),
        ("d_ogfm", [128, NDC * 1024], BF16),
        ("d_xln2fm", [128, NDC * 1024], BF16),
        ("d_x2", [128, TPT * D], F32),
        ("d_stats", [128, 64], F32),
        ("d_stats2", [128, 64], F32),
        ("d_stats3", [128, 64], F32),
    ):
        if name in debug:
            dbg[name] = nc.dram_tensor(name, shape, dt, kind="ExternalOutput")

    top = ExitStack()

    def mk(stk, name, shape, dtype):
        return stk.enter_context(nc.sbuf_tensor(name, shape, dtype))

    def mkp(stk, name, shape, dtype):
        return stk.enter_context(nc.psum_tensor(name, shape, dtype))

    T = {}
    T["XLNFM"] = mk(top, "XLNFM", [128, NDC * 1024], BF16)
    T["G1"] = mk(top, "G1", [128, NT], BF16)
    T["WG2B"] = mk(top, "WG2B", [HD, D], BF16)
    T["WG1B"] = mk(top, "WG1B", [128, NDC * HD], BF16)
    T["NW"] = mk(top, "NW", [128, 16], F32)
    T["BIASB"] = mk(top, "BIASB", [1, D], BF16)
    T["ONES"] = mk(top, "ONES", [1, 128], BF16)
    T["IDN"] = mk(top, "IDN", [128, 128], BF16)
    T["DQV"] = mk(top, "DQV", [128, NH], F32)
    T["CORR"] = mk(top, "CORR", [128, TCC * NH], F32)
    T["STATS"] = mk(top, "STATS", [128, 64], F32)
    T["S2S"] = mk(top, "S2S", [128, 16], F32)
    T["Q2S"] = mk(top, "Q2S", [128, 16], F32)
    T["MS"] = mk(top, "MS", [128, 16], F32)
    T["SD2"] = mk(top, "SD2", [128, 16], F32)
    T["R2"] = mk(top, "R2", [128, 16], F32)
    T["EPSC"] = mk(top, "EPSC", [128, 4], F32)
    T["DUMT"] = mk(top, "DUMT", [128, 1], F32)
    T["LAMCT"] = mk(top, "LAMCT", [128, NH], F32)

    qs = ExitStack()
    T["COS"] = mk(qs, "COS", [128, TCC * 64], BF16)
    T["SIN"] = mk(qs, "SIN", [128, TCC * 64], BF16)
    T["DMASK"] = mk(qs, "DMASK", [CH, D], BF16)
    T["DKT"] = mk(qs, "DKT", [CH, D], BF16)
    T["S"] = mk(qs, "S", [128, NH * HD], F32)
    T["SB"] = mk(qs, "SB", [128, NH * HD], BF16)
    T["XCH"] = [mk(qs, f"XCH{i}", [128, D], F32) for i in range(2)]
    T["XLNT"] = [mk(qs, f"XLNT{i}", [128, D], BF16) for i in range(2)]
    T["TRASH"] = mk(qs, "TRASH", [128, D], BF16)
    T["WQ"] = mk(qs, "WQ", [128, NDC * 512], BF16)
    T["WK"] = mk(qs, "WK", [128, NDC * 512], BF16)
    T["WV"] = mk(qs, "WV", [128, NDC * 512], BF16)
    T["QSB"] = [mk(qs, f"QSB{i}", [128, 512], BF16) for i in range(2)]
    T["QTM"] = [mk(qs, f"QTM{i}", [128, 512], BF16) for i in range(2)]
    T["TMP1"] = mk(qs, "TMP1", [128, QH * 64], F32)
    T["TMP2"] = mk(qs, "TMP2", [128, QH * 64], F32)
    T["QFM"] = mk(qs, "QFM", [128, QH * 1024], BF16)
    T["KFM"] = mk(qs, "KFM", [128, QH * 1024], BF16)
    T["K2"] = mk(qs, "K2", [128, TPT * 512], BF16)
    T["V"] = mk(qs, "V", [128, TPT * 512], BF16)
    T["SCT"] = [mk(qs, f"SCT{i}", [128, 512], BF16) for i in range(2)]
    T["TMPOE"] = mk(qs, "TMPOE", [128, 512], F32)
    T["OQ"] = mk(qs, "OQ", [128, TPT * 512], BF16)
    qps = ExitStack()
    T["PP"] = [mkp(qps, f"PP{i}", [128, 512], F32) for i in range(2)]
    T["PT"] = mkp(qps, "PT", [128, D], BF16)
    T["PSC"] = mkp(qps, "PSC", [128, 512], F32)
    T["POI"] = mkp(qps, "POI", [128, 512], F32)
    T["POE"] = mkp(qps, "POE", [128, 512], F32)
    T["PKV"] = mkp(qps, "PKV", [128, 512], F32)
    qps.close()
    qs.close()

    for tile in range(2):
        ts = ExitStack()
        T[f"S16ST{tile}"] = mk(ts, f"S16ST{tile}", [128, NH * HD], F32)
        T[f"S16B{tile}"] = mk(ts, f"S16B{tile}", [128, NH * HD], BF16)
        T[f"OCH{tile}"] = mk(ts, f"OCH{tile}", [128, TPT * D], BF16)
        T[f"QCH{tile}"] = [mk(ts, f"QCH{tile}_{i}", [128, NH * CH], BF16)
                           for i in range(2)]
        T[f"TMPC{tile}"] = mk(ts, f"TMPC{tile}", [128, 512], F32)
        T[f"GATS{tile}"] = mk(ts, f"GATS{tile}", [128, TPT * D], BF16)
        T[f"XG{tile}"] = mk(ts, f"XG{tile}", [128, D], F32)
        T[f"SQB{tile}"] = mk(ts, f"SQB{tile}", [128, D], F32)
        T[f"OGT{tile}"] = mk(ts, f"OGT{tile}", [128, D], BF16)
        aps = ExitStack()
        T[f"PCORR{tile}"] = [mkp(aps, f"PCORR{tile}_{i}", [128, 512], F32)
                             for i in range(2)]
        T[f"PGATE{tile}"] = mkp(aps, f"PGATE{tile}", [128, D], F32)
        T[f"PT2{tile}"] = mkp(aps, f"PT2{tile}", [128, D], BF16)
        aps.close()
        ts.close()
        bs = ExitStack()
        T[f"XLN2FM{tile}"] = mk(bs, f"XLN2FM{tile}", [128, NDC * 1024], BF16)
        T[f"WOH{tile}"] = mk(bs, f"WOH{tile}", [128, NDC * 1024], BF16)
        T[f"XRCH{tile}"] = [mk(bs, f"XRCH{tile}_{i}", [128, 1024], F32)
                            for i in range(2)]
        T[f"X2H{tile}"] = [mk(bs, f"X2H{tile}_{i}", [128, 1024], F32)
                           for i in range(2)]
        T[f"TRASH2{tile}"] = mk(bs, f"TRASH2{tile}", [128, 1024], F32)
        bps = ExitStack()
        T[f"PB{tile}"] = [mkp(bps, f"PB{tile}_{i}", [128, 1024], F32)
                          for i in range(2)]
        bps.close()
        cs = ExitStack()
        T[f"XB2{tile}"] = [mk(cs, f"XB2{tile}_{i}", [128, D], F32)
                           for i in range(2)]
        T[f"XLNT2{tile}"] = [mk(cs, f"XLNT2{tile}_{i}", [128, D], BF16)
                             for i in range(2)]
        T[f"FSP{tile}"] = [mk(cs, f"FSP{tile}_{i}", [128, NDC * 512], BF16)
                           for i in range(2)]
        T[f"OUTES{tile}"] = [mk(cs, f"OUTES{tile}_{i}", [128, 4 * 512], F32)
                             for i in range(2)]
        cps = ExitStack()
        T[f"PT3{tile}"] = mkp(cps, f"PT3{tile}", [128, D], BF16)
        T[f"PF{tile}"] = [mkp(cps, f"PF{tile}_{i}", [128, 512], F32)
                          for i in range(6)]
        cps.close()
        cs.close()
        bs.close()

    NSX, NGP = 8, 8
    SXP = [Sem(top.enter_context(nc.semaphore(f"dsx{i}"))) for i in range(NSX)]
    GPP = [Sem(top.enter_context(nc.semaphore(f"dgp{i}"))) for i in range(NGP)]
    CCP = [Sem(top.enter_context(nc.semaphore(f"dcc{i}"))) for i in range(5)]
    CPE = Sem(top.enter_context(nc.semaphore("cpe")))

    class ChainSem:
        W = 4

        def __init__(self, name):
            self.sems = [Sem(top.enter_context(nc.semaphore(f"{name}{i}")))
                         for i in range(self.W)]
            self.n = 0

        def mark(self):
            if self.n == 0:
                return (self.sems[0], 0)
            i = self.n - 1
            return (self.sems[i % self.W], i // self.W + 1)

    CACT = ChainSem("cact")
    CDVE = ChainSem("cdve")
    ALLSEMS = tuple(SXP) + tuple(GPP) + tuple(CCP) + (CPE,)         + tuple(CACT.sems) + tuple(CDVE.sems)

    def walk(eng_name, E):
        for s in ALLSEMS:
            s.n = 0
        CACT.n = 0
        CDVE.n = 0

        SELF = {"vector": CDVE, "scalar": CACT}
        last_inst = {e: None for e in ENGINES}
        last_mark = {e: None for e in ENGINES}

        def op(engine, method, *a, inc=None, incn=1, waits=(), **k):
            if PE_DENSE_INC and engine == "tensor":
                assert inc is None or inc is CPE
                CPE.n += 1
                if engine == eng_name:
                    for s_, v_ in waits:
                        if s_ is not None:
                            E.wait_ge(s_.h, v_)
                    instr = getattr(E, method)(*a, **k)
                    instr.then_inc(CPE.h, 1)
                    last_inst[engine] = instr
                    last_mark[engine] = (CPE, CPE.n)
                return
            selfsem = SELF.get(engine)
            if method == "dma_start":
                selfsem = None
            if selfsem is not None:
                assert inc is None or inc is selfsem
                prevmark = selfsem.mark()
                selfsem.n += 1
                mysem, myval = selfsem.mark()
                if engine == eng_name:
                    for s_, v_ in waits:
                        if s_ is not None:
                            E.wait_ge(s_.h, v_)
                    E.wait_ge(prevmark[0].h, prevmark[1])
                    instr = getattr(E, method)(*a, **k)
                    instr.then_inc(mysem.h, 1)
                    last_inst[engine] = instr
                return
            if inc is not None:
                inc.n += incn
            if engine == eng_name:
                for s_, v_ in waits:
                    if s_ is not None:
                        E.wait_ge(s_.h, v_)
                instr = getattr(E, method)(*a, **k)
                if instr is not None:
                    last_inst[engine] = instr
                    last_mark[engine] = None
                if inc is not None:
                    instr.then_inc(inc.h, incn)
                    last_mark[engine] = (inc, inc.n)

        def inc_last(engine, sem, n=1):
            if SELF.get(engine) is not None:
                assert sem is SELF[engine]
                return
            if PE_DENSE_INC and engine == "tensor":
                assert sem is CPE
                return
            prev = last_mark[engine]
            sem.n += n
            if engine == eng_name:
                if last_inst[engine] is None or prev is not None:
                    if prev is not None:
                        E.wait_ge(prev[0].h, prev[1])
                    E.nop().then_inc(sem.h, n)
                else:
                    last_inst[engine].then_inc(sem.h, n)
            last_mark[engine] = (sem, sem.n)

        qstate = {"sync": [SXP, 0], "gpsimd": [GPP, 0], "scalar": [SXP, 0]}

        def dma(engine, out_ap, in_ap, dsem=None, waits=(), **k):
            pool, idx = qstate[engine]
            s = pool[idx % len(pool)]
            qstate[engine][1] = idx + 1
            op(engine, "dma_start", inc=s, incn=16,
               waits=tuple(waits) + ((s, s.n),),
               out=out_ap, in_=in_ap, **k)
            return (s, s.n)

        def w(sem_):
            if isinstance(sem_, ChainSem):
                return sem_.mark()
            return (sem_, sem_.n)

        def fence():
            marks = []
            for en, cs in (("tensor", CPE), ("scalar", CACT), ("vector", CDVE)):
                if en == "tensor":
                    inc_last(en, cs)
                marks.append((en, w(cs)))
            dm = [(s, s.n) for s in SXP + GPP]
            for en in ENGINES:
                for other, mk_ in marks:
                    if other != en:
                        op(en, "nop", waits=(mk_,))
                for mk_ in dm:
                    op(en, "nop", waits=(mk_,))

        WAITS_NONE = ()

        ln_evac = {}
        ln_tp = {}
        ln_chunk_evac = {}
        g1_done = {}
        marks = {}
        s16_sent = [None] * NQ
        spill_o = {}
        spill_q = {}

        idn_dma = dma("gpsimd", T["IDN"][:, :], idn_in.ap())
        cos_dma = dma("gpsimd",
                      T["COS"][:].rearrange("p (tc f) -> p tc f", tc=TCC),
                      cos_in.ap().rearrange("(tc p) f -> p tc f", p=128))
        sin_dma = dma("gpsimd",
                      T["SIN"][:].rearrange("p (tc f) -> p tc f", tc=TCC),
                      sin_in.ap().rearrange("(tc p) f -> p tc f", p=128))
        dmask_dma = dma("gpsimd", T["DMASK"][:, :], dmask_in.ap())
        dkt_dma = dma("gpsimd", T["DKT"][:, :], dkt_in.ap())
        dqv_dma = dma("gpsimd", T["DQV"][:, :], dqv_in.ap())
        dma("gpsimd", T["CORR"][:, :], corr_in.ap())
        dma("gpsimd", T["NW"][:, :],
            nw_in.ap().rearrange("(dc p) -> p dc", p=128),
            allow_slow_non_contiguous=True)
        dma("gpsimd", T["BIASB"][:, :], bias_in.ap())
        wg1_dma = dma("gpsimd",
                      T["WG1B"][:].rearrange("p (dc r) -> p dc r", dc=NDC),
                      wg1.ap().rearrange("(dc p) r -> p dc r", p=128))
        lamc_dma = dma("gpsimd", T["LAMCT"][:, :], lamc_in.ap())
        consts_dma = dma("gpsimd", T["WG2B"][:, :], wg2.ap())
        op("vector", "memset", T["EPSC"][:, 0:1], EPS_LN)
        op("vector", "memset", T["EPSC"][:, 1:2], EPS_RMS)
        op("vector", "memset", T["EPSC"][:, 2:3], float(1.0 / HD))
        op("vector", "memset", T["ONES"][:, :], 1.0)
        op("vector", "memset", T["S"][:, :], 0.0)
        op("vector", "memset", T["SB"][:, :], 0.0, inc=CDVE)
        minit = w(CDVE)

        for wname, wt, pj in (("WQ", wq, "q"), ("WK", wk, "k"),
                              ("WV", wv, "v")):
            if PRELUDE_CUT == 2:
                marks[("wpre", pj)] = consts_dma
                continue
            marks[("wpre", pj)] = dma(
                "gpsimd",
                T[wname][:].rearrange("p (dc e) -> p dc e", dc=NDC),
                wt[:, 0:512].rearrange("(dc p) e -> p dc e", p=128))

        fprev = {}
        fac_pieces = []
        if PRELUDE_CUT != 1:
            for b in range(16):
                fprev[b] = dma("gpsimd", cc_fs_in[b * 128:(b + 1) * 128, :],
                               fac_sl[0, b * 128:(b + 1) * 128, :])
            fac_pieces = [(r, b) for r in range(1, 8) for b in range(16)]

        def drip_fac(n):
            for r, b in fac_pieces[:n]:
                fprev[b] = dma(
                    "gpsimd", cc_fs_in[b * 128:(b + 1) * 128, :],
                    fac_sl[r, b * 128:(b + 1) * 128, :],
                    waits=(fprev[b],), accum_op=ALU.add)
            del fac_pieces[:n]

        def fsum_gate():
            drip_fac(len(fac_pieces))
            for b in range(8, 16):
                if fprev.get(b):
                    op("gpsimd", "nop", waits=(fprev[b],))
            if not SKIP_CC:
                op("gpsimd", "collective_compute", "AllGather", ALU.bypass,
                   replica_groups=[list(range(8))],
                   ins=[cc_fs_in.ap().opt()], outs=[cc_fs_out.ap().opt()],
                   inc=CCP[0])
            else:
                op("gpsimd", "nop", inc=CCP[0])
            return (CCP[0], CCP[0].n)


        def ln_phase(tile):
            xch_free = {}
            for c in range(TPT):
                tc = tile * TPT + c
                slot = c % 2
                XCH = T["XCH"][slot]
                rw = list(xch_free.get(c - 2, ()))
                if tile == 1 and c < 2:
                    rw.append(marks.get(("lnfree", 0, TPT - 2 + c)))
                xdma = dma("scalar", XCH[:, :], x_sl[tc * CH:(tc + 1) * CH, :],
                           waits=tuple(x for x in rw if x))
                if TRUNCATE == "ln0dma" and tile == 0 and c == 0:
                    return
                op("vector", "tensor_reduce", T["STATS"][:, c:c + 1], XCH[:, :],
                   AX.X, ALU.add, waits=(xdma,), inc=CDVE)
                sum_done = w(CDVE)
                if TRUNCATE == "ln0red" and tile == 0 and c == 0:
                    return
                op("scalar", "activation", out=T["TRASH"][:, :], in_=XCH[:, :],
                   func=AF.Square, waits=(xdma,), inc=CACT)
                sq_sq = w(CACT)
                op("vector", "tensor_reduce", T["STATS"][:, 8 + c:9 + c],
                   T["TRASH"][:, :], AX.X, ALU.add, waits=(sq_sq,), inc=CDVE)
                sq_done = w(CDVE)
                if TRUNCATE == "ln0sq" and tile == 0 and c == 0:
                    return
                mcol = T["STATS"][:, 16 + c:17 + c]
                vcol = T["STATS"][:, 24 + c:25 + c]
                rcol = T["STATS"][:, 32 + c:33 + c]
                bcol = T["STATS"][:, 40 + c:41 + c]
                tcol = T["STATS"][:, 48 + c:49 + c]
                scol = T["STATS"][:, 56 + c:57 + c]
                op("vector", "tensor_scalar", out=mcol, in0=T["STATS"][:, c:c + 1],
                   scalar1=RECIP_D, scalar2=None, op0=ALU.mult, waits=(sum_done,))
                op("vector", "tensor_scalar", out=vcol,
                   in0=T["STATS"][:, 8 + c:9 + c], scalar1=RECIP_D, scalar2=None,
                   op0=ALU.mult, waits=(sq_done,))
                op("vector", "tensor_tensor", tcol, mcol, mcol, op=ALU.mult)
                op("vector", "tensor_tensor", vcol, vcol, tcol,
                   op=ALU.subtract, inc=CDVE)
                op("scalar", "activation", out=scol, in_=vcol, func=AF.Sqrt,
                   bias=T["EPSC"][:, 0:1], waits=(w(CDVE),), inc=CACT)
                op("vector", "reciprocal", rcol, scol, waits=(w(CACT),))
                op("vector", "tensor_scalar", out=bcol, in0=mcol, scalar1=rcol,
                   scalar2=-1.0, op0=ALU.mult, op1=ALU.mult, inc=CDVE)
                rb = w(CDVE)
                XLNT = T["XLNT"][slot]
                nwt = [rb]
                pv = ln_tp.get((tile, c - 2)) or (
                    ln_tp.get((0, TPT - 2 + c)) if tile == 1 else None)
                if pv:
                    nwt.append(pv)
                op("scalar", "activation", out=XLNT[:, :], in_=XCH[:, :],
                   func=AF.Identity, scale=rcol, bias=bcol, waits=tuple(nwt),
                   inc=CACT)
                nrm = w(CACT)
                if TRUNCATE == "ln0nrm" and tile == 0 and c == 0:
                    return
                marks[("lnfree", tile, c)] = nrm
                xch_free[c] = (nrm, sq_done, sum_done)
                tw = [nrm]
                if tile == 0 and c == 0:
                    tw.append(idn_dma)
                if (tile, c - 1) in ln_chunk_evac:
                    tw.append(ln_chunk_evac[(tile, c - 1)])
                elif tile == 1 and c == 0:
                    tw.append(ln_chunk_evac[(0, TPT - 1)])
                    tw.append(marks[("fme", "k", 0, NQ - 1)])
                for dc in range(NDC):
                    op("tensor", "transpose",
                       T["PT"][:, dc * 128:(dc + 1) * 128],
                       XLNT[:, dc * 128:(dc + 1) * 128], T["IDN"][:, :],
                       waits=tuple(tw) if dc == 0 else WAITS_NONE)
                inc_last("tensor", CPE)
                ln_tp[(tile, c)] = w(CPE)
                if TRUNCATE == "ln0tp" and tile == 0 and c == 0:
                    return
                if EVAC_MODE == 1:
                    dst = T["XLNFM"][:, c * D:(c + 1) * D]
                    src = T["PT"][:, :]
                elif EVAC_MODE == 2:
                    dst = T["XLNFM"][:, c * D:(c + 1) * D].rearrange(
                        "p (dc t) -> p dc t", dc=NDC)
                    src = T["PT"][:].rearrange("p (dc t) -> p dc t", dc=NDC)
                elif EVAC_MODE == 3:
                    dst = T["XLNFM"][:].rearrange("p (dc t) -> p dc t", dc=NDC)[
                        :, :, c * CH:(c + 1) * CH]
                    src = T["PT"][:, :].rearrange("p (t dc) -> p dc t", dc=NDC)
                else:
                    dst = T["XLNFM"][:].rearrange("p (dc t) -> p dc t", dc=NDC)[
                        :, :, c * CH:(c + 1) * CH]
                    src = T["PT"][:].rearrange("p (dc t) -> p dc t", dc=NDC)
                ecw = [ln_tp[(tile, c)]]
                if tile == 1 and c == 0:
                    ecw.append(marks[("g1e", 0)])
                op("vector", "tensor_copy", dst, src, waits=tuple(ecw),
                   inc=CDVE)
                ln_chunk_evac[(tile, c)] = w(CDVE)
                if TRUNCATE == f"ln{tile}c{c}":
                    return
            ln_evac[tile] = w(CDVE)
            if "d_stats3" in debug and tile == 0:
                dma("sync", dbg["d_stats3"].ap(), T["STATS"][:, :],
                    waits=(w(CACT), w(CDVE)))
            if "d_xlnfm" in debug and tile == 0:
                dma("sync", dbg["d_xlnfm"].ap(), T["XLNFM"][:, :],
                    waits=(ln_evac[tile],))
        def g1_phase(tile):
            g1w = [ln_evac[tile], wg1_dma, marks[("projevac", tile, NQ - 1)]]
            for p in range(2):
                for dc in range(NDC):
                    op("tensor", "matmul", T["PP"][p][:, :],
                       T["WG1B"][:, dc * HD:(dc + 1) * HD],
                       T["XLNFM"][:, dc * 1024 + p * 512:dc * 1024 + (p + 1) * 512],
                       start=(dc == 0), stop=(dc == NDC - 1),
                       waits=tuple(g1w) if (dc == 0 and p == 0)
                       else WAITS_NONE)
                inc_last("tensor", CPE)
                op("vector", "tensor_copy",
                   T["G1"][:, tile * 1024 + p * 512:tile * 1024 + (p + 1) * 512],
                   T["PP"][p][:, :], waits=(w(CPE),), inc=CDVE)
            marks[("g1e", tile)] = w(CPE)
            g1_done[tile] = w(CDVE)
            if "d_g1" in debug and tile == 1:
                dma("sync", dbg["d_g1"].ap(), T["G1"][:, :],
                    waits=(g1_done[tile],))

        sstate = {"lin": 0, "mask_m": None, "intra_m": [None, None],
                  "oevac": None, "dq": None, "sadd": None, "sb": None,
                  "fme_q": {}, "fme_k": {}, "vev": {}}
        pending_scan = [None]
        pstate = {}

        def scan_step(tile, g, c):
            st = sstate
            lin = st["lin"]
            st["lin"] += 1
            slot = lin % 2
            qfm3 = T["QFM"][:].rearrange("p (h t) -> p h t", h=QH)
            kfm3 = T["KFM"][:].rearrange("p (h t) -> p h t", h=QH)
            sw = [st["fme_q"].pop((g, c)), st["fme_k"].pop((g, c))]
            if st["mask_m"]:
                sw.append(st["mask_m"])
            for hl in range(QH):
                op("tensor", "matmul", T["PSC"][:, hl * 128:(hl + 1) * 128],
                   kfm3[:, hl, c * CH:(c + 1) * CH],
                   qfm3[:, hl, c * CH:(c + 1) * CH],
                   start=True, stop=True,
                   waits=tuple(sw) if hl == 0 else WAITS_NONE)
            inc_last("tensor", CPE)
            scd = w(CPE)
            mw = [scd]
            if tile == 0 and g == 0 and c == 0:
                mw.append(dmask_dma)
            if st["intra_m"][slot]:
                mw.append(st["intra_m"][slot])
            op("vector", "tensor_tensor", T["SCT"][slot][:, :],
               T["PSC"][:, :], T["DMASK"][:, g * 512:(g + 1) * 512],
               op=ALU.mult, waits=tuple(mw), inc=CDVE)
            st["mask_m"] = w(CDVE)
            ow = [st["mask_m"], st["vev"].pop((g, c))]
            if st["oevac"]:
                ow.append(st["oevac"])
            for hl in range(QH):
                op("tensor", "matmul", T["POI"][:, hl * 128:(hl + 1) * 128],
                   T["SCT"][slot][:, hl * 128:(hl + 1) * 128],
                   T["V"][:, c * 512 + hl * 128:c * 512 + (hl + 1) * 128],
                   start=True, stop=True,
                   waits=tuple(ow) if hl == 0 else WAITS_NONE)
            iw = [x for x in (st["dq"], st["sb"]) if x]
            if tile == 0 and g == 0 and c == 0:
                iw.append(minit)
            for hl in range(QH):
                h = g * QH + hl
                op("tensor", "matmul", T["POE"][:, hl * 128:(hl + 1) * 128],
                   qfm3[:, hl, c * CH:(c + 1) * CH],
                   T["SB"][:, h * HD:(h + 1) * HD],
                   start=True, stop=True,
                   waits=tuple(iw) if hl == 0 else WAITS_NONE)
            kw = [st["sadd"]] if st["sadd"] else []
            for hl in range(QH):
                op("tensor", "matmul", T["PKV"][:, hl * 128:(hl + 1) * 128],
                   T["K2"][:, c * 512 + hl * 128:c * 512 + (hl + 1) * 128],
                   T["V"][:, c * 512 + hl * 128:c * 512 + (hl + 1) * 128],
                   start=True, stop=True,
                   waits=tuple(kw) if hl == 0 else WAITS_NONE)
            inc_last("tensor", CPE)
            mm2 = w(CPE)
            st["intra_m"][slot] = mm2
            marks[("scanstep", tile, g, c)] = mm2
            dqw = [mm2]
            if tile == 0 and g == 0 and c == 0:
                dqw.append(dqv_dma)
            for hl in range(QH):
                h = g * QH + hl
                op("scalar", "activation",
                   out=T["TMPOE"][:, hl * 128:(hl + 1) * 128],
                   in_=T["POE"][:, hl * 128:(hl + 1) * 128],
                   func=AF.Identity, scale=T["DQV"][:, h:h + 1],
                   waits=tuple(dqw) if hl == 0 else WAITS_NONE)
            st["dq"] = w(CACT)
            dw = [st["dq"]]
            if st["sadd"]:
                dw.append(st["sadd"])
            if st["sb"]:
                dw.append(st["sb"])
            if tile == 0 and g == 0 and c == 0:
                dw += [lamc_dma, minit]
            for hl in range(QH):
                h = g * QH + hl
                op("scalar", "activation",
                   out=T["S"][:, h * HD:(h + 1) * HD],
                   in_=T["S"][:, h * HD:(h + 1) * HD], func=AF.Identity,
                   scale=T["LAMCT"][:, h:h + 1],
                   waits=tuple(dw) if hl == 0 else WAITS_NONE)
            decay_done = w(CACT)
            oqw = [st["dq"]]
            if c == 0:
                pq = (tile, g - 1) if g > 0 else (
                    (tile - 1, NQ - 1) if tile > 0 else None)
                pv = spill_o.get(pq) if pq else None
                if pv:
                    oqw.append(pv)
            op("vector", "tensor_tensor", T["OQ"][:, c * 512:(c + 1) * 512],
               T["POI"][:, :], T["TMPOE"][:, :], op=ALU.add,
               waits=tuple(oqw), inc=CDVE)
            st["oevac"] = w(CDVE)
            op("vector", "tensor_tensor", T["S"][:, g * 512:(g + 1) * 512],
               T["S"][:, g * 512:(g + 1) * 512], T["PKV"][:, :],
               op=ALU.add, waits=(decay_done, mm2), inc=CDVE)
            st["sadd"] = w(CDVE)
            op("vector", "tensor_copy", T["SB"][:, g * 512:(g + 1) * 512],
               T["S"][:, g * 512:(g + 1) * 512], inc=CDVE)
            st["sb"] = w(CDVE)

            if c == TPT - 1:
                marks[("scan", tile, g)] = mm2
                spill_o[(tile, g)] = dma(
                    "sync",
                    o_spill[tile * TPT:(tile + 1) * TPT, :,
                            g * 512:(g + 1) * 512].rearrange(
                        "tc t e -> t tc e"),
                    T["OQ"][:].rearrange("p (tc e) -> p tc e", tc=TPT),
                    waits=(st["oevac"],))
                spill_q[(tile, g)] = []
                for c_ in range(TPT):
                    spill_q[(tile, g)].append(dma(
                        "sync",
                        q_spill[tile * TPT + c_,
                                g * QH:(g + 1) * QH].rearrange(
                            "h hd t -> hd h t"),
                        T["QFM"][:].rearrange("p (h t) -> p h t", h=QH)[
                            :, :, c_ * CH:(c_ + 1) * CH],
                        waits=(mm2,) if c_ == 0 else WAITS_NONE))
                if tile == 1:
                    sdma = dma("gpsimd", cc_s_in[g].ap(),
                               T["S"][:, g * 512:(g + 1) * 512],
                               waits=(st["sadd"],))
                    op("gpsimd", "nop", waits=(sdma,))
                    if not SKIP_CC:
                        op("gpsimd", "collective_compute", "AllGather",
                           ALU.bypass,
                           replica_groups=[[0, 1], [2, 3], [4, 5], [6, 7]],
                           ins=[cc_s_in[g].ap().opt()],
                           outs=[cc_s_out[g].ap().opt()], inc=CCP[1 + g])
                    else:
                        op("gpsimd", "nop", inc=CCP[1 + g])
                    s16_sent[g] = (CCP[1 + g], CCP[1 + g].n)

        def quad_phase(tile, g):
            prev_quad = (tile, g - 1) if g > 0 else (
                (tile - 1, NQ - 1) if tile > 0 else None)

            wdmas = {}
            for wname, wt, pj in (("WQ", wq, "q"), ("WK", wk, "k"),
                                  ("WV", wv, "v")):
                if tile == 0 and g == 0:
                    wdmas[pj] = marks[("wpre", pj)]
                    continue
                pv = marks.get(("proj", pj) + prev_quad) if prev_quad else None
                wdmas[pj] = dma(
                    "gpsimd",
                    T[wname][:].rearrange("p (dc e) -> p dc e", dc=NDC),
                    wt[:, g * 512:(g + 1) * 512].rearrange(
                        "(dc p) e -> p dc e", p=128),
                    waits=(pv,) if pv else WAITS_NONE)
            drip_fac(16)

            ppfree = {}
            rotp = {}
            tpp = {}
            silup = {}
            mmk = {}

            def proj_chunk(which, wsb, c, wi):
                gi = c * 3 + wi
                slot = gi % 2
                pw = [ln_chunk_evac[(tile, c)], wdmas[which]]
                if ppfree.get(gi - 2):
                    pw.append(ppfree[gi - 2])
                elif gi < 2:
                    if tile == 1 and g == 0:
                        pw.append(g1_done[0])
                    if prev_quad:
                        pv = marks.get(("projevac",) + prev_quad)
                        if pv:
                            pw.append(pv)
                for dc in range(NDC):
                    op("tensor", "matmul", T["PP"][slot][:, :],
                       T["XLNFM"][:, dc * 1024 + c * CH:dc * 1024 + (c + 1) * CH],
                       wsb[:, dc * 512:(dc + 1) * 512],
                       start=(dc == 0), stop=(dc == NDC - 1),
                       waits=tuple(pw) if dc == 0 else WAITS_NONE)
                inc_last("tensor", CPE)
                mmd = w(CPE)
                mmk[(which, c)] = mmd
                prevscan = marks.get(("scanstep",) + prev_quad + (c,)) \
                    if prev_quad else None

                if which == "v":
                    ew = [mmd]
                    if prevscan:
                        ew.append(prevscan)
                    op("scalar", "activation",
                       out=T["V"][:, c * 512:(c + 1) * 512],
                       in_=T["PP"][slot][:, :], func=AF.Identity,
                       waits=tuple(ew), inc=CACT)
                    ppfree[gi] = w(CACT)
                    sstate["vev"][(g, c)] = w(CACT)
                    return

                QSB = T["QSB"][0 if which == "q" else 1]
                QTM = T["QTM"][0 if which == "q" else 1]
                sw = [mmd]
                if rotp.get((which, c - 1)):
                    sw.append(rotp[(which, c - 1)])
                elif pstate.get(("rot", which)):
                    sw.append(pstate[("rot", which)])
                op("scalar", "activation", out=QSB[:, :],
                   in_=T["PP"][slot][:, :], func=AF.Silu, waits=tuple(sw),
                   inc=CACT)
                silup[(which, c)] = w(CACT)
                ppfree[gi] = w(CACT)
                tc = tile * TPT + c
                q3 = QSB[:].rearrange("p (h e) -> p h e", h=QH)
                o3 = QTM[:].rearrange("p (h e) -> p h e", h=QH)
                cosap = T["COS"][:, tc * 64:(tc + 1) * 64][:, None, :] \
                    .broadcast_to([128, QH, 64])
                sinap = T["SIN"][:, tc * 64:(tc + 1) * 64][:, None, :] \
                    .broadcast_to([128, QH, 64])
                t1 = T["TMP1"][:].rearrange("p (h e) -> p h e", h=QH)
                t2 = T["TMP2"][:].rearrange("p (h e) -> p h e", h=QH)
                rw = [silup[(which, c)]]
                if tile == 0 and g == 0 and c == 0:
                    rw += [cos_dma, sin_dma]
                if tpp.get((which, c - 1)):
                    rw.append(tpp[(which, c - 1)])
                elif pstate.get(("tp", which)):
                    rw.append(pstate[("tp", which)])
                op("vector", "tensor_tensor", t1, q3[:, :, 0:64], cosap,
                   op=ALU.mult, waits=tuple(rw))
                op("vector", "tensor_tensor", t2, q3[:, :, 64:128], sinap,
                   op=ALU.mult)
                op("vector", "tensor_tensor", o3[:, :, 0:64], t1, t2,
                   op=ALU.subtract)
                op("vector", "tensor_tensor", t1, q3[:, :, 0:64], sinap,
                   op=ALU.mult)
                op("vector", "tensor_tensor", t2, q3[:, :, 64:128], cosap,
                   op=ALU.mult)
                op("vector", "tensor_tensor", o3[:, :, 64:128], t1, t2,
                   op=ALU.add, inc=CDVE)
                rotp[(which, c)] = w(CDVE)
                if which == "k":
                    kw = [w(CDVE)]
                    if tile == 0 and g == 0 and c == 0:
                        kw.append(dkt_dma)
                    if prevscan:
                        kw.append(prevscan)
                    op("vector", "tensor_tensor",
                       T["K2"][:, c * 512:(c + 1) * 512], QTM[:, :],
                       T["DKT"][:, g * 512:(g + 1) * 512], op=ALU.mult,
                       waits=tuple(kw), inc=CDVE)
                    rotp[(which, c)] = w(CDVE)

            def proj_transpose(which, c):
                QTM = T["QTM"][0 if which == "q" else 1]
                dst_fm = T["QFM"] if which == "q" else T["KFM"]
                off = 0 if which == "q" else 1024
                prevscan = marks.get(("scanstep",) + prev_quad + (c,)) \
                    if prev_quad else None
                tw = [rotp[(which, c)]]
                if tpp.get((which, c - 1)):
                    tw.append(marks.get(("fme", which, tile, g, c - 1)))
                elif marks.get(("lastPT", tile)):
                    tw.append(marks[("lastPT", tile)])
                for hl in range(QH):
                    op("tensor", "transpose",
                       T["PT"][:, off + hl * 128:off + (hl + 1) * 128],
                       QTM[:, hl * 128:(hl + 1) * 128], T["IDN"][:, :],
                       waits=tuple(tw) if hl == 0 else WAITS_NONE)
                inc_last("tensor", CPE)
                tpp[(which, c)] = w(CPE)
                ew = [tpp[(which, c)]]
                if prevscan:
                    ew.append(prevscan)
                pq = prev_quad
                if pq and spill_q.get(pq):
                    ew.append(spill_q[pq][c])
                dstap = dst_fm[:].rearrange("p (h t) -> p h t", h=QH)[
                    :, :, c * CH:(c + 1) * CH]
                srcap = T["PT"][:, off:off + 512].rearrange(
                    "p (h t) -> p h t", h=QH)
                op("vector", "tensor_copy", dstap, srcap, waits=tuple(ew),
                   inc=CDVE)
                fm = w(CDVE)
                marks[("fme", which, tile, g, c)] = fm
                if which == "q":
                    sstate["fme_q"][(g, c)] = fm
                else:
                    sstate["fme_k"][(g, c)] = fm

            for c in range(TPT):
                if c == 0 and pending_scan[0] is not None:
                    scan_step(*pending_scan[0])
                    pending_scan[0] = None
                proj_chunk("q", T["WQ"], c, 0)
                proj_chunk("k", T["WK"], c, 1)
                proj_chunk("v", T["WV"], c, 2)
                if c > 0:
                    scan_step(tile, g, c - 1)
                proj_transpose("q", c)
                proj_transpose("k", c)
            marks[("proj", "q", tile, g)] = mmk[("q", TPT - 1)]
            marks[("proj", "k", tile, g)] = mmk[("k", TPT - 1)]
            marks[("proj", "v", tile, g)] = mmk[("v", TPT - 1)]
            marks[("projevac", tile, g)] = w(CACT)
            marks[("fme", "k", tile, g)] = marks[("fme", "k", tile, g, TPT - 1)]
            marks[("lastPT", tile)] = marks[("fme", "k", tile, g, TPT - 1)]
            pstate[("rot", "q")] = rotp[("q", TPT - 1)]
            pstate[("rot", "k")] = rotp[("k", TPT - 1)]
            pstate[("tp", "q")] = tpp[("q", TPT - 1)]
            pstate[("tp", "k")] = tpp[("k", TPT - 1)]
            if g == NQ - 1:
                scan_step(tile, g, TPT - 1)
                pending_scan[0] = None
            else:
                pending_scan[0] = (tile, g, TPT - 1)

        def tail_phase(tile):
            S16ST = T[f"S16ST{tile}"]
            OCH = T[f"OCH{tile}"]
            QCH = T[f"QCH{tile}"]
            TMPC = T[f"TMPC{tile}"]
            GATS = T[f"GATS{tile}"]
            XG = T[f"XG{tile}"]
            SQB = T[f"SQB{tile}"]
            OGT = T[f"OGT{tile}"]
            PCORR = T[f"PCORR{tile}"]
            PGATE = T[f"PGATE{tile}"]
            PT2 = T[f"PT2{tile}"]

            s16marks = []
            for g in range(NQ):
                s16marks.append(dma(
                    "scalar", S16ST[:, g * 512:(g + 1) * 512],
                    cc_s_out[g][0:HD, :], waits=(s16_sent[g],)))
            S16B = T[f"S16B{tile}"]
            op("vector", "tensor_copy", S16B[:, :], S16ST[:, :],
               waits=tuple(s16marks), inc=CDVE)
            s16b = w(CDVE)
            if "d_s16" in debug and tile == 0:
                dma("sync", dbg["d_s16"].ap(), S16ST[:, :], waits=(s16b,))

            sig_done = [None] * TPT
            corr_add = [None] * TPT
            qch_used = [None] * TPT
            prev_vadd = None
            for c in range(TPT):
                tc = tile * TPT + c
                slot = c % 2
                odma = dma("scalar", OCH[:, c * D:(c + 1) * D], o_spill[tc],
                           waits=tuple(spill_o[(tile, gg)]
                                       for gg in range(NQ)) if c == 0
                           else WAITS_NONE)
                qw = []
                if c == 0:
                    for gg in range(NQ):
                        qw.extend(spill_q[(tile, gg)])
                if c >= 2 and qch_used[c - 2]:
                    qw.append(qch_used[c - 2])
                qdma = dma("scalar",
                           QCH[slot][:].rearrange("p (h t) -> p h t", h=NH),
                           q_spill[tc].rearrange("h hd t -> hd h t"),
                           waits=tuple(qw))
                for g in range(NQ):
                    ps_ = PCORR[g % 2]
                    cw = [qdma, s16b]
                    if g >= 2:
                        cw.append(marks[("corrva", tile, c, g - 2)])
                    elif c >= 1:
                        cw.append(marks[("corrva", tile, c - 1, g + 2)])
                    for hl in range(QH):
                        h = g * QH + hl
                        op("tensor", "matmul", ps_[:, hl * 128:(hl + 1) * 128],
                           QCH[slot][:, h * 128:(h + 1) * 128],
                           S16B[:, h * HD:(h + 1) * HD],
                           start=True, stop=True,
                           waits=tuple(x for x in cw if x) if hl == 0
                           else WAITS_NONE)
                    inc_last("tensor", CPE)
                    cmm = w(CPE)
                    scw = [cmm]
                    if prev_vadd:
                        scw.append(prev_vadd)
                    for hl in range(QH):
                        h = g * QH + hl
                        op("scalar", "activation",
                           out=TMPC[:, hl * 128:(hl + 1) * 128],
                           in_=ps_[:, hl * 128:(hl + 1) * 128], func=AF.Identity,
                           scale=T["CORR"][:, tc * NH + h:tc * NH + h + 1],
                           waits=tuple(scw) if hl == 0 else WAITS_NONE)
                    inc_last("scalar", CACT)
                    csc = w(CACT)
                    op("vector", "tensor_tensor",
                       OCH[:, c * D + g * 512:c * D + (g + 1) * 512],
                       OCH[:, c * D + g * 512:c * D + (g + 1) * 512],
                       TMPC[:, :], op=ALU.add, waits=(csc, odma), inc=CDVE)
                    marks[("corrva", tile, c, g)] = w(CDVE)
                    prev_vadd = w(CDVE)
                corr_add[c] = w(CDVE)
                qch_used[c] = w(CPE)

                gw = [g1_done[tile], consts_dma]
                if c >= 1 and sig_done[c - 1]:
                    gw.append(sig_done[c - 1])
                for q4 in range(4):
                    op("tensor", "matmul", PGATE[:, q4 * 512:(q4 + 1) * 512],
                       T["G1"][:, tile * 1024 + c * CH:tile * 1024 + (c + 1) * CH],
                       T["WG2B"][:, q4 * 512:(q4 + 1) * 512],
                       start=True, stop=True,
                       waits=tuple(gw) if q4 == 0 else WAITS_NONE)
                inc_last("tensor", CPE)
                gmm = w(CPE)
                for q4 in range(4):
                    op("scalar", "activation",
                       out=GATS[:, c * D + q4 * 512:c * D + (q4 + 1) * 512],
                       in_=PGATE[:, q4 * 512:(q4 + 1) * 512], func=AF.Sigmoid,
                       waits=(gmm,) if q4 == 0 else WAITS_NONE)
                inc_last("scalar", CACT)
                sig_done[c] = w(CACT)
            if "d_gats" in debug and tile == 0:
                dma("sync", dbg["d_gats"].ap(), GATS[:, :],
                    waits=(w(CACT),))

            ogfm_e = [None] * TPT
            sq_marks = [None] * TPT
            for c in range(TPT):
                xw = [corr_add[c], sig_done[c]]
                if c >= 1 and ogfm_e[c - 1]:
                    xw.append(ogfm_e[c - 1])
                if c >= 1 and sq_marks[c - 1]:
                    xw.append(sq_marks[c - 1])
                op("vector", "tensor_tensor", XG[:, :], OCH[:, c * D:(c + 1) * D],
                   GATS[:, c * D:(c + 1) * D], op=ALU.mult, waits=tuple(xw),
                   inc=CDVE)
                xg_done = w(CDVE)
                op("scalar", "activation", out=SQB[:, :], in_=XG[:, :],
                   func=AF.Square, waits=(xg_done,), inc=CACT)
                sq_marks[c] = w(CACT)
                op("vector", "tensor_reduce", T["MS"][:, :],
                   SQB[:].rearrange("p (h e) -> p h e", h=NH), AX.X, ALU.add,
                   waits=(w(CACT),), inc=CDVE)
                op("scalar", "activation", out=T["SD2"][:, :], in_=T["MS"][:, :],
                   func=AF.Sqrt, scale=T["EPSC"][:, 2:3],
                   bias=T["EPSC"][:, 1:2], waits=(w(CDVE),), inc=CACT)
                op("vector", "reciprocal", T["R2"][:, :], T["SD2"][:, :],
                   waits=(w(CACT),))
                inc_last("vector", CDVE)
                r2d = w(CDVE)
                for h in range(NH):
                    op("vector", "tensor_scalar",
                       out=OGT[:, h * HD:(h + 1) * HD],
                       in0=XG[:, h * HD:(h + 1) * HD],
                       scalar1=T["R2"][:, h:h + 1], scalar2=None, op0=ALU.mult,
                       waits=(r2d,) if h == 0 else WAITS_NONE)
                inc_last("vector", CDVE)
                ogd = w(CDVE)
                tw = [ogd]
                if c >= 1 and ogfm_e[c - 1]:
                    tw.append(ogfm_e[c - 1])
                if c == 0 and tile == 1:
                    tw.append(marks[("ogfmall", 0)])
                for dc in range(NDC):
                    op("tensor", "transpose", PT2[:, dc * 128:(dc + 1) * 128],
                       OGT[:, dc * 128:(dc + 1) * 128], T["IDN"][:, :],
                       waits=tuple(tw) if dc == 0 else WAITS_NONE)
                inc_last("tensor", CPE)
                tp2 = w(CPE)
                for dc in range(NDC):
                    op("vector", "tensor_scalar",
                       out=T["XLNFM"][:, dc * 1024 + c * CH:dc * 1024 + (c + 1) * CH],
                       in0=PT2[:, dc * 128:(dc + 1) * 128],
                       scalar1=T["NW"][:, dc:dc + 1], scalar2=None, op0=ALU.mult,
                       waits=(tp2,) if dc == 0 else WAITS_NONE)
                inc_last("vector", CDVE)
                ogfm_e[c] = w(CDVE)
            marks[("ogfmall", tile)] = w(CDVE)
            og_all = w(CDVE)
            if "d_ogfm" in debug and tile == 0:
                dma("sync", dbg["d_ogfm"].ap(), T["XLNFM"][:, :],
                    waits=(og_all,))

            fence()
            WOH = T[f"WOH{tile}"]
            XRCH = T[f"XRCH{tile}"]
            X2H = T[f"X2H{tile}"]
            TRASH2 = T[f"TRASH2{tile}"]
            PB = T[f"PB{tile}"]
            x2w = [[None] * TPT, [None] * TPT]
            x2free = {}
            pbe_all = []
            for H in range(2):
                wodma = dma("gpsimd",
                            WOH[:].rearrange("p (dc e) -> p dc e", dc=NDC),
                            wo[:, H * 1024:(H + 1) * 1024].rearrange(
                                "(dc p) e -> p dc e", p=128),
                            waits=(w(CPE),) if H == 1 else WAITS_NONE)
                pbe = [None] * TPT
                for c in range(TPT):
                    tc = tile * TPT + c
                    slot = c % 2
                    gi = H * TPT + c
                    rw = [pbe_all[gi - 2]] if gi >= 2 else []
                    xrdma = dma("scalar", XRCH[slot][:, :],
                                x_sl[tc * CH:(tc + 1) * CH,
                                     H * 1024:(H + 1) * 1024],
                                waits=tuple(x for x in rw if x))
                    mw = [og_all, wodma]
                    if gi >= 2 and pbe_all[gi - 2]:
                        mw.append(pbe_all[gi - 2])
                    for dc in range(NDC):
                        for n in range(2):
                            op("tensor", "matmul",
                               PB[slot][:, n * 512:(n + 1) * 512],
                               T["XLNFM"][:, dc * 1024 + c * CH:dc * 1024 + (c + 1) * CH],
                               WOH[:, dc * 1024 + n * 512:dc * 1024 + (n + 1) * 512],
                               start=(dc == 0), stop=(dc == NDC - 1),
                               waits=tuple(mw) if (dc == 0 and n == 0)
                               else WAITS_NONE)
                    inc_last("tensor", CPE)
                    bmm = w(CPE)
                    ew = [bmm, xrdma]
                    ew += list(x2free.get(gi - 2, ()))
                    op("vector", "tensor_tensor", X2H[slot][:, :],
                       PB[slot][:, :], XRCH[slot][:, :],
                       op=ALU.add, waits=tuple(ew), inc=CDVE)
                    ev = w(CDVE)
                    pbe[c] = ev
                    pbe_all.append(ev)
                    op("vector", "tensor_reduce",
                       T["S2S"][:, c * 2 + H:c * 2 + H + 1], X2H[slot][:, :],
                       AX.X, ALU.add, inc=CDVE)
                    red = w(CDVE)
                    op("scalar", "activation", out=TRASH2[:, :],
                       in_=X2H[slot][:, :], func=AF.Square,
                       accum_out=T["Q2S"][:, c * 2 + H:c * 2 + H + 1],
                       waits=(ev,))
                    op("scalar", "activation", out=T["DUMT"][:, :],
                       in_=T["EPSC"][:, 0:1], func=AF.Identity, inc=CACT)
                    sqd = w(CACT)
                    x2w[H][c] = dma(
                        "gpsimd",
                        out_sl[tc * CH:(tc + 1) * CH,
                               H * 1024:(H + 1) * 1024],
                        X2H[slot][:, :], waits=(ev,))
                    x2free[gi] = (x2w[H][c], sqd, red)
            bstats = (w(CDVE), w(CACT))

            fence()
            XB2 = T[f"XB2{tile}"]
            XLNT2 = T[f"XLNT2{tile}"]
            PT3 = T[f"PT3{tile}"]
            xl2e = [None] * TPT
            nrm_done = [None] * TPT
            if "d_stats" in debug and tile == 0:
                dma("sync", dbg["d_stats"][:, 0:16], T["S2S"][:, :],
                    waits=bstats)
                dma("sync", dbg["d_stats"][:, 16:32], T["Q2S"][:, :],
                    waits=bstats)
            for c in range(TPT):
                tc = tile * TPT + c
                slot = c % 2
                rw = [nrm_done[c - 2]] if c >= 2 else []
                rw += [x2w[0][c], x2w[1][c]]
                xbdma = dma("scalar", XB2[slot][:, :],
                            out_sl[tc * CH:(tc + 1) * CH, :],
                            waits=tuple(x for x in rw if x))
                if "d_x2" in debug and tile == 0:
                    dma("sync", dbg["d_x2"][:, c * D:(c + 1) * D],
                        XB2[slot][:, :], waits=(xbdma,))
                mcol = T["STATS"][:, c:c + 1]
                vcol = T["STATS"][:, 8 + c:9 + c]
                rcol = T["STATS"][:, 16 + c:17 + c]
                bcol = T["STATS"][:, 24 + c:25 + c]
                tcol = T["STATS"][:, 32 + c:33 + c]
                scol = T["STATS"][:, 40 + c:41 + c]
                op("vector", "tensor_tensor", mcol,
                   T["S2S"][:, c * 2:c * 2 + 1],
                   T["S2S"][:, c * 2 + 1:c * 2 + 2], op=ALU.add,
                   waits=bstats if c == 0 else WAITS_NONE)
                op("vector", "tensor_scalar", out=mcol, in0=mcol,
                   scalar1=RECIP_D, scalar2=None, op0=ALU.mult)
                op("vector", "tensor_tensor", vcol,
                   T["Q2S"][:, c * 2:c * 2 + 1],
                   T["Q2S"][:, c * 2 + 1:c * 2 + 2], op=ALU.add)
                op("vector", "tensor_scalar", out=vcol, in0=vcol,
                   scalar1=RECIP_D, scalar2=None, op0=ALU.mult)
                op("vector", "tensor_tensor", tcol, mcol, mcol, op=ALU.mult)
                op("vector", "tensor_tensor", vcol, vcol, tcol,
                   op=ALU.subtract, inc=CDVE)
                op("scalar", "activation", out=scol, in_=vcol, func=AF.Sqrt,
                   bias=T["EPSC"][:, 0:1], waits=(w(CDVE),), inc=CACT)
                op("vector", "reciprocal", rcol, scol, waits=(w(CACT),))
                op("vector", "tensor_scalar", out=bcol, in0=mcol, scalar1=rcol,
                   scalar2=-1.0, op0=ALU.mult, op1=ALU.mult, inc=CDVE)
                nwt = [w(CDVE), xbdma]
                if c >= 2 and nrm_done[c - 2]:
                    nwt.append(nrm_done[c - 2])
                op("scalar", "activation", out=XLNT2[slot][:, :],
                   in_=XB2[slot][:, :], func=AF.Identity, scale=rcol, bias=bcol,
                   waits=tuple(nwt), inc=CACT)
                nrm_done[c] = w(CACT)
                tw = [nrm_done[c]]
                if c >= 1 and xl2e[c - 1]:
                    tw.append(xl2e[c - 1])
                for dc in range(NDC):
                    op("tensor", "transpose", PT3[:, dc * 128:(dc + 1) * 128],
                       XLNT2[slot][:, dc * 128:(dc + 1) * 128], T["IDN"][:, :],
                       waits=tuple(tw) if dc == 0 else WAITS_NONE)
                inc_last("tensor", CPE)
                tpd = w(CPE)
                dst = T[f"XLN2FM{tile}"][:].rearrange("p (dc t) -> p dc t", dc=NDC)[
                    :, :, c * CH:(c + 1) * CH]
                src = PT3[:].rearrange("p (dc t) -> p dc t", dc=NDC)
                op("vector", "tensor_scalar", out=dst, in0=src,
                   scalar1=float(1.0 / np.sqrt(8.0)), scalar2=None,
                   op0=ALU.mult, waits=(tpd,), inc=CDVE)
                xl2e[c] = w(CDVE)
            xl2all = w(CDVE)
            if "d_stats2" in debug and tile == 0:
                dma("sync", dbg["d_stats2"].ap(), T["STATS"][:, :],
                    waits=(w(CACT), w(CDVE)))
            if "d_xln2fm" in debug and tile == 0:
                dma("sync", dbg["d_xln2fm"].ap(), T[f"XLN2FM{tile}"][:, :],
                    waits=(xl2all,))

            FSP = T[f"FSP{tile}"]
            OUTES = T[f"OUTES{tile}"]
            PF = T[f"PF{tile}"]
            fsp_used = [None] * 4
            prev_pf_evac = None
            prev_outes = [None, None]
            for es in range(4):
                slot = es % 2
                fw = [fsum_cc]
                if es >= 2 and fsp_used[es - 2]:
                    fw.append(fsp_used[es - 2])
                fsdma = []
                for rr in range(2):
                    dstp = FSP[slot][:].rearrange("p (dc e) -> p dc e", dc=NDC)[
                        :, :, rr * 256:(rr + 1) * 256]
                    fsdma.append(dma(
                        "gpsimd", dstp,
                        cc_fs_out[(2 * es + rr) * D:(2 * es + rr + 1) * D]
                        .rearrange("(dc p) e -> p dc e", p=128),
                        waits=tuple(fw) if rr == 0 else WAITS_NONE))
                for half in range(2):
                    hw_ = [xl2all] + fsdma
                    if prev_pf_evac:
                        hw_.append(prev_pf_evac)
                    for tcq in range(4):
                        c = half * 4 + tcq
                        op("tensor", "matmul", PF[tcq][:, :], T["ONES"][:, :],
                           T["BIASB"][:, es * 512:(es + 1) * 512],
                           start=True, stop=False,
                           waits=tuple(x for x in hw_ if x) if tcq == 0
                           else WAITS_NONE)
                        for dc in range(NDC):
                            op("tensor", "matmul", PF[tcq][:, :],
                               T[f"XLN2FM{tile}"][:, dc * 1024 + c * CH:dc * 1024 + (c + 1) * CH],
                               FSP[slot][:, dc * 512:(dc + 1) * 512],
                               start=False, stop=(dc == NDC - 1))
                    inc_last("tensor", CPE)
                    fmm = w(CPE)
                    evw = [fmm]
                    if prev_outes[half]:
                        evw.append(prev_outes[half])
                    for tcq in range(4):
                        op("vector", "tensor_copy",
                           OUTES[half][:, tcq * 512:(tcq + 1) * 512],
                           PF[tcq][:, :], waits=tuple(evw) if tcq == 0
                           else WAITS_NONE)
                    inc_last("vector", CDVE)
                    evd = w(CDVE)
                    prev_pf_evac = evd
                    prev_outes[half] = dma("gpsimd",
                        out_sl[tile * 1024 + half * 512:
                               tile * 1024 + (half + 1) * 512,
                               es * 512:(es + 1) * 512].rearrange(
                            "(tc t) e -> t tc e", t=CH),
                        OUTES[half][:].rearrange("p (tc e) -> p tc e", tc=4),
                        waits=(evd,), accum_op=ALU.add)
                    _ = prev_outes
                fsp_used[es] = w(CPE)

        def trunc(name):
            return TRUNCATE == name

        while True:
            if trunc("prelude"):
                break
            ln_phase(0)
            if trunc("ln0") or (TRUNCATE or "").startswith("ln0c") \
                    or (TRUNCATE or "").startswith("ln0d") \
                    or (TRUNCATE or "").startswith("ln0r") \
                    or (TRUNCATE or "").startswith("ln0s") \
                    or (TRUNCATE or "").startswith("ln0n") \
                    or TRUNCATE == "ln0tp":
                break
            for g in range(NQ):
                quad_phase(0, g)
            marks[("allproj", 0)] = marks[("proj", "v", 0, NQ - 1)]
            g1_phase(0)
            if trunc("quads0"):
                break
            ln_phase(1)
            if trunc("ln1"):
                break
            for g in range(NQ):
                quad_phase(1, g)
            g1_phase(1)
            fsum_cc = fsum_gate()
            if trunc("quads1"):
                break
            fence()
            tail_phase(0)
            if trunc("tail0"):
                break
            fence()
            tail_phase(1)
            break

        for s in GPP:
            op("gpsimd", "nop", waits=((s, s.n),))
        for s in SXP:
            op("sync", "nop", waits=((s, s.n),))

    with nc.Block() as block:
        block.sync(lambda e: walk("sync", e))
        block.scalar(lambda e: walk("scalar", e))
        block.vector(lambda e: walk("vector", e))
        block.tensor(lambda e: walk("tensor", e))
        block.gpsimd(lambda e: walk("gpsimd", e))

    top.close()
    return nc


def _host_tables(core):
    half = core % 2
    p0 = half * NT
    ll = _LL.astype(np.float32)
    theta = (np.float32(10000.0) ** (np.float32(-2.0)
             * np.arange(64, dtype=np.float32) / np.float32(HD))).astype(np.float32)
    pos = np.arange(p0, p0 + NT, dtype=np.float32)
    ang = (pos[:, None] * theta[None, :]).astype(np.float32)
    cos_t = np.cos(ang.astype(np.float64)).astype(np.float32)
    sin_t = np.sin(ang.astype(np.float64)).astype(np.float32)
    i = np.arange(CH, dtype=np.float64)
    diff = i[None, :] - i[:, None]
    dm = np.where(diff >= 0, np.exp(ll[:, None, None].astype(np.float64)
                                    * diff[None]), 0.0)
    dmask = np.ascontiguousarray(
        dm.transpose(1, 0, 2).reshape(CH, NH * CH)).astype(np.float32)
    dk = np.exp(ll[None, :].astype(np.float64)
                * (CH - 1.0 - i[:, None])).astype(np.float32)
    dkt = np.repeat(dk[:, :, None], CH, axis=2).reshape(CH, NH * CH)
    dkt = np.ascontiguousarray(dkt).astype(np.float32)
    dqv = np.exp(ll[None, :].astype(np.float64) * (i[:, None] + 1.0)
                 ).astype(np.float32)
    oddf = 1.0 if (core % 2 == 1) else 0.0
    lamc = np.exp(ll.astype(np.float64) * CH)
    corr = np.zeros((CH, TCC * NH), np.float64)
    for c in range(TCC):
        corr[:, c * NH:(c + 1) * NH] = dqv.astype(np.float64) * (lamc ** c)[None, :]
    corr = (corr * oddf).astype(np.float32)
    lamct = np.tile(lamc.astype(np.float32)[None, :], (CH, 1))
    return cos_t, sin_t, dmask, dkt, dqv, corr, lamct


_NC_CACHE = {}


def _get_nc(debug=()):
    key = tuple(sorted(debug))
    if key not in _NC_CACHE:
        _NC_CACHE[key] = build_program(debug=key)
    return _NC_CACHE[key]


def kernel(L0_attn_x, L0_attn_Wq, L0_attn_Wk, L0_attn_Wv, L0_attn_Wo,
           L0_attn_Wg1, L0_attn_Wg2, L0_attn_norm_weight, L0_ffn_factors,
           L0_ffn_bias, _debug=(), _nc=None):
    x = np.ascontiguousarray(np.asarray(L0_attn_x, np.float32))
    weights = {
        "wq": np.ascontiguousarray(np.asarray(L0_attn_Wq, np.float32)),
        "wk": np.ascontiguousarray(np.asarray(L0_attn_Wk, np.float32)),
        "wv": np.ascontiguousarray(np.asarray(L0_attn_Wv, np.float32)),
        "wo": np.ascontiguousarray(np.asarray(L0_attn_Wo, np.float32)),
        "wg1": np.ascontiguousarray(np.asarray(L0_attn_Wg1, np.float32)),
        "wg2": np.ascontiguousarray(np.asarray(L0_attn_Wg2, np.float32)),
        "nw_in": np.ascontiguousarray(np.asarray(L0_attn_norm_weight, np.float32)),
        "bias_in": np.ascontiguousarray(
            np.asarray(L0_ffn_bias, np.float32).reshape(1, D)),
    }
    factors = np.asarray(L0_ffn_factors, np.float32)
    iden = np.eye(128, dtype=np.float32)

    nc = _nc if _nc is not None else _get_nc(_debug)
    in_maps = []
    for core in range(8):
        b, half = core // 2, core % 2
        cos_t, sin_t, dmask, dkt, dqv, corr, lamct = _host_tables(core)
        m = dict(weights)
        m["x_sl"] = np.ascontiguousarray(x[b, half * NT:(half + 1) * NT, :])
        m["fac_sl"] = np.ascontiguousarray(
            factors[:, :, core * 256:(core + 1) * 256])
        m["cos_in"] = cos_t
        m["sin_in"] = sin_t
        m["dmask_in"] = dmask
        m["dkt_in"] = dkt
        m["dqv_in"] = dqv
        m["corr_in"] = corr
        m["lamc_in"] = lamct
        m["idn_in"] = iden
        in_maps.append(m)

    res = run_bass_kernel_spmd(nc, in_maps, core_ids=list(range(8)))
    out = np.empty((4, 4096, D), np.float32)
    for core in range(8):
        b, half = core // 2, core % 2
        out[b, half * NT:(half + 1) * NT, :] = res.results[core]["out_sl"]
    if _debug:
        return out, res
    return out


# version 14
# speedup vs baseline: 1.0356x; 1.0356x over previous
import numpy as np
from contextlib import ExitStack

import concourse.bass as bass
import concourse.mybir as mybir
from concourse.bass_utils import run_bass_kernel_spmd

F32 = mybir.dt.float32
BF16 = mybir.dt.bfloat16
AF = mybir.ActivationFunctionType
ALU = mybir.AluOpType
AX = mybir.AxisListType

D = 2048
NT = 2048
CH = 128
TCC = 16
TPT = 8
NH = 16
HD = 128
QH = 4
NQ = 4
NDC = 16
EPS_LN = 1e-5
EPS_RMS = 1e-6
RECIP_D = 1.0 / 2048.0

ENGINES = ("sync", "scalar", "vector", "tensor", "gpsimd")
RACE_DETECT = False
PE_DENSE_INC = False
TRUNCATE = None
EVAC_MODE = 0
SKIP_CC = False
PRELUDE_CUT = 0

_LL = -(2.0 ** (-8.0 * (np.arange(NH) + 1.0) / NH))
_LAMC = np.exp(_LL * CH)


class Sem:
    __slots__ = ("h", "n")

    def __init__(self, h):
        self.h = h
        self.n = 0


def build_program(debug=()):
    nc = bass.Bass("TRN2", target_bir_lowering=False, debug=False,
                   detect_race_conditions=RACE_DETECT)

    x_sl = nc.dram_tensor("x_sl", [NT, D], F32, kind="ExternalInput")
    wq = nc.dram_tensor("wq", [D, D], F32, kind="ExternalInput")
    wk = nc.dram_tensor("wk", [D, D], F32, kind="ExternalInput")
    wv = nc.dram_tensor("wv", [D, D], F32, kind="ExternalInput")
    wo = nc.dram_tensor("wo", [D, D], F32, kind="ExternalInput")
    wg1 = nc.dram_tensor("wg1", [D, HD], F32, kind="ExternalInput")
    wg2 = nc.dram_tensor("wg2", [HD, D], F32, kind="ExternalInput")
    nw_in = nc.dram_tensor("nw_in", [D], F32, kind="ExternalInput")
    bias_in = nc.dram_tensor("bias_in", [1, D], F32, kind="ExternalInput")
    fac_sl = nc.dram_tensor("fac_sl", [8, D, 256], F32, kind="ExternalInput")
    cos_in = nc.dram_tensor("cos_in", [NT, 64], F32, kind="ExternalInput")
    sin_in = nc.dram_tensor("sin_in", [NT, 64], F32, kind="ExternalInput")
    dmask_in = nc.dram_tensor("dmask_in", [CH, D], F32, kind="ExternalInput")
    dkt_in = nc.dram_tensor("dkt_in", [CH, D], F32, kind="ExternalInput")
    dqv_in = nc.dram_tensor("dqv_in", [CH, NH], F32, kind="ExternalInput")
    corr_in = nc.dram_tensor("corr_in", [CH, TCC * NH], F32, kind="ExternalInput")
    idn_in = nc.dram_tensor("idn_in", [128, 128], F32, kind="ExternalInput")
    lamc_in = nc.dram_tensor("lamc_in", [CH, NH], F32, kind="ExternalInput")
    out_sl = nc.dram_tensor("out_sl", [NT, D], F32, kind="ExternalOutput")

    def dscratch(name, shape, dtype):
        kind = "ExternalOutput" if name in debug else "Internal"
        return nc.dram_tensor(name, shape, dtype, kind=kind)

    o_spill = dscratch("o_spill", [TCC, CH, D], BF16)
    q_spill = dscratch("q_spill", [TCC, NH, HD, CH], BF16)
    cc_fs_in = nc.dram_tensor("cc_fs_in", [D, 256], F32)
    cc_fs_out = nc.dram_tensor("cc_fs_out", [8 * D, 256], F32)
    cc_s_in = [nc.dram_tensor(f"cc_s_in{g}", [HD, QH * HD], F32) for g in range(NQ)]
    cc_s_out = [nc.dram_tensor(f"cc_s_out{g}", [2 * HD, QH * HD], F32)
                for g in range(NQ)]
    dbg = {}
    for name, shape, dt in (
        ("d_xlnfm", [128, NDC * 1024], BF16),
        ("d_g1", [128, NT], BF16),
        ("d_s16", [128, NH * HD], F32),
        ("d_gats", [128, TPT * D], BF16),
        ("d_ogfm", [128, NDC * 1024], BF16),
        ("d_xln2fm", [128, NDC * 1024], BF16),
        ("d_x2", [128, TPT * D], F32),
        ("d_stats", [128, 64], F32),
        ("d_stats2", [128, 64], F32),
        ("d_stats3", [128, 64], F32),
    ):
        if name in debug:
            dbg[name] = nc.dram_tensor(name, shape, dt, kind="ExternalOutput")

    top = ExitStack()

    def mk(stk, name, shape, dtype):
        return stk.enter_context(nc.sbuf_tensor(name, shape, dtype))

    def mkp(stk, name, shape, dtype):
        return stk.enter_context(nc.psum_tensor(name, shape, dtype))

    T = {}
    T["XLNFM"] = mk(top, "XLNFM", [128, NDC * 1024], BF16)
    T["G1"] = mk(top, "G1", [128, NT], BF16)
    T["WG2B"] = mk(top, "WG2B", [HD, D], BF16)
    T["WG1B"] = mk(top, "WG1B", [128, NDC * HD], BF16)
    T["NW"] = mk(top, "NW", [128, 16], F32)
    T["BIASB"] = mk(top, "BIASB", [1, D], BF16)
    T["ONES"] = mk(top, "ONES", [1, 128], BF16)
    T["IDN"] = mk(top, "IDN", [128, 128], BF16)
    T["DQV"] = mk(top, "DQV", [128, NH], F32)
    T["CORR"] = mk(top, "CORR", [128, TCC * NH], F32)
    T["STATS"] = mk(top, "STATS", [128, 64], F32)
    T["S2S"] = mk(top, "S2S", [128, 16], F32)
    T["Q2S"] = mk(top, "Q2S", [128, 16], F32)
    T["MS"] = mk(top, "MS", [128, 16], F32)
    T["SD2"] = mk(top, "SD2", [128, 16], F32)
    T["R2"] = mk(top, "R2", [128, 16], F32)
    T["EPSC"] = mk(top, "EPSC", [128, 4], F32)
    T["DUMT"] = mk(top, "DUMT", [128, 1], F32)
    T["LAMCT"] = mk(top, "LAMCT", [128, NH], F32)

    qs = ExitStack()
    T["COS"] = mk(qs, "COS", [128, TCC * 64], BF16)
    T["SIN"] = mk(qs, "SIN", [128, TCC * 64], BF16)
    T["DMASK"] = mk(qs, "DMASK", [CH, D], BF16)
    T["DKT"] = mk(qs, "DKT", [CH, D], BF16)
    T["S"] = mk(qs, "S", [128, NH * HD], F32)
    T["SB"] = mk(qs, "SB", [128, NH * HD], BF16)
    T["XCH"] = [mk(qs, f"XCH{i}", [128, D], F32) for i in range(2)]
    T["XLNT"] = [mk(qs, f"XLNT{i}", [128, D], BF16) for i in range(2)]
    T["TRASH"] = mk(qs, "TRASH", [128, D], BF16)
    T["WQ"] = mk(qs, "WQ", [128, NDC * 512], BF16)
    T["WK"] = mk(qs, "WK", [128, NDC * 512], BF16)
    T["WV"] = mk(qs, "WV", [128, NDC * 512], BF16)
    T["QSB"] = [mk(qs, f"QSB{i}", [128, 512], BF16) for i in range(2)]
    T["QTM"] = [mk(qs, f"QTM{i}", [128, 512], BF16) for i in range(2)]
    T["TMP1"] = mk(qs, "TMP1", [128, QH * 64], F32)
    T["TMP2"] = mk(qs, "TMP2", [128, QH * 64], F32)
    T["QFM"] = mk(qs, "QFM", [128, QH * 1024], BF16)
    T["KFM"] = mk(qs, "KFM", [128, QH * 1024], BF16)
    T["K2"] = mk(qs, "K2", [128, TPT * 512], BF16)
    T["V"] = mk(qs, "V", [128, TPT * 512], BF16)
    T["SCT"] = [mk(qs, f"SCT{i}", [128, 512], BF16) for i in range(2)]
    T["TMPOE"] = mk(qs, "TMPOE", [128, 512], F32)
    T["OQ"] = mk(qs, "OQ", [128, TPT * 512], BF16)
    qps = ExitStack()
    T["PP"] = [mkp(qps, f"PP{i}", [128, 512], F32) for i in range(2)]
    T["PT"] = mkp(qps, "PT", [128, D], BF16)
    T["PSC"] = mkp(qps, "PSC", [128, 512], F32)
    T["POI"] = mkp(qps, "POI", [128, 512], F32)
    T["POE"] = mkp(qps, "POE", [128, 512], F32)
    T["PKV"] = mkp(qps, "PKV", [128, 512], F32)
    qps.close()
    qs.close()

    for tile in range(2):
        ts = ExitStack()
        T[f"S16ST{tile}"] = mk(ts, f"S16ST{tile}", [128, NH * HD], F32)
        T[f"S16B{tile}"] = mk(ts, f"S16B{tile}", [128, NH * HD], BF16)
        T[f"OCH{tile}"] = mk(ts, f"OCH{tile}", [128, TPT * D], BF16)
        T[f"QCH{tile}"] = [mk(ts, f"QCH{tile}_{i}", [128, NH * CH], BF16)
                           for i in range(2)]
        T[f"TMPC{tile}"] = mk(ts, f"TMPC{tile}", [128, 512], F32)
        T[f"GATS{tile}"] = mk(ts, f"GATS{tile}", [128, TPT * D], BF16)
        T[f"XG{tile}"] = mk(ts, f"XG{tile}", [128, D], F32)
        T[f"SQB{tile}"] = mk(ts, f"SQB{tile}", [128, D], F32)
        T[f"OGT{tile}"] = mk(ts, f"OGT{tile}", [128, D], BF16)
        aps = ExitStack()
        T[f"PCORR{tile}"] = [mkp(aps, f"PCORR{tile}_{i}", [128, 512], F32)
                             for i in range(2)]
        T[f"PGATE{tile}"] = mkp(aps, f"PGATE{tile}", [128, D], F32)
        T[f"PT2{tile}"] = mkp(aps, f"PT2{tile}", [128, D], BF16)
        aps.close()
        ts.close()
        bs = ExitStack()
        T[f"XLN2FM{tile}"] = mk(bs, f"XLN2FM{tile}", [128, NDC * 1024], BF16)
        T[f"WOH{tile}"] = mk(bs, f"WOH{tile}", [128, NDC * 1024], BF16)
        T[f"XRCH{tile}"] = [mk(bs, f"XRCH{tile}_{i}", [128, 1024], F32)
                            for i in range(2)]
        T[f"X2H{tile}"] = [mk(bs, f"X2H{tile}_{i}", [128, 1024], F32)
                           for i in range(2)]
        T[f"TRASH2{tile}"] = mk(bs, f"TRASH2{tile}", [128, 1024], F32)
        bps = ExitStack()
        T[f"PB{tile}"] = [mkp(bps, f"PB{tile}_{i}", [128, 1024], F32)
                          for i in range(2)]
        bps.close()
        cs = ExitStack()
        T[f"XB2{tile}"] = [mk(cs, f"XB2{tile}_{i}", [128, D], F32)
                           for i in range(2)]
        T[f"XLNT2{tile}"] = [mk(cs, f"XLNT2{tile}_{i}", [128, D], BF16)
                             for i in range(2)]
        T[f"FSP{tile}"] = [mk(cs, f"FSP{tile}_{i}", [128, NDC * 512], BF16)
                           for i in range(2)]
        T[f"OUTES{tile}"] = [mk(cs, f"OUTES{tile}_{i}", [128, 4 * 512], F32)
                             for i in range(2)]
        cps = ExitStack()
        T[f"PT3{tile}"] = mkp(cps, f"PT3{tile}", [128, D], BF16)
        T[f"PF{tile}"] = [mkp(cps, f"PF{tile}_{i}", [128, 512], F32)
                          for i in range(6)]
        cps.close()
        cs.close()
        bs.close()

    NSX, NGP = 8, 8
    SXP = [Sem(top.enter_context(nc.semaphore(f"dsx{i}"))) for i in range(NSX)]
    GPP = [Sem(top.enter_context(nc.semaphore(f"dgp{i}"))) for i in range(NGP)]
    CCP = [Sem(top.enter_context(nc.semaphore(f"dcc{i}"))) for i in range(5)]
    CPE = Sem(top.enter_context(nc.semaphore("cpe")))

    class ChainSem:
        W = 4

        def __init__(self, name):
            self.sems = [Sem(top.enter_context(nc.semaphore(f"{name}{i}")))
                         for i in range(self.W)]
            self.n = 0

        def mark(self):
            if self.n == 0:
                return (self.sems[0], 0)
            i = self.n - 1
            return (self.sems[i % self.W], i // self.W + 1)

    CACT = ChainSem("cact")
    CDVE = ChainSem("cdve")
    ALLSEMS = tuple(SXP) + tuple(GPP) + tuple(CCP) + (CPE,)         + tuple(CACT.sems) + tuple(CDVE.sems)

    def walk(eng_name, E):
        for s in ALLSEMS:
            s.n = 0
        CACT.n = 0
        CDVE.n = 0

        SELF = {"vector": CDVE, "scalar": CACT}
        last_inst = {e: None for e in ENGINES}
        last_mark = {e: None for e in ENGINES}

        def op(engine, method, *a, inc=None, incn=1, waits=(), **k):
            if PE_DENSE_INC and engine == "tensor":
                assert inc is None or inc is CPE
                CPE.n += 1
                if engine == eng_name:
                    for s_, v_ in waits:
                        if s_ is not None:
                            E.wait_ge(s_.h, v_)
                    instr = getattr(E, method)(*a, **k)
                    instr.then_inc(CPE.h, 1)
                    last_inst[engine] = instr
                    last_mark[engine] = (CPE, CPE.n)
                return
            selfsem = SELF.get(engine)
            if method == "dma_start":
                selfsem = None
            if selfsem is not None:
                assert inc is None or inc is selfsem
                prevmark = selfsem.mark()
                selfsem.n += 1
                mysem, myval = selfsem.mark()
                if engine == eng_name:
                    for s_, v_ in waits:
                        if s_ is not None:
                            E.wait_ge(s_.h, v_)
                    E.wait_ge(prevmark[0].h, prevmark[1])
                    instr = getattr(E, method)(*a, **k)
                    instr.then_inc(mysem.h, 1)
                    last_inst[engine] = instr
                return
            if inc is not None:
                inc.n += incn
            if engine == eng_name:
                for s_, v_ in waits:
                    if s_ is not None:
                        E.wait_ge(s_.h, v_)
                instr = getattr(E, method)(*a, **k)
                if instr is not None:
                    last_inst[engine] = instr
                    last_mark[engine] = None
                if inc is not None:
                    instr.then_inc(inc.h, incn)
                    last_mark[engine] = (inc, inc.n)

        def inc_last(engine, sem, n=1):
            if SELF.get(engine) is not None:
                assert sem is SELF[engine]
                return
            if PE_DENSE_INC and engine == "tensor":
                assert sem is CPE
                return
            prev = last_mark[engine]
            sem.n += n
            if engine == eng_name:
                if last_inst[engine] is None or prev is not None:
                    if prev is not None:
                        E.wait_ge(prev[0].h, prev[1])
                    E.nop().then_inc(sem.h, n)
                else:
                    last_inst[engine].then_inc(sem.h, n)
            last_mark[engine] = (sem, sem.n)

        qstate = {"sync": [SXP, 0], "gpsimd": [GPP, 0], "scalar": [SXP, 0]}

        def dma(engine, out_ap, in_ap, dsem=None, waits=(), **k):
            pool, idx = qstate[engine]
            s = pool[idx % len(pool)]
            qstate[engine][1] = idx + 1
            op(engine, "dma_start", inc=s, incn=16,
               waits=tuple(waits) + ((s, s.n),),
               out=out_ap, in_=in_ap, **k)
            return (s, s.n)

        def w(sem_):
            if isinstance(sem_, ChainSem):
                return sem_.mark()
            return (sem_, sem_.n)

        def fence():
            marks = []
            for en, cs in (("tensor", CPE), ("scalar", CACT), ("vector", CDVE)):
                if en == "tensor":
                    inc_last(en, cs)
                marks.append((en, w(cs)))
            dm = [(s, s.n) for s in SXP + GPP]
            for en in ENGINES:
                for other, mk_ in marks:
                    if other != en:
                        op(en, "nop", waits=(mk_,))
                for mk_ in dm:
                    op(en, "nop", waits=(mk_,))

        WAITS_NONE = ()

        ln_evac = {}
        ln_tp = {}
        ln_chunk_evac = {}
        g1_done = {}
        marks = {}
        s16_sent = [None] * NQ
        spill_o = {}
        spill_q = {}

        idn_dma = dma("gpsimd", T["IDN"][:, :], idn_in.ap())
        cos_dma = dma("gpsimd",
                      T["COS"][:].rearrange("p (tc f) -> p tc f", tc=TCC),
                      cos_in.ap().rearrange("(tc p) f -> p tc f", p=128))
        sin_dma = dma("gpsimd",
                      T["SIN"][:].rearrange("p (tc f) -> p tc f", tc=TCC),
                      sin_in.ap().rearrange("(tc p) f -> p tc f", p=128))
        dmask_dma = dma("gpsimd", T["DMASK"][:, :], dmask_in.ap())
        dkt_dma = dma("gpsimd", T["DKT"][:, :], dkt_in.ap())
        dqv_dma = dma("gpsimd", T["DQV"][:, :], dqv_in.ap())
        dma("gpsimd", T["CORR"][:, :], corr_in.ap())
        dma("gpsimd", T["NW"][:, :],
            nw_in.ap().rearrange("(dc p) -> p dc", p=128),
            allow_slow_non_contiguous=True)
        dma("gpsimd", T["BIASB"][:, :], bias_in.ap())
        wg1_dma = dma("gpsimd",
                      T["WG1B"][:].rearrange("p (dc r) -> p dc r", dc=NDC),
                      wg1.ap().rearrange("(dc p) r -> p dc r", p=128))
        lamc_dma = dma("gpsimd", T["LAMCT"][:, :], lamc_in.ap())
        consts_dma = dma("gpsimd", T["WG2B"][:, :], wg2.ap())
        op("vector", "memset", T["EPSC"][:, 0:1], EPS_LN)
        op("vector", "memset", T["EPSC"][:, 1:2], EPS_RMS)
        op("vector", "memset", T["EPSC"][:, 2:3], float(1.0 / HD))
        op("vector", "memset", T["ONES"][:, :], 1.0)
        op("vector", "memset", T["S"][:, :], 0.0)
        op("vector", "memset", T["SB"][:, :], 0.0, inc=CDVE)
        minit = w(CDVE)

        for wname, wt, pj in (("WQ", wq, "q"), ("WK", wk, "k"),
                              ("WV", wv, "v")):
            if PRELUDE_CUT == 2:
                marks[("wpre", pj)] = consts_dma
                continue
            marks[("wpre", pj)] = dma(
                "gpsimd",
                T[wname][:].rearrange("p (dc e) -> p dc e", dc=NDC),
                wt[:, 0:512].rearrange("(dc p) e -> p dc e", p=128))

        fprev = {}
        fac_pieces = []
        if PRELUDE_CUT != 1:
            for b in range(16):
                fprev[b] = dma("gpsimd", cc_fs_in[b * 128:(b + 1) * 128, :],
                               fac_sl[0, b * 128:(b + 1) * 128, :])
            fac_pieces = [(r, b) for r in range(1, 8) for b in range(16)]

        def drip_fac(n):
            for r, b in fac_pieces[:n]:
                fprev[b] = dma(
                    "gpsimd", cc_fs_in[b * 128:(b + 1) * 128, :],
                    fac_sl[r, b * 128:(b + 1) * 128, :],
                    waits=(fprev[b],), accum_op=ALU.add)
            del fac_pieces[:n]

        def fsum_gate():
            drip_fac(len(fac_pieces))
            for b in range(8, 16):
                if fprev.get(b):
                    op("gpsimd", "nop", waits=(fprev[b],))
            if not SKIP_CC:
                op("gpsimd", "collective_compute", "AllGather", ALU.bypass,
                   replica_groups=[list(range(8))],
                   ins=[cc_fs_in.ap().opt()], outs=[cc_fs_out.ap().opt()],
                   inc=CCP[0])
            else:
                op("gpsimd", "nop", inc=CCP[0])
            return (CCP[0], CCP[0].n)


        def ln_phase(tile):
            xch_free = {}
            for c in range(TPT):
                tc = tile * TPT + c
                slot = c % 2
                XCH = T["XCH"][slot]
                rw = list(xch_free.get(c - 2, ()))
                if tile == 1 and c < 2:
                    rw.append(marks.get(("lnfree", 0, TPT - 2 + c)))
                xdma = dma("sync", XCH[:, :], x_sl[tc * CH:(tc + 1) * CH, :],
                           waits=tuple(x for x in rw if x))
                if TRUNCATE == "ln0dma" and tile == 0 and c == 0:
                    return
                op("vector", "tensor_reduce", T["STATS"][:, c:c + 1], XCH[:, :],
                   AX.X, ALU.add, waits=(xdma,), inc=CDVE)
                sum_done = w(CDVE)
                if TRUNCATE == "ln0red" and tile == 0 and c == 0:
                    return
                op("scalar", "activation", out=T["TRASH"][:, :], in_=XCH[:, :],
                   func=AF.Square, waits=(xdma,), inc=CACT)
                sq_sq = w(CACT)
                op("vector", "tensor_reduce", T["STATS"][:, 8 + c:9 + c],
                   T["TRASH"][:, :], AX.X, ALU.add, waits=(sq_sq,), inc=CDVE)
                sq_done = w(CDVE)
                if TRUNCATE == "ln0sq" and tile == 0 and c == 0:
                    return
                mcol = T["STATS"][:, 16 + c:17 + c]
                vcol = T["STATS"][:, 24 + c:25 + c]
                rcol = T["STATS"][:, 32 + c:33 + c]
                bcol = T["STATS"][:, 40 + c:41 + c]
                tcol = T["STATS"][:, 48 + c:49 + c]
                scol = T["STATS"][:, 56 + c:57 + c]
                op("vector", "tensor_scalar", out=mcol, in0=T["STATS"][:, c:c + 1],
                   scalar1=RECIP_D, scalar2=None, op0=ALU.mult, waits=(sum_done,))
                op("vector", "tensor_scalar", out=vcol,
                   in0=T["STATS"][:, 8 + c:9 + c], scalar1=RECIP_D, scalar2=None,
                   op0=ALU.mult, waits=(sq_done,))
                op("vector", "tensor_tensor", tcol, mcol, mcol, op=ALU.mult)
                op("vector", "tensor_tensor", vcol, vcol, tcol,
                   op=ALU.subtract, inc=CDVE)
                op("scalar", "activation", out=scol, in_=vcol, func=AF.Sqrt,
                   bias=T["EPSC"][:, 0:1], waits=(w(CDVE),), inc=CACT)
                op("vector", "reciprocal", rcol, scol, waits=(w(CACT),))
                op("vector", "tensor_scalar", out=bcol, in0=mcol, scalar1=rcol,
                   scalar2=-1.0, op0=ALU.mult, op1=ALU.mult, inc=CDVE)
                rb = w(CDVE)
                XLNT = T["XLNT"][slot]
                nwt = [rb]
                pv = ln_tp.get((tile, c - 2)) or (
                    ln_tp.get((0, TPT - 2 + c)) if tile == 1 else None)
                if pv:
                    nwt.append(pv)
                op("scalar", "activation", out=XLNT[:, :], in_=XCH[:, :],
                   func=AF.Identity, scale=rcol, bias=bcol, waits=tuple(nwt),
                   inc=CACT)
                nrm = w(CACT)
                if TRUNCATE == "ln0nrm" and tile == 0 and c == 0:
                    return
                marks[("lnfree", tile, c)] = nrm
                xch_free[c] = (nrm, sq_done, sum_done)
                tw = [nrm]
                if tile == 0 and c == 0:
                    tw.append(idn_dma)
                if (tile, c - 1) in ln_chunk_evac:
                    tw.append(ln_chunk_evac[(tile, c - 1)])
                elif tile == 1 and c == 0:
                    tw.append(ln_chunk_evac[(0, TPT - 1)])
                    tw.append(marks[("fme", "k", 0, NQ - 1)])
                for dc in range(NDC):
                    op("tensor", "transpose",
                       T["PT"][:, dc * 128:(dc + 1) * 128],
                       XLNT[:, dc * 128:(dc + 1) * 128], T["IDN"][:, :],
                       waits=tuple(tw) if dc == 0 else WAITS_NONE)
                inc_last("tensor", CPE)
                ln_tp[(tile, c)] = w(CPE)
                if TRUNCATE == "ln0tp" and tile == 0 and c == 0:
                    return
                if EVAC_MODE == 1:
                    dst = T["XLNFM"][:, c * D:(c + 1) * D]
                    src = T["PT"][:, :]
                elif EVAC_MODE == 2:
                    dst = T["XLNFM"][:, c * D:(c + 1) * D].rearrange(
                        "p (dc t) -> p dc t", dc=NDC)
                    src = T["PT"][:].rearrange("p (dc t) -> p dc t", dc=NDC)
                elif EVAC_MODE == 3:
                    dst = T["XLNFM"][:].rearrange("p (dc t) -> p dc t", dc=NDC)[
                        :, :, c * CH:(c + 1) * CH]
                    src = T["PT"][:, :].rearrange("p (t dc) -> p dc t", dc=NDC)
                else:
                    dst = T["XLNFM"][:].rearrange("p (dc t) -> p dc t", dc=NDC)[
                        :, :, c * CH:(c + 1) * CH]
                    src = T["PT"][:].rearrange("p (dc t) -> p dc t", dc=NDC)
                ecw = [ln_tp[(tile, c)]]
                if tile == 1 and c == 0:
                    ecw.append(marks[("g1e", 0)])
                op("vector", "tensor_copy", dst, src, waits=tuple(ecw),
                   inc=CDVE)
                ln_chunk_evac[(tile, c)] = w(CDVE)
                if TRUNCATE == f"ln{tile}c{c}":
                    return
            ln_evac[tile] = w(CDVE)
            if "d_stats3" in debug and tile == 0:
                dma("sync", dbg["d_stats3"].ap(), T["STATS"][:, :],
                    waits=(w(CACT), w(CDVE)))
            if "d_xlnfm" in debug and tile == 0:
                dma("sync", dbg["d_xlnfm"].ap(), T["XLNFM"][:, :],
                    waits=(ln_evac[tile],))
        def g1_phase(tile):
            g1w = [ln_evac[tile], wg1_dma, marks[("projevac", tile, NQ - 1)]]
            for p in range(2):
                for dc in range(NDC):
                    op("tensor", "matmul", T["PP"][p][:, :],
                       T["WG1B"][:, dc * HD:(dc + 1) * HD],
                       T["XLNFM"][:, dc * 1024 + p * 512:dc * 1024 + (p + 1) * 512],
                       start=(dc == 0), stop=(dc == NDC - 1),
                       waits=tuple(g1w) if (dc == 0 and p == 0)
                       else WAITS_NONE)
                inc_last("tensor", CPE)
                op("vector", "tensor_copy",
                   T["G1"][:, tile * 1024 + p * 512:tile * 1024 + (p + 1) * 512],
                   T["PP"][p][:, :], waits=(w(CPE),), inc=CDVE)
            marks[("g1e", tile)] = w(CPE)
            g1_done[tile] = w(CDVE)
            if "d_g1" in debug and tile == 1:
                dma("sync", dbg["d_g1"].ap(), T["G1"][:, :],
                    waits=(g1_done[tile],))

        sstate = {"lin": 0, "mask_m": None, "intra_m": [None, None],
                  "oevac": None, "dq": None, "sadd": None, "sb": None,
                  "fme_q": {}, "fme_k": {}, "vev": {}}
        pending_scan = [None]
        pstate = {}

        def scan_step(tile, g, c):
            st = sstate
            lin = st["lin"]
            st["lin"] += 1
            slot = lin % 2
            qfm3 = T["QFM"][:].rearrange("p (h t) -> p h t", h=QH)
            kfm3 = T["KFM"][:].rearrange("p (h t) -> p h t", h=QH)
            sw = [st["fme_q"].pop((g, c)), st["fme_k"].pop((g, c))]
            if st["mask_m"]:
                sw.append(st["mask_m"])
            for hl in range(QH):
                op("tensor", "matmul", T["PSC"][:, hl * 128:(hl + 1) * 128],
                   kfm3[:, hl, c * CH:(c + 1) * CH],
                   qfm3[:, hl, c * CH:(c + 1) * CH],
                   start=True, stop=True,
                   waits=tuple(sw) if hl == 0 else WAITS_NONE)
            inc_last("tensor", CPE)
            scd = w(CPE)
            mw = [scd]
            if tile == 0 and g == 0 and c == 0:
                mw.append(dmask_dma)
            if st["intra_m"][slot]:
                mw.append(st["intra_m"][slot])
            op("vector", "tensor_tensor", T["SCT"][slot][:, :],
               T["PSC"][:, :], T["DMASK"][:, g * 512:(g + 1) * 512],
               op=ALU.mult, waits=tuple(mw), inc=CDVE)
            st["mask_m"] = w(CDVE)
            ow = [st["mask_m"], st["vev"].pop((g, c))]
            if st["oevac"]:
                ow.append(st["oevac"])
            for hl in range(QH):
                op("tensor", "matmul", T["POI"][:, hl * 128:(hl + 1) * 128],
                   T["SCT"][slot][:, hl * 128:(hl + 1) * 128],
                   T["V"][:, c * 512 + hl * 128:c * 512 + (hl + 1) * 128],
                   start=True, stop=True,
                   waits=tuple(ow) if hl == 0 else WAITS_NONE)
            iw = [x for x in (st["dq"], st["sb"]) if x]
            if tile == 0 and g == 0 and c == 0:
                iw.append(minit)
            for hl in range(QH):
                h = g * QH + hl
                op("tensor", "matmul", T["POE"][:, hl * 128:(hl + 1) * 128],
                   qfm3[:, hl, c * CH:(c + 1) * CH],
                   T["SB"][:, h * HD:(h + 1) * HD],
                   start=True, stop=True,
                   waits=tuple(iw) if hl == 0 else WAITS_NONE)
            kw = [st["sadd"]] if st["sadd"] else []
            for hl in range(QH):
                op("tensor", "matmul", T["PKV"][:, hl * 128:(hl + 1) * 128],
                   T["K2"][:, c * 512 + hl * 128:c * 512 + (hl + 1) * 128],
                   T["V"][:, c * 512 + hl * 128:c * 512 + (hl + 1) * 128],
                   start=True, stop=True,
                   waits=tuple(kw) if hl == 0 else WAITS_NONE)
            inc_last("tensor", CPE)
            mm2 = w(CPE)
            st["intra_m"][slot] = mm2
            marks[("scanstep", tile, g, c)] = mm2
            dqw = [mm2]
            if tile == 0 and g == 0 and c == 0:
                dqw.append(dqv_dma)
            for hl in range(QH):
                h = g * QH + hl
                op("scalar", "activation",
                   out=T["TMPOE"][:, hl * 128:(hl + 1) * 128],
                   in_=T["POE"][:, hl * 128:(hl + 1) * 128],
                   func=AF.Identity, scale=T["DQV"][:, h:h + 1],
                   waits=tuple(dqw) if hl == 0 else WAITS_NONE)
            st["dq"] = w(CACT)
            dw = [st["dq"]]
            if st["sadd"]:
                dw.append(st["sadd"])
            if st["sb"]:
                dw.append(st["sb"])
            if tile == 0 and g == 0 and c == 0:
                dw += [lamc_dma, minit]
            for hl in range(QH):
                h = g * QH + hl
                op("scalar", "activation",
                   out=T["S"][:, h * HD:(h + 1) * HD],
                   in_=T["S"][:, h * HD:(h + 1) * HD], func=AF.Identity,
                   scale=T["LAMCT"][:, h:h + 1],
                   waits=tuple(dw) if hl == 0 else WAITS_NONE)
            decay_done = w(CACT)
            oqw = [st["dq"]]
            if c == 0:
                pq = (tile, g - 1) if g > 0 else (
                    (tile - 1, NQ - 1) if tile > 0 else None)
                pv = spill_o.get(pq) if pq else None
                if pv:
                    oqw.append(pv)
            op("vector", "tensor_tensor", T["OQ"][:, c * 512:(c + 1) * 512],
               T["POI"][:, :], T["TMPOE"][:, :], op=ALU.add,
               waits=tuple(oqw), inc=CDVE)
            st["oevac"] = w(CDVE)
            op("vector", "tensor_tensor", T["S"][:, g * 512:(g + 1) * 512],
               T["S"][:, g * 512:(g + 1) * 512], T["PKV"][:, :],
               op=ALU.add, waits=(decay_done, mm2), inc=CDVE)
            st["sadd"] = w(CDVE)
            op("vector", "tensor_copy", T["SB"][:, g * 512:(g + 1) * 512],
               T["S"][:, g * 512:(g + 1) * 512], inc=CDVE)
            st["sb"] = w(CDVE)

            if c == TPT - 1:
                marks[("scan", tile, g)] = mm2
                spill_o[(tile, g)] = dma(
                    "sync",
                    o_spill[tile * TPT:(tile + 1) * TPT, :,
                            g * 512:(g + 1) * 512].rearrange(
                        "tc t e -> t tc e"),
                    T["OQ"][:].rearrange("p (tc e) -> p tc e", tc=TPT),
                    waits=(st["oevac"],))
                spill_q[(tile, g)] = []
                for c_ in range(TPT):
                    spill_q[(tile, g)].append(dma(
                        "sync",
                        q_spill[tile * TPT + c_,
                                g * QH:(g + 1) * QH].rearrange(
                            "h hd t -> hd h t"),
                        T["QFM"][:].rearrange("p (h t) -> p h t", h=QH)[
                            :, :, c_ * CH:(c_ + 1) * CH],
                        waits=(mm2,) if c_ == 0 else WAITS_NONE))
                if tile == 1:
                    sdma = dma("gpsimd", cc_s_in[g].ap(),
                               T["S"][:, g * 512:(g + 1) * 512],
                               waits=(st["sadd"],))
                    op("gpsimd", "nop", waits=(sdma,))
                    if not SKIP_CC:
                        op("gpsimd", "collective_compute", "AllGather",
                           ALU.bypass,
                           replica_groups=[[0, 1], [2, 3], [4, 5], [6, 7]],
                           ins=[cc_s_in[g].ap().opt()],
                           outs=[cc_s_out[g].ap().opt()], inc=CCP[1 + g])
                    else:
                        op("gpsimd", "nop", inc=CCP[1 + g])
                    s16_sent[g] = (CCP[1 + g], CCP[1 + g].n)

        def quad_phase(tile, g):
            prev_quad = (tile, g - 1) if g > 0 else (
                (tile - 1, NQ - 1) if tile > 0 else None)

            wdmas = {}
            for wname, wt, pj in (("WQ", wq, "q"), ("WK", wk, "k"),
                                  ("WV", wv, "v")):
                if tile == 0 and g == 0:
                    wdmas[pj] = marks[("wpre", pj)]
                    continue
                pv = marks.get(("proj", pj) + prev_quad) if prev_quad else None
                wdmas[pj] = dma(
                    "gpsimd",
                    T[wname][:].rearrange("p (dc e) -> p dc e", dc=NDC),
                    wt[:, g * 512:(g + 1) * 512].rearrange(
                        "(dc p) e -> p dc e", p=128),
                    waits=(pv,) if pv else WAITS_NONE)
            drip_fac(16)

            ppfree = {}
            rotp = {}
            tpp = {}
            silup = {}
            mmk = {}

            def proj_chunk(which, wsb, c, wi):
                gi = c * 3 + wi
                slot = gi % 2
                pw = [ln_chunk_evac[(tile, c)], wdmas[which]]
                if ppfree.get(gi - 2):
                    pw.append(ppfree[gi - 2])
                elif gi < 2:
                    if tile == 1 and g == 0:
                        pw.append(g1_done[0])
                    if prev_quad:
                        pv = marks.get(("projevac",) + prev_quad)
                        if pv:
                            pw.append(pv)
                for dc in range(NDC):
                    op("tensor", "matmul", T["PP"][slot][:, :],
                       T["XLNFM"][:, dc * 1024 + c * CH:dc * 1024 + (c + 1) * CH],
                       wsb[:, dc * 512:(dc + 1) * 512],
                       start=(dc == 0), stop=(dc == NDC - 1),
                       waits=tuple(pw) if dc == 0 else WAITS_NONE)
                inc_last("tensor", CPE)
                mmd = w(CPE)
                mmk[(which, c)] = mmd
                prevscan = marks.get(("scanstep",) + prev_quad + (c,)) \
                    if prev_quad else None

                if which == "v":
                    ew = [mmd]
                    if prevscan:
                        ew.append(prevscan)
                    op("scalar", "activation",
                       out=T["V"][:, c * 512:(c + 1) * 512],
                       in_=T["PP"][slot][:, :], func=AF.Identity,
                       waits=tuple(ew), inc=CACT)
                    ppfree[gi] = w(CACT)
                    sstate["vev"][(g, c)] = w(CACT)
                    return

                QSB = T["QSB"][0 if which == "q" else 1]
                QTM = T["QTM"][0 if which == "q" else 1]
                sw = [mmd]
                if rotp.get((which, c - 1)):
                    sw.append(rotp[(which, c - 1)])
                elif pstate.get(("rot", which)):
                    sw.append(pstate[("rot", which)])
                op("scalar", "activation", out=QSB[:, :],
                   in_=T["PP"][slot][:, :], func=AF.Silu, waits=tuple(sw),
                   inc=CACT)
                silup[(which, c)] = w(CACT)
                ppfree[gi] = w(CACT)
                tc = tile * TPT + c
                q3 = QSB[:].rearrange("p (h e) -> p h e", h=QH)
                o3 = QTM[:].rearrange("p (h e) -> p h e", h=QH)
                cosap = T["COS"][:, tc * 64:(tc + 1) * 64][:, None, :] \
                    .broadcast_to([128, QH, 64])
                sinap = T["SIN"][:, tc * 64:(tc + 1) * 64][:, None, :] \
                    .broadcast_to([128, QH, 64])
                t1 = T["TMP1"][:].rearrange("p (h e) -> p h e", h=QH)
                t2 = T["TMP2"][:].rearrange("p (h e) -> p h e", h=QH)
                rw = [silup[(which, c)]]
                if tile == 0 and g == 0 and c == 0:
                    rw += [cos_dma, sin_dma]
                if tpp.get((which, c - 1)):
                    rw.append(tpp[(which, c - 1)])
                elif pstate.get(("tp", which)):
                    rw.append(pstate[("tp", which)])
                op("vector", "tensor_tensor", t1, q3[:, :, 0:64], cosap,
                   op=ALU.mult, waits=tuple(rw))
                op("vector", "tensor_tensor", t2, q3[:, :, 64:128], sinap,
                   op=ALU.mult)
                op("vector", "tensor_tensor", o3[:, :, 0:64], t1, t2,
                   op=ALU.subtract)
                op("vector", "tensor_tensor", t1, q3[:, :, 0:64], sinap,
                   op=ALU.mult)
                op("vector", "tensor_tensor", t2, q3[:, :, 64:128], cosap,
                   op=ALU.mult)
                op("vector", "tensor_tensor", o3[:, :, 64:128], t1, t2,
                   op=ALU.add, inc=CDVE)
                rotp[(which, c)] = w(CDVE)
                if which == "k":
                    kw = [w(CDVE)]
                    if tile == 0 and g == 0 and c == 0:
                        kw.append(dkt_dma)
                    if prevscan:
                        kw.append(prevscan)
                    op("vector", "tensor_tensor",
                       T["K2"][:, c * 512:(c + 1) * 512], QTM[:, :],
                       T["DKT"][:, g * 512:(g + 1) * 512], op=ALU.mult,
                       waits=tuple(kw), inc=CDVE)
                    rotp[(which, c)] = w(CDVE)

            def proj_transpose(which, c):
                QTM = T["QTM"][0 if which == "q" else 1]
                dst_fm = T["QFM"] if which == "q" else T["KFM"]
                off = 0 if which == "q" else 1024
                prevscan = marks.get(("scanstep",) + prev_quad + (c,)) \
                    if prev_quad else None
                tw = [rotp[(which, c)]]
                if tpp.get((which, c - 1)):
                    tw.append(marks.get(("fme", which, tile, g, c - 1)))
                elif marks.get(("lastPT", tile)):
                    tw.append(marks[("lastPT", tile)])
                for hl in range(QH):
                    op("tensor", "transpose",
                       T["PT"][:, off + hl * 128:off + (hl + 1) * 128],
                       QTM[:, hl * 128:(hl + 1) * 128], T["IDN"][:, :],
                       waits=tuple(tw) if hl == 0 else WAITS_NONE)
                inc_last("tensor", CPE)
                tpp[(which, c)] = w(CPE)
                ew = [tpp[(which, c)]]
                if prevscan:
                    ew.append(prevscan)
                pq = prev_quad
                if pq and spill_q.get(pq):
                    ew.append(spill_q[pq][c])
                dstap = dst_fm[:].rearrange("p (h t) -> p h t", h=QH)[
                    :, :, c * CH:(c + 1) * CH]
                srcap = T["PT"][:, off:off + 512].rearrange(
                    "p (h t) -> p h t", h=QH)
                op("vector", "tensor_copy", dstap, srcap, waits=tuple(ew),
                   inc=CDVE)
                fm = w(CDVE)
                marks[("fme", which, tile, g, c)] = fm
                if which == "q":
                    sstate["fme_q"][(g, c)] = fm
                else:
                    sstate["fme_k"][(g, c)] = fm

            for c in range(TPT):
                if c == 0 and pending_scan[0] is not None:
                    scan_step(*pending_scan[0])
                    pending_scan[0] = None
                proj_chunk("q", T["WQ"], c, 0)
                proj_chunk("k", T["WK"], c, 1)
                proj_chunk("v", T["WV"], c, 2)
                if c > 0:
                    scan_step(tile, g, c - 1)
                proj_transpose("q", c)
                proj_transpose("k", c)
            marks[("proj", "q", tile, g)] = mmk[("q", TPT - 1)]
            marks[("proj", "k", tile, g)] = mmk[("k", TPT - 1)]
            marks[("proj", "v", tile, g)] = mmk[("v", TPT - 1)]
            marks[("projevac", tile, g)] = w(CACT)
            marks[("fme", "k", tile, g)] = marks[("fme", "k", tile, g, TPT - 1)]
            marks[("lastPT", tile)] = marks[("fme", "k", tile, g, TPT - 1)]
            pstate[("rot", "q")] = rotp[("q", TPT - 1)]
            pstate[("rot", "k")] = rotp[("k", TPT - 1)]
            pstate[("tp", "q")] = tpp[("q", TPT - 1)]
            pstate[("tp", "k")] = tpp[("k", TPT - 1)]
            if g == NQ - 1:
                scan_step(tile, g, TPT - 1)
                pending_scan[0] = None
            else:
                pending_scan[0] = (tile, g, TPT - 1)

        fsum_holder = [None]
        def tail_phase(tile, pre_c2=None):
            S16ST = T[f"S16ST{tile}"]
            OCH = T[f"OCH{tile}"]
            QCH = T[f"QCH{tile}"]
            TMPC = T[f"TMPC{tile}"]
            GATS = T[f"GATS{tile}"]
            XG = T[f"XG{tile}"]
            SQB = T[f"SQB{tile}"]
            OGT = T[f"OGT{tile}"]
            PCORR = T[f"PCORR{tile}"]
            PGATE = T[f"PGATE{tile}"]
            PT2 = T[f"PT2{tile}"]

            s16marks = []
            for g in range(NQ):
                s16marks.append(dma(
                    "sync", S16ST[:, g * 512:(g + 1) * 512],
                    cc_s_out[g][0:HD, :], waits=(s16_sent[g],)))
            S16B = T[f"S16B{tile}"]
            op("vector", "tensor_copy", S16B[:, :], S16ST[:, :],
               waits=tuple(s16marks), inc=CDVE)
            s16b = w(CDVE)
            if "d_s16" in debug and tile == 0:
                dma("sync", dbg["d_s16"].ap(), S16ST[:, :], waits=(s16b,))

            sig_done = [None] * TPT
            corr_add = [None] * TPT
            qch_used = [None] * TPT
            prev_vadd = None
            for c in range(TPT):
                tc = tile * TPT + c
                slot = c % 2
                odma = dma("sync", OCH[:, c * D:(c + 1) * D], o_spill[tc],
                           waits=tuple(spill_o[(tile, gg)]
                                       for gg in range(NQ)) if c == 0
                           else WAITS_NONE)
                qw = []
                if c == 0:
                    for gg in range(NQ):
                        qw.extend(spill_q[(tile, gg)])
                if c >= 2 and qch_used[c - 2]:
                    qw.append(qch_used[c - 2])
                qdma = dma("sync",
                           QCH[slot][:].rearrange("p (h t) -> p h t", h=NH),
                           q_spill[tc].rearrange("h hd t -> hd h t"),
                           waits=tuple(qw))
                for g in range(NQ):
                    ps_ = PCORR[g % 2]
                    cw = [qdma, s16b]
                    if g >= 2:
                        cw.append(marks[("corrva", tile, c, g - 2)])
                    elif c >= 1:
                        cw.append(marks[("corrva", tile, c - 1, g + 2)])
                    for hl in range(QH):
                        h = g * QH + hl
                        op("tensor", "matmul", ps_[:, hl * 128:(hl + 1) * 128],
                           QCH[slot][:, h * 128:(h + 1) * 128],
                           S16B[:, h * HD:(h + 1) * HD],
                           start=True, stop=True,
                           waits=tuple(x for x in cw if x) if hl == 0
                           else WAITS_NONE)
                    inc_last("tensor", CPE)
                    cmm = w(CPE)
                    scw = [cmm]
                    if prev_vadd:
                        scw.append(prev_vadd)
                    for hl in range(QH):
                        h = g * QH + hl
                        op("scalar", "activation",
                           out=TMPC[:, hl * 128:(hl + 1) * 128],
                           in_=ps_[:, hl * 128:(hl + 1) * 128], func=AF.Identity,
                           scale=T["CORR"][:, tc * NH + h:tc * NH + h + 1],
                           waits=tuple(scw) if hl == 0 else WAITS_NONE)
                    inc_last("scalar", CACT)
                    csc = w(CACT)
                    op("vector", "tensor_tensor",
                       OCH[:, c * D + g * 512:c * D + (g + 1) * 512],
                       OCH[:, c * D + g * 512:c * D + (g + 1) * 512],
                       TMPC[:, :], op=ALU.add, waits=(csc, odma), inc=CDVE)
                    marks[("corrva", tile, c, g)] = w(CDVE)
                    prev_vadd = w(CDVE)
                corr_add[c] = w(CDVE)
                qch_used[c] = w(CPE)

                gw = [g1_done[tile], consts_dma]
                if c >= 1 and sig_done[c - 1]:
                    gw.append(sig_done[c - 1])
                for q4 in range(4):
                    op("tensor", "matmul", PGATE[:, q4 * 512:(q4 + 1) * 512],
                       T["G1"][:, tile * 1024 + c * CH:tile * 1024 + (c + 1) * CH],
                       T["WG2B"][:, q4 * 512:(q4 + 1) * 512],
                       start=True, stop=True,
                       waits=tuple(gw) if q4 == 0 else WAITS_NONE)
                inc_last("tensor", CPE)
                gmm = w(CPE)
                for q4 in range(4):
                    op("scalar", "activation",
                       out=GATS[:, c * D + q4 * 512:c * D + (q4 + 1) * 512],
                       in_=PGATE[:, q4 * 512:(q4 + 1) * 512], func=AF.Sigmoid,
                       waits=(gmm,) if q4 == 0 else WAITS_NONE)
                inc_last("scalar", CACT)
                sig_done[c] = w(CACT)
            if "d_gats" in debug and tile == 0:
                dma("sync", dbg["d_gats"].ap(), GATS[:, :],
                    waits=(w(CACT),))

            ogfm_e = [None] * TPT
            sq_marks = [None] * TPT
            for c in range(TPT):
                xw = [corr_add[c], sig_done[c]]
                if c >= 1 and ogfm_e[c - 1]:
                    xw.append(ogfm_e[c - 1])
                if c >= 1 and sq_marks[c - 1]:
                    xw.append(sq_marks[c - 1])
                op("vector", "tensor_tensor", XG[:, :], OCH[:, c * D:(c + 1) * D],
                   GATS[:, c * D:(c + 1) * D], op=ALU.mult, waits=tuple(xw),
                   inc=CDVE)
                xg_done = w(CDVE)
                op("scalar", "activation", out=SQB[:, :], in_=XG[:, :],
                   func=AF.Square, waits=(xg_done,), inc=CACT)
                sq_marks[c] = w(CACT)
                op("vector", "tensor_reduce", T["MS"][:, :],
                   SQB[:].rearrange("p (h e) -> p h e", h=NH), AX.X, ALU.add,
                   waits=(w(CACT),), inc=CDVE)
                op("scalar", "activation", out=T["SD2"][:, :], in_=T["MS"][:, :],
                   func=AF.Sqrt, scale=T["EPSC"][:, 2:3],
                   bias=T["EPSC"][:, 1:2], waits=(w(CDVE),), inc=CACT)
                op("vector", "reciprocal", T["R2"][:, :], T["SD2"][:, :],
                   waits=(w(CACT),))
                inc_last("vector", CDVE)
                r2d = w(CDVE)
                for h in range(NH):
                    op("vector", "tensor_scalar",
                       out=OGT[:, h * HD:(h + 1) * HD],
                       in0=XG[:, h * HD:(h + 1) * HD],
                       scalar1=T["R2"][:, h:h + 1], scalar2=None, op0=ALU.mult,
                       waits=(r2d,) if h == 0 else WAITS_NONE)
                inc_last("vector", CDVE)
                ogd = w(CDVE)
                tw = [ogd]
                if c >= 1 and ogfm_e[c - 1]:
                    tw.append(ogfm_e[c - 1])
                if c == 0 and tile == 1:
                    tw.append(marks[("ogfmall", 0)])
                for dc in range(NDC):
                    op("tensor", "transpose", PT2[:, dc * 128:(dc + 1) * 128],
                       OGT[:, dc * 128:(dc + 1) * 128], T["IDN"][:, :],
                       waits=tuple(tw) if dc == 0 else WAITS_NONE)
                inc_last("tensor", CPE)
                tp2 = w(CPE)
                for dc in range(NDC):
                    op("vector", "tensor_scalar",
                       out=T["XLNFM"][:, dc * 1024 + c * CH:dc * 1024 + (c + 1) * CH],
                       in0=PT2[:, dc * 128:(dc + 1) * 128],
                       scalar1=T["NW"][:, dc:dc + 1], scalar2=None, op0=ALU.mult,
                       waits=(tp2,) if dc == 0 else WAITS_NONE)
                inc_last("vector", CDVE)
                ogfm_e[c] = w(CDVE)
            marks[("ogfmall", tile)] = w(CDVE)
            og_all = w(CDVE)
            if "d_ogfm" in debug and tile == 0:
                dma("sync", dbg["d_ogfm"].ap(), T["XLNFM"][:, :],
                    waits=(og_all,))

            fence()
            WOH = T[f"WOH{tile}"]
            XRCH = T[f"XRCH{tile}"]
            X2H = T[f"X2H{tile}"]
            TRASH2 = T[f"TRASH2{tile}"]
            PB = T[f"PB{tile}"]
            x2w = [[None] * TPT, [None] * TPT]
            x2free = {}
            pbe_all = []
            for H in range(2):
                wodma = dma("gpsimd",
                            WOH[:].rearrange("p (dc e) -> p dc e", dc=NDC),
                            wo[:, H * 1024:(H + 1) * 1024].rearrange(
                                "(dc p) e -> p dc e", p=128),
                            waits=(w(CPE),) if H == 1 else WAITS_NONE)
                pbe = [None] * TPT
                for c in range(TPT):
                    tc = tile * TPT + c
                    slot = c % 2
                    gi = H * TPT + c
                    rw = [pbe_all[gi - 2]] if gi >= 2 else []
                    xrdma = dma("sync", XRCH[slot][:, :],
                                x_sl[tc * CH:(tc + 1) * CH,
                                     H * 1024:(H + 1) * 1024],
                                waits=tuple(x for x in rw if x))
                    mw = [og_all, wodma]
                    if gi >= 2 and pbe_all[gi - 2]:
                        mw.append(pbe_all[gi - 2])
                    for dc in range(NDC):
                        for n in range(2):
                            op("tensor", "matmul",
                               PB[slot][:, n * 512:(n + 1) * 512],
                               T["XLNFM"][:, dc * 1024 + c * CH:dc * 1024 + (c + 1) * CH],
                               WOH[:, dc * 1024 + n * 512:dc * 1024 + (n + 1) * 512],
                               start=(dc == 0), stop=(dc == NDC - 1),
                               waits=tuple(mw) if (dc == 0 and n == 0)
                               else WAITS_NONE)
                    inc_last("tensor", CPE)
                    bmm = w(CPE)
                    ew = [bmm, xrdma]
                    ew += list(x2free.get(gi - 2, ()))
                    op("vector", "tensor_tensor", X2H[slot][:, :],
                       PB[slot][:, :], XRCH[slot][:, :],
                       op=ALU.add, waits=tuple(ew), inc=CDVE)
                    ev = w(CDVE)
                    pbe[c] = ev
                    pbe_all.append(ev)
                    op("vector", "tensor_reduce",
                       T["S2S"][:, c * 2 + H:c * 2 + H + 1], X2H[slot][:, :],
                       AX.X, ALU.add, inc=CDVE)
                    red = w(CDVE)
                    op("scalar", "activation", out=TRASH2[:, :],
                       in_=X2H[slot][:, :], func=AF.Square,
                       accum_out=T["Q2S"][:, c * 2 + H:c * 2 + H + 1],
                       waits=(ev,))
                    op("scalar", "activation", out=T["DUMT"][:, :],
                       in_=T["EPSC"][:, 0:1], func=AF.Identity, inc=CACT)
                    sqd = w(CACT)
                    x2w[H][c] = dma(
                        "gpsimd",
                        out_sl[tc * CH:(tc + 1) * CH,
                               H * 1024:(H + 1) * 1024],
                        X2H[slot][:, :], waits=(ev,))
                    x2free[gi] = (x2w[H][c], sqd, red)
            bstats = (w(CDVE), w(CACT))

            fence()
            XB2 = T[f"XB2{tile}"]
            XLNT2 = T[f"XLNT2{tile}"]
            PT3 = T[f"PT3{tile}"]
            xl2e = [None] * TPT
            nrm_done = [None] * TPT
            if "d_stats" in debug and tile == 0:
                dma("sync", dbg["d_stats"][:, 0:16], T["S2S"][:, :],
                    waits=bstats)
                dma("sync", dbg["d_stats"][:, 16:32], T["Q2S"][:, :],
                    waits=bstats)
            for c in range(TPT):
                tc = tile * TPT + c
                slot = c % 2
                rw = [nrm_done[c - 2]] if c >= 2 else []
                rw += [x2w[0][c], x2w[1][c]]
                xbdma = dma("sync", XB2[slot][:, :],
                            out_sl[tc * CH:(tc + 1) * CH, :],
                            waits=tuple(x for x in rw if x))
                if "d_x2" in debug and tile == 0:
                    dma("sync", dbg["d_x2"][:, c * D:(c + 1) * D],
                        XB2[slot][:, :], waits=(xbdma,))
                mcol = T["STATS"][:, c:c + 1]
                vcol = T["STATS"][:, 8 + c:9 + c]
                rcol = T["STATS"][:, 16 + c:17 + c]
                bcol = T["STATS"][:, 24 + c:25 + c]
                tcol = T["STATS"][:, 32 + c:33 + c]
                scol = T["STATS"][:, 40 + c:41 + c]
                op("vector", "tensor_tensor", mcol,
                   T["S2S"][:, c * 2:c * 2 + 1],
                   T["S2S"][:, c * 2 + 1:c * 2 + 2], op=ALU.add,
                   waits=bstats if c == 0 else WAITS_NONE)
                op("vector", "tensor_scalar", out=mcol, in0=mcol,
                   scalar1=RECIP_D, scalar2=None, op0=ALU.mult)
                op("vector", "tensor_tensor", vcol,
                   T["Q2S"][:, c * 2:c * 2 + 1],
                   T["Q2S"][:, c * 2 + 1:c * 2 + 2], op=ALU.add)
                op("vector", "tensor_scalar", out=vcol, in0=vcol,
                   scalar1=RECIP_D, scalar2=None, op0=ALU.mult)
                op("vector", "tensor_tensor", tcol, mcol, mcol, op=ALU.mult)
                op("vector", "tensor_tensor", vcol, vcol, tcol,
                   op=ALU.subtract, inc=CDVE)
                op("scalar", "activation", out=scol, in_=vcol, func=AF.Sqrt,
                   bias=T["EPSC"][:, 0:1], waits=(w(CDVE),), inc=CACT)
                op("vector", "reciprocal", rcol, scol, waits=(w(CACT),))
                op("vector", "tensor_scalar", out=bcol, in0=mcol, scalar1=rcol,
                   scalar2=-1.0, op0=ALU.mult, op1=ALU.mult, inc=CDVE)
                nwt = [w(CDVE), xbdma]
                if c >= 2 and nrm_done[c - 2]:
                    nwt.append(nrm_done[c - 2])
                op("scalar", "activation", out=XLNT2[slot][:, :],
                   in_=XB2[slot][:, :], func=AF.Identity, scale=rcol, bias=bcol,
                   waits=tuple(nwt), inc=CACT)
                nrm_done[c] = w(CACT)
                tw = [nrm_done[c]]
                if c >= 1 and xl2e[c - 1]:
                    tw.append(xl2e[c - 1])
                for dc in range(NDC):
                    op("tensor", "transpose", PT3[:, dc * 128:(dc + 1) * 128],
                       XLNT2[slot][:, dc * 128:(dc + 1) * 128], T["IDN"][:, :],
                       waits=tuple(tw) if dc == 0 else WAITS_NONE)
                inc_last("tensor", CPE)
                tpd = w(CPE)
                dst = T[f"XLN2FM{tile}"][:].rearrange("p (dc t) -> p dc t", dc=NDC)[
                    :, :, c * CH:(c + 1) * CH]
                src = PT3[:].rearrange("p (dc t) -> p dc t", dc=NDC)
                op("vector", "tensor_scalar", out=dst, in0=src,
                   scalar1=float(1.0 / np.sqrt(8.0)), scalar2=None,
                   op0=ALU.mult, waits=(tpd,), inc=CDVE)
                xl2e[c] = w(CDVE)
            xl2all = w(CDVE)
            if "d_stats2" in debug and tile == 0:
                dma("sync", dbg["d_stats2"].ap(), T["STATS"][:, :],
                    waits=(w(CACT), w(CDVE)))
            if "d_xln2fm" in debug and tile == 0:
                dma("sync", dbg["d_xln2fm"].ap(), T[f"XLN2FM{tile}"][:, :],
                    waits=(xl2all,))

            if pre_c2 is not None:
                fsum_holder[0] = pre_c2()
            fsum_local = fsum_holder[0]
            FSP = T[f"FSP{tile}"]
            OUTES = T[f"OUTES{tile}"]
            PF = T[f"PF{tile}"]
            fsp_used = [None] * 4
            prev_pf_evac = None
            prev_outes = [None, None]
            for es in range(4):
                slot = es % 2
                fw = [fsum_local]
                if es >= 2 and fsp_used[es - 2]:
                    fw.append(fsp_used[es - 2])
                fsdma = []
                for rr in range(2):
                    dstp = FSP[slot][:].rearrange("p (dc e) -> p dc e", dc=NDC)[
                        :, :, rr * 256:(rr + 1) * 256]
                    fsdma.append(dma(
                        "gpsimd", dstp,
                        cc_fs_out[(2 * es + rr) * D:(2 * es + rr + 1) * D]
                        .rearrange("(dc p) e -> p dc e", p=128),
                        waits=tuple(fw) if rr == 0 else WAITS_NONE))
                for half in range(2):
                    hw_ = [xl2all] + fsdma
                    if prev_pf_evac:
                        hw_.append(prev_pf_evac)
                    for tcq in range(4):
                        c = half * 4 + tcq
                        op("tensor", "matmul", PF[tcq][:, :], T["ONES"][:, :],
                           T["BIASB"][:, es * 512:(es + 1) * 512],
                           start=True, stop=False,
                           waits=tuple(x for x in hw_ if x) if tcq == 0
                           else WAITS_NONE)
                        for dc in range(NDC):
                            op("tensor", "matmul", PF[tcq][:, :],
                               T[f"XLN2FM{tile}"][:, dc * 1024 + c * CH:dc * 1024 + (c + 1) * CH],
                               FSP[slot][:, dc * 512:(dc + 1) * 512],
                               start=False, stop=(dc == NDC - 1))
                    inc_last("tensor", CPE)
                    fmm = w(CPE)
                    evw = [fmm]
                    if prev_outes[half]:
                        evw.append(prev_outes[half])
                    for tcq in range(4):
                        op("vector", "tensor_copy",
                           OUTES[half][:, tcq * 512:(tcq + 1) * 512],
                           PF[tcq][:, :], waits=tuple(evw) if tcq == 0
                           else WAITS_NONE)
                    inc_last("vector", CDVE)
                    evd = w(CDVE)
                    prev_pf_evac = evd
                    prev_outes[half] = dma("gpsimd",
                        out_sl[tile * 1024 + half * 512:
                               tile * 1024 + (half + 1) * 512,
                               es * 512:(es + 1) * 512].rearrange(
                            "(tc t) e -> t tc e", t=CH),
                        OUTES[half][:].rearrange("p (tc e) -> p tc e", tc=4),
                        waits=(evd,), accum_op=ALU.add)
                    _ = prev_outes
                fsp_used[es] = w(CPE)

        def trunc(name):
            return TRUNCATE == name

        while True:
            if trunc("prelude"):
                break
            ln_phase(0)
            if trunc("ln0") or (TRUNCATE or "").startswith("ln0c") \
                    or (TRUNCATE or "").startswith("ln0d") \
                    or (TRUNCATE or "").startswith("ln0r") \
                    or (TRUNCATE or "").startswith("ln0s") \
                    or (TRUNCATE or "").startswith("ln0n") \
                    or TRUNCATE == "ln0tp":
                break
            for g in range(NQ):
                quad_phase(0, g)
            marks[("allproj", 0)] = marks[("proj", "v", 0, NQ - 1)]
            g1_phase(0)
            if trunc("quads0"):
                break
            ln_phase(1)
            if trunc("ln1"):
                break
            for g in range(NQ):
                quad_phase(1, g)
            g1_phase(1)
            if trunc("quads1"):
                break
            fence()
            tail_phase(0, pre_c2=fsum_gate)
            if trunc("tail0"):
                break
            fence()
            tail_phase(1)
            break

        for s in GPP:
            op("gpsimd", "nop", waits=((s, s.n),))
        for s in SXP:
            op("sync", "nop", waits=((s, s.n),))

    with nc.Block() as block:
        block.sync(lambda e: walk("sync", e))
        block.scalar(lambda e: walk("scalar", e))
        block.vector(lambda e: walk("vector", e))
        block.tensor(lambda e: walk("tensor", e))
        block.gpsimd(lambda e: walk("gpsimd", e))

    top.close()
    return nc


def _host_tables(core):
    half = core % 2
    p0 = half * NT
    ll = _LL.astype(np.float32)
    theta = (np.float32(10000.0) ** (np.float32(-2.0)
             * np.arange(64, dtype=np.float32) / np.float32(HD))).astype(np.float32)
    pos = np.arange(p0, p0 + NT, dtype=np.float32)
    ang = (pos[:, None] * theta[None, :]).astype(np.float32)
    cos_t = np.cos(ang.astype(np.float64)).astype(np.float32)
    sin_t = np.sin(ang.astype(np.float64)).astype(np.float32)
    i = np.arange(CH, dtype=np.float64)
    diff = i[None, :] - i[:, None]
    dm = np.where(diff >= 0, np.exp(ll[:, None, None].astype(np.float64)
                                    * diff[None]), 0.0)
    dmask = np.ascontiguousarray(
        dm.transpose(1, 0, 2).reshape(CH, NH * CH)).astype(np.float32)
    dk = np.exp(ll[None, :].astype(np.float64)
                * (CH - 1.0 - i[:, None])).astype(np.float32)
    dkt = np.repeat(dk[:, :, None], CH, axis=2).reshape(CH, NH * CH)
    dkt = np.ascontiguousarray(dkt).astype(np.float32)
    dqv = np.exp(ll[None, :].astype(np.float64) * (i[:, None] + 1.0)
                 ).astype(np.float32)
    oddf = 1.0 if (core % 2 == 1) else 0.0
    lamc = np.exp(ll.astype(np.float64) * CH)
    corr = np.zeros((CH, TCC * NH), np.float64)
    for c in range(TCC):
        corr[:, c * NH:(c + 1) * NH] = dqv.astype(np.float64) * (lamc ** c)[None, :]
    corr = (corr * oddf).astype(np.float32)
    lamct = np.tile(lamc.astype(np.float32)[None, :], (CH, 1))
    return cos_t, sin_t, dmask, dkt, dqv, corr, lamct


_NC_CACHE = {}


def _get_nc(debug=()):
    key = tuple(sorted(debug))
    if key not in _NC_CACHE:
        _NC_CACHE[key] = build_program(debug=key)
    return _NC_CACHE[key]


def kernel(L0_attn_x, L0_attn_Wq, L0_attn_Wk, L0_attn_Wv, L0_attn_Wo,
           L0_attn_Wg1, L0_attn_Wg2, L0_attn_norm_weight, L0_ffn_factors,
           L0_ffn_bias, _debug=(), _nc=None):
    x = np.ascontiguousarray(np.asarray(L0_attn_x, np.float32))
    weights = {
        "wq": np.ascontiguousarray(np.asarray(L0_attn_Wq, np.float32)),
        "wk": np.ascontiguousarray(np.asarray(L0_attn_Wk, np.float32)),
        "wv": np.ascontiguousarray(np.asarray(L0_attn_Wv, np.float32)),
        "wo": np.ascontiguousarray(np.asarray(L0_attn_Wo, np.float32)),
        "wg1": np.ascontiguousarray(np.asarray(L0_attn_Wg1, np.float32)),
        "wg2": np.ascontiguousarray(np.asarray(L0_attn_Wg2, np.float32)),
        "nw_in": np.ascontiguousarray(np.asarray(L0_attn_norm_weight, np.float32)),
        "bias_in": np.ascontiguousarray(
            np.asarray(L0_ffn_bias, np.float32).reshape(1, D)),
    }
    factors = np.asarray(L0_ffn_factors, np.float32)
    iden = np.eye(128, dtype=np.float32)

    nc = _nc if _nc is not None else _get_nc(_debug)
    in_maps = []
    for core in range(8):
        b, half = core // 2, core % 2
        cos_t, sin_t, dmask, dkt, dqv, corr, lamct = _host_tables(core)
        m = dict(weights)
        m["x_sl"] = np.ascontiguousarray(x[b, half * NT:(half + 1) * NT, :])
        m["fac_sl"] = np.ascontiguousarray(
            factors[:, :, core * 256:(core + 1) * 256])
        m["cos_in"] = cos_t
        m["sin_in"] = sin_t
        m["dmask_in"] = dmask
        m["dkt_in"] = dkt
        m["dqv_in"] = dqv
        m["corr_in"] = corr
        m["lamc_in"] = lamct
        m["idn_in"] = iden
        in_maps.append(m)

    res = run_bass_kernel_spmd(nc, in_maps, core_ids=list(range(8)))
    out = np.empty((4, 4096, D), np.float32)
    for core in range(8):
        b, half = core // 2, core % 2
        out[b, half * NT:(half + 1) * NT, :] = res.results[core]["out_sl"]
    if _debug:
        return out, res
    return out


# version 15
# speedup vs baseline: 1.0774x; 1.0404x over previous
import numpy as np
from contextlib import ExitStack

import concourse.bass as bass
import concourse.mybir as mybir
from concourse.bass_utils import run_bass_kernel_spmd

F32 = mybir.dt.float32
BF16 = mybir.dt.bfloat16
AF = mybir.ActivationFunctionType
ALU = mybir.AluOpType
AX = mybir.AxisListType

D = 2048
NT = 2048
CH = 128
TCC = 16
TPT = 8
NH = 16
HD = 128
QH = 4
NQ = 4
NDC = 16
EPS_LN = 1e-5
EPS_RMS = 1e-6
RECIP_D = 1.0 / 2048.0

ENGINES = ("sync", "scalar", "vector", "tensor", "gpsimd")
RACE_DETECT = False
PE_DENSE_INC = False
TRUNCATE = None
EVAC_MODE = 0
SKIP_CC = False

_LL = -(2.0 ** (-8.0 * (np.arange(NH) + 1.0) / NH))
_LAMC = np.exp(_LL * CH)


class Sem:
    __slots__ = ("h", "n")

    def __init__(self, h):
        self.h = h
        self.n = 0


def build_program(debug=()):
    nc = bass.Bass("TRN2", target_bir_lowering=False, debug=False,
                   detect_race_conditions=RACE_DETECT)

    x_sl = nc.dram_tensor("x_sl", [NT, D], F32, kind="ExternalInput")
    wq = nc.dram_tensor("wq", [D, D], F32, kind="ExternalInput")
    wk = nc.dram_tensor("wk", [D, D], F32, kind="ExternalInput")
    wv = nc.dram_tensor("wv", [D, D], F32, kind="ExternalInput")
    wo = nc.dram_tensor("wo", [D, D], F32, kind="ExternalInput")
    wg1 = nc.dram_tensor("wg1", [D, HD], F32, kind="ExternalInput")
    wg2 = nc.dram_tensor("wg2", [HD, D], F32, kind="ExternalInput")
    nw_in = nc.dram_tensor("nw_in", [D], F32, kind="ExternalInput")
    bias_in = nc.dram_tensor("bias_in", [1, D], F32, kind="ExternalInput")
    fac_sl = nc.dram_tensor("fac_sl", [8, D, 256], F32, kind="ExternalInput")
    cos_in = nc.dram_tensor("cos_in", [NT, 64], F32, kind="ExternalInput")
    sin_in = nc.dram_tensor("sin_in", [NT, 64], F32, kind="ExternalInput")
    dmask_in = nc.dram_tensor("dmask_in", [CH, D], F32, kind="ExternalInput")
    dkt_in = nc.dram_tensor("dkt_in", [CH, D], F32, kind="ExternalInput")
    dqv_in = nc.dram_tensor("dqv_in", [CH, NH], F32, kind="ExternalInput")
    corr_in = nc.dram_tensor("corr_in", [CH, TCC * NH], F32, kind="ExternalInput")
    idn_in = nc.dram_tensor("idn_in", [128, 128], F32, kind="ExternalInput")
    lamc_in = nc.dram_tensor("lamc_in", [CH, NH], F32, kind="ExternalInput")
    out_sl = nc.dram_tensor("out_sl", [NT, D], F32, kind="ExternalOutput")

    def dscratch(name, shape, dtype):
        kind = "ExternalOutput" if name in debug else "Internal"
        return nc.dram_tensor(name, shape, dtype, kind=kind)

    o_spill = dscratch("o_spill", [TCC, CH, D], BF16)
    q_spill = dscratch("q_spill", [TCC, NH, HD, CH], BF16)
    cc_fs_in = nc.dram_tensor("cc_fs_in", [D, 256], BF16)
    cc_fs_out = nc.dram_tensor("cc_fs_out", [8 * D, 256], BF16)
    cc_s_in = [nc.dram_tensor(f"cc_s_in{g}", [HD, QH * HD], F32) for g in range(NQ)]
    cc_s_out = [nc.dram_tensor(f"cc_s_out{g}", [2 * HD, QH * HD], F32)
                for g in range(NQ)]
    dbg = {}
    for name, shape, dt in (
        ("d_xlnfm", [128, NDC * 1024], BF16),
        ("d_g1", [128, NT], BF16),
        ("d_s16", [128, NH * HD], F32),
        ("d_gats", [128, TPT * D], BF16),
        ("d_ogfm", [128, NDC * 1024], BF16),
        ("d_xln2fm", [128, NDC * 1024], BF16),
        ("d_x2", [128, TPT * D], F32),
        ("d_stats", [128, 64], F32),
        ("d_stats2", [128, 64], F32),
        ("d_stats3", [128, 64], F32),
    ):
        if name in debug:
            dbg[name] = nc.dram_tensor(name, shape, dt, kind="ExternalOutput")

    top = ExitStack()

    def mk(stk, name, shape, dtype):
        return stk.enter_context(nc.sbuf_tensor(name, shape, dtype))

    def mkp(stk, name, shape, dtype):
        return stk.enter_context(nc.psum_tensor(name, shape, dtype))

    T = {}
    T["XLNFM"] = mk(top, "XLNFM", [128, NDC * 1024], BF16)
    T["G1"] = mk(top, "G1", [128, NT], BF16)
    T["WG2B"] = mk(top, "WG2B", [HD, D], BF16)
    T["WG1B"] = mk(top, "WG1B", [128, NDC * HD], BF16)
    T["NW"] = mk(top, "NW", [128, 16], F32)
    T["BIASB"] = mk(top, "BIASB", [1, D], BF16)
    T["ONES"] = mk(top, "ONES", [1, 128], BF16)
    T["IDN"] = mk(top, "IDN", [128, 128], BF16)
    T["DQV"] = mk(top, "DQV", [128, NH], F32)
    T["CORR"] = mk(top, "CORR", [128, TCC * NH], F32)
    T["STATS"] = mk(top, "STATS", [128, 64], F32)
    T["S2S"] = mk(top, "S2S", [128, 16], F32)
    T["Q2S"] = mk(top, "Q2S", [128, 16], F32)
    T["MS"] = mk(top, "MS", [128, 16], F32)
    T["SD2"] = mk(top, "SD2", [128, 16], F32)
    T["R2"] = mk(top, "R2", [128, 16], F32)
    T["EPSC"] = mk(top, "EPSC", [128, 4], F32)
    T["DUMT"] = mk(top, "DUMT", [128, 1], F32)
    T["LAMCT"] = mk(top, "LAMCT", [128, NH], F32)

    qs = ExitStack()
    T["COS"] = mk(qs, "COS", [128, TCC * 64], BF16)
    T["SIN"] = mk(qs, "SIN", [128, TCC * 64], BF16)
    T["DMASK"] = mk(qs, "DMASK", [CH, D], BF16)
    T["DKT"] = mk(qs, "DKT", [CH, D], BF16)
    T["S"] = mk(qs, "S", [128, NH * HD], F32)
    T["SB"] = mk(qs, "SB", [128, NH * HD], BF16)
    T["XCH"] = [mk(qs, f"XCH{i}", [128, D], F32) for i in range(2)]
    T["XLNT"] = [mk(qs, f"XLNT{i}", [128, D], BF16) for i in range(2)]
    T["TRASH"] = mk(qs, "TRASH", [128, D], BF16)
    T["WQ"] = mk(qs, "WQ", [128, NDC * 512], BF16)
    T["WK"] = mk(qs, "WK", [128, NDC * 512], BF16)
    T["WV"] = mk(qs, "WV", [128, NDC * 512], BF16)
    T["QSB"] = [mk(qs, f"QSB{i}", [128, 512], BF16) for i in range(2)]
    T["QTM"] = [mk(qs, f"QTM{i}", [128, 512], BF16) for i in range(2)]
    T["TMP1"] = mk(qs, "TMP1", [128, QH * 64], F32)
    T["TMP2"] = mk(qs, "TMP2", [128, QH * 64], F32)
    T["QFM"] = mk(qs, "QFM", [128, QH * 1024], BF16)
    T["KFM"] = mk(qs, "KFM", [128, QH * 1024], BF16)
    T["K2"] = mk(qs, "K2", [128, TPT * 512], BF16)
    T["V"] = mk(qs, "V", [128, TPT * 512], BF16)
    T["SCT"] = [mk(qs, f"SCT{i}", [128, 512], BF16) for i in range(2)]
    T["TMPOE"] = mk(qs, "TMPOE", [128, 512], F32)
    T["OQ"] = mk(qs, "OQ", [128, TPT * 512], BF16)
    qps = ExitStack()
    T["PP"] = [mkp(qps, f"PP{i}", [128, 512], F32) for i in range(2)]
    T["PT"] = mkp(qps, "PT", [128, D], BF16)
    T["PSC"] = mkp(qps, "PSC", [128, 512], F32)
    T["POI"] = mkp(qps, "POI", [128, 512], F32)
    T["POE"] = mkp(qps, "POE", [128, 512], F32)
    T["PKV"] = mkp(qps, "PKV", [128, 512], F32)
    qps.close()
    qs.close()

    for tile in range(2):
        ts = ExitStack()
        T[f"S16ST{tile}"] = mk(ts, f"S16ST{tile}", [128, NH * HD], F32)
        T[f"S16B{tile}"] = mk(ts, f"S16B{tile}", [128, NH * HD], BF16)
        T[f"OCH{tile}"] = mk(ts, f"OCH{tile}", [128, TPT * D], BF16)
        T[f"QCH{tile}"] = [mk(ts, f"QCH{tile}_{i}", [128, NH * CH], BF16)
                           for i in range(2)]
        T[f"TMPC{tile}"] = [mk(ts, f"TMPC{tile}_{i}", [128, 512], F32)
                            for i in range(2)]
        T[f"GATS{tile}"] = mk(ts, f"GATS{tile}", [128, TPT * D], BF16)
        T[f"XG{tile}"] = mk(ts, f"XG{tile}", [128, D], F32)
        T[f"SQB{tile}"] = mk(ts, f"SQB{tile}", [128, D], F32)
        T[f"OGT{tile}"] = mk(ts, f"OGT{tile}", [128, D], BF16)
        aps = ExitStack()
        T[f"PCORR{tile}"] = [mkp(aps, f"PCORR{tile}_{i}", [128, 512], F32)
                             for i in range(2)]
        T[f"PGATE{tile}"] = mkp(aps, f"PGATE{tile}", [128, D], F32)
        T[f"PT2{tile}"] = mkp(aps, f"PT2{tile}", [128, D], BF16)
        aps.close()
        ts.close()
        bs = ExitStack()
        T[f"XLN2FM{tile}"] = mk(bs, f"XLN2FM{tile}", [128, NDC * 1024], BF16)
        T[f"WOH{tile}"] = mk(bs, f"WOH{tile}", [128, NDC * 1024], BF16)
        T[f"XRCH{tile}"] = [mk(bs, f"XRCH{tile}_{i}", [128, 1024], F32)
                            for i in range(2)]
        T[f"X2H{tile}"] = [mk(bs, f"X2H{tile}_{i}", [128, 1024], F32)
                           for i in range(2)]
        T[f"TRASH2{tile}"] = mk(bs, f"TRASH2{tile}", [128, 1024], F32)
        bps = ExitStack()
        T[f"PB{tile}"] = [mkp(bps, f"PB{tile}_{i}", [128, 1024], F32)
                          for i in range(2)]
        bps.close()
        cs = ExitStack()
        T[f"XB2{tile}"] = [mk(cs, f"XB2{tile}_{i}", [128, D], F32)
                           for i in range(2)]
        T[f"XLNT2{tile}"] = [mk(cs, f"XLNT2{tile}_{i}", [128, D], BF16)
                             for i in range(2)]
        T[f"FSP{tile}"] = [mk(cs, f"FSP{tile}_{i}", [128, NDC * 512], BF16)
                           for i in range(2)]
        T[f"OUTES{tile}"] = [mk(cs, f"OUTES{tile}_{i}", [128, 4 * 512], F32)
                             for i in range(2)]
        cps = ExitStack()
        T[f"PT3{tile}"] = mkp(cps, f"PT3{tile}", [128, D], BF16)
        T[f"PF{tile}"] = [mkp(cps, f"PF{tile}_{i}", [128, 512], F32)
                          for i in range(6)]
        cps.close()
        cs.close()
        bs.close()

    NSX, NGP = 8, 8
    SXP = [Sem(top.enter_context(nc.semaphore(f"dsx{i}"))) for i in range(NSX)]
    GPP = [Sem(top.enter_context(nc.semaphore(f"dgp{i}"))) for i in range(NGP)]
    CCP = [Sem(top.enter_context(nc.semaphore(f"dcc{i}"))) for i in range(5)]
    CPE = Sem(top.enter_context(nc.semaphore("cpe")))

    class ChainSem:
        W = 4

        def __init__(self, name):
            self.sems = [Sem(top.enter_context(nc.semaphore(f"{name}{i}")))
                         for i in range(self.W)]
            self.n = 0

        def mark(self):
            if self.n == 0:
                return (self.sems[0], 0)
            i = self.n - 1
            return (self.sems[i % self.W], i // self.W + 1)

    CACT = ChainSem("cact")
    CDVE = ChainSem("cdve")
    ALLSEMS = tuple(SXP) + tuple(GPP) + tuple(CCP) + (CPE,)         + tuple(CACT.sems) + tuple(CDVE.sems)

    def walk(eng_name, E):
        for s in ALLSEMS:
            s.n = 0
        CACT.n = 0
        CDVE.n = 0

        SELF = {"vector": CDVE, "scalar": CACT}
        last_inst = {e: None for e in ENGINES}
        last_mark = {e: None for e in ENGINES}

        def op(engine, method, *a, inc=None, incn=1, waits=(), **k):
            if PE_DENSE_INC and engine == "tensor":
                assert inc is None or inc is CPE
                CPE.n += 1
                if engine == eng_name:
                    for s_, v_ in waits:
                        if s_ is not None:
                            E.wait_ge(s_.h, v_)
                    instr = getattr(E, method)(*a, **k)
                    instr.then_inc(CPE.h, 1)
                    last_inst[engine] = instr
                    last_mark[engine] = (CPE, CPE.n)
                return
            selfsem = SELF.get(engine)
            if selfsem is not None:
                assert inc is None or inc is selfsem
                prevmark = selfsem.mark()
                selfsem.n += 1
                mysem, myval = selfsem.mark()
                if engine == eng_name:
                    for s_, v_ in waits:
                        if s_ is not None:
                            E.wait_ge(s_.h, v_)
                    E.wait_ge(prevmark[0].h, prevmark[1])
                    instr = getattr(E, method)(*a, **k)
                    instr.then_inc(mysem.h, 1)
                    last_inst[engine] = instr
                return
            if inc is not None:
                inc.n += incn
            if engine == eng_name:
                for s_, v_ in waits:
                    if s_ is not None:
                        E.wait_ge(s_.h, v_)
                instr = getattr(E, method)(*a, **k)
                if instr is not None:
                    last_inst[engine] = instr
                    last_mark[engine] = None
                if inc is not None:
                    instr.then_inc(inc.h, incn)
                    last_mark[engine] = (inc, inc.n)

        def inc_last(engine, sem, n=1):
            if SELF.get(engine) is not None:
                assert sem is SELF[engine]
                return
            if PE_DENSE_INC and engine == "tensor":
                assert sem is CPE
                return
            prev = last_mark[engine]
            sem.n += n
            if engine == eng_name:
                if last_inst[engine] is None or prev is not None:
                    if prev is not None:
                        E.wait_ge(prev[0].h, prev[1])
                    E.nop().then_inc(sem.h, n)
                else:
                    last_inst[engine].then_inc(sem.h, n)
            last_mark[engine] = (sem, sem.n)

        qstate = {"sync": [SXP, 0], "gpsimd": [GPP, 0], "scalar": [SXP, 0]}

        def dma(engine, out_ap, in_ap, dsem=None, waits=(), **k):
            pool, idx = qstate[engine]
            s = pool[idx % len(pool)]
            qstate[engine][1] = idx + 1
            op(engine, "dma_start", inc=s, incn=16,
               waits=tuple(waits) + ((s, s.n),),
               out=out_ap, in_=in_ap, **k)
            return (s, s.n)

        def w(sem_):
            if isinstance(sem_, ChainSem):
                return sem_.mark()
            return (sem_, sem_.n)

        def fence():
            marks = []
            for en, cs in (("tensor", CPE), ("scalar", CACT), ("vector", CDVE)):
                if en == "tensor":
                    inc_last(en, cs)
                marks.append((en, w(cs)))
            dm = [(s, s.n) for s in SXP + GPP]
            for en in ENGINES:
                for other, mk_ in marks:
                    if other != en:
                        op(en, "nop", waits=(mk_,))
                for mk_ in dm:
                    op(en, "nop", waits=(mk_,))

        WAITS_NONE = ()

        dma("gpsimd", T["IDN"][:, :], idn_in.ap())
        dma("gpsimd", T["COS"][:].rearrange("p (tc f) -> p tc f", tc=TCC),
            cos_in.ap().rearrange("(tc p) f -> p tc f", p=128))
        dma("gpsimd", T["SIN"][:].rearrange("p (tc f) -> p tc f", tc=TCC),
            sin_in.ap().rearrange("(tc p) f -> p tc f", p=128))
        dma("gpsimd", T["DMASK"][:, :], dmask_in.ap())
        dma("gpsimd", T["DKT"][:, :], dkt_in.ap())
        dma("gpsimd", T["DQV"][:, :], dqv_in.ap())
        dma("gpsimd", T["CORR"][:, :], corr_in.ap())
        dma("gpsimd", T["NW"][:, :],
            nw_in.ap().rearrange("(dc p) -> p dc", p=128),
            allow_slow_non_contiguous=True)
        dma("gpsimd", T["BIASB"][:, :], bias_in.ap())
        dma("gpsimd", T["WG1B"][:].rearrange("p (dc r) -> p dc r", dc=NDC),
            wg1.ap().rearrange("(dc p) r -> p dc r", p=128))
        dma("gpsimd", T["LAMCT"][:, :], lamc_in.ap())
        consts_dma = dma("gpsimd", T["WG2B"][:, :], wg2.ap())
        op("vector", "memset", T["EPSC"][:, 0:1], EPS_LN)
        op("vector", "memset", T["EPSC"][:, 1:2], EPS_RMS)
        op("vector", "memset", T["EPSC"][:, 2:3], float(1.0 / HD))
        op("vector", "memset", T["ONES"][:, :], 1.0)
        op("vector", "memset", T["S"][:, :], 0.0)
        op("vector", "memset", T["SB"][:, :], 0.0, inc=CDVE)

        FACCv = T["WQ"][:].bitcast(F32)
        FBUFv = [T["WK"][:].bitcast(F32), T["WV"][:].bitcast(F32)]
        FBCv = T["OQ"][:]
        add_done = [None] * 8
        for r in range(8):
            wt = (add_done[r - 2],) if r >= 2 else WAITS_NONE
            fdma = dma(
                "gpsimd",
                FBUFv[r % 2].rearrange("p (dc e) -> p dc e", dc=NDC),
                fac_sl[r].rearrange("(dc p) e -> p dc e", p=128),
                waits=wt)
            if r == 0:
                op("vector", "tensor_copy", FACCv, FBUFv[0], waits=(fdma,),
                   inc=CDVE)
            else:
                op("vector", "tensor_tensor", FACCv, FACCv, FBUFv[r % 2],
                   op=ALU.add, waits=(fdma,), inc=CDVE)
            add_done[r] = w(CDVE)
        op("vector", "tensor_scalar", out=FBCv, in0=FACCv,
           scalar1=float(1.0 / np.sqrt(8.0)), scalar2=None, op0=ALU.mult,
           inc=CDVE)
        fbc = w(CDVE)
        fsin_dma = dma("gpsimd",
                       cc_fs_in.ap().rearrange("(dc p) e -> p dc e", p=128),
                       FBCv.rearrange("p (dc e) -> p dc e", dc=NDC),
                       waits=(fbc,))
        op("gpsimd", "nop", waits=(fsin_dma,))
        if not SKIP_CC:
            op("gpsimd", "collective_compute", "AllGather", ALU.bypass,
               replica_groups=[list(range(8))],
               ins=[cc_fs_in.ap().opt()], outs=[cc_fs_out.ap().opt()],
               inc=CCP[0])
        else:
            op("gpsimd", "nop", inc=CCP[0])
        fsum_cc = (CCP[0], CCP[0].n)
        fence()

        ln_evac = {}
        ln_tp = {}
        ln_chunk_evac = {}
        g1_done = {}
        marks = {}
        s16_sent = [None] * NQ
        spill_o = {}
        spill_q = {}

        def ln_phase(tile):
            xch_free = {}
            for c in range(TPT):
                tc = tile * TPT + c
                slot = c % 2
                XCH = T["XCH"][slot]
                rw = list(xch_free.get(c - 2, ()))
                if tile == 1 and c < 2:
                    rw.append(marks.get(("lnfree", 0, TPT - 2 + c)))
                xdma = dma("sync", XCH[:, :], x_sl[tc * CH:(tc + 1) * CH, :],
                           waits=tuple(x for x in rw if x))
                if TRUNCATE == "ln0dma" and tile == 0 and c == 0:
                    return
                op("vector", "tensor_reduce", T["STATS"][:, c:c + 1], XCH[:, :],
                   AX.X, ALU.add, waits=(xdma,), inc=CDVE)
                sum_done = w(CDVE)
                if TRUNCATE == "ln0red" and tile == 0 and c == 0:
                    return
                op("scalar", "activation", out=T["TRASH"][:, :], in_=XCH[:, :],
                   func=AF.Square, waits=(xdma,), inc=CACT)
                sq_sq = w(CACT)
                op("vector", "tensor_reduce", T["STATS"][:, 8 + c:9 + c],
                   T["TRASH"][:, :], AX.X, ALU.add, waits=(sq_sq,), inc=CDVE)
                sq_done = w(CDVE)
                if TRUNCATE == "ln0sq" and tile == 0 and c == 0:
                    return
                mcol = T["STATS"][:, 16 + c:17 + c]
                vcol = T["STATS"][:, 24 + c:25 + c]
                rcol = T["STATS"][:, 32 + c:33 + c]
                bcol = T["STATS"][:, 40 + c:41 + c]
                tcol = T["STATS"][:, 48 + c:49 + c]
                scol = T["STATS"][:, 56 + c:57 + c]
                op("vector", "tensor_scalar", out=mcol, in0=T["STATS"][:, c:c + 1],
                   scalar1=RECIP_D, scalar2=None, op0=ALU.mult, waits=(sum_done,))
                op("vector", "tensor_scalar", out=vcol,
                   in0=T["STATS"][:, 8 + c:9 + c], scalar1=RECIP_D, scalar2=None,
                   op0=ALU.mult, waits=(sq_done,))
                op("vector", "tensor_tensor", tcol, mcol, mcol, op=ALU.mult)
                op("vector", "tensor_tensor", vcol, vcol, tcol,
                   op=ALU.subtract, inc=CDVE)
                op("scalar", "activation", out=scol, in_=vcol, func=AF.Sqrt,
                   bias=T["EPSC"][:, 0:1], waits=(w(CDVE),), inc=CACT)
                op("vector", "reciprocal", rcol, scol, waits=(w(CACT),))
                op("vector", "tensor_scalar", out=bcol, in0=mcol, scalar1=rcol,
                   scalar2=-1.0, op0=ALU.mult, op1=ALU.mult, inc=CDVE)
                rb = w(CDVE)
                XLNT = T["XLNT"][slot]
                nwt = [rb]
                pv = ln_tp.get((tile, c - 2)) or (
                    ln_tp.get((0, TPT - 2 + c)) if tile == 1 else None)
                if pv:
                    nwt.append(pv)
                op("scalar", "activation", out=XLNT[:, :], in_=XCH[:, :],
                   func=AF.Identity, scale=rcol, bias=bcol, waits=tuple(nwt),
                   inc=CACT)
                nrm = w(CACT)
                if TRUNCATE == "ln0nrm" and tile == 0 and c == 0:
                    return
                marks[("lnfree", tile, c)] = nrm
                xch_free[c] = (nrm, sq_done, sum_done)
                tw = [nrm]
                if (tile, c - 1) in ln_chunk_evac:
                    tw.append(ln_chunk_evac[(tile, c - 1)])
                elif tile == 1 and c == 0:
                    tw.append(ln_chunk_evac[(0, TPT - 1)])
                    tw.append(marks[("fme", "k", 0, NQ - 1)])
                for dc in range(NDC):
                    op("tensor", "transpose",
                       T["PT"][:, dc * 128:(dc + 1) * 128],
                       XLNT[:, dc * 128:(dc + 1) * 128], T["IDN"][:, :],
                       waits=tuple(tw) if dc == 0 else WAITS_NONE)
                inc_last("tensor", CPE)
                ln_tp[(tile, c)] = w(CPE)
                if TRUNCATE == "ln0tp" and tile == 0 and c == 0:
                    return
                if EVAC_MODE == 1:
                    dst = T["XLNFM"][:, c * D:(c + 1) * D]
                    src = T["PT"][:, :]
                elif EVAC_MODE == 2:
                    dst = T["XLNFM"][:, c * D:(c + 1) * D].rearrange(
                        "p (dc t) -> p dc t", dc=NDC)
                    src = T["PT"][:].rearrange("p (dc t) -> p dc t", dc=NDC)
                elif EVAC_MODE == 3:
                    dst = T["XLNFM"][:].rearrange("p (dc t) -> p dc t", dc=NDC)[
                        :, :, c * CH:(c + 1) * CH]
                    src = T["PT"][:, :].rearrange("p (t dc) -> p dc t", dc=NDC)
                else:
                    dst = T["XLNFM"][:].rearrange("p (dc t) -> p dc t", dc=NDC)[
                        :, :, c * CH:(c + 1) * CH]
                    src = T["PT"][:].rearrange("p (dc t) -> p dc t", dc=NDC)
                ecw = [ln_tp[(tile, c)]]
                if tile == 1 and c == 0:
                    ecw.append(marks[("allproj", 0)])
                op("vector", "tensor_copy", dst, src, waits=tuple(ecw),
                   inc=CDVE)
                ln_chunk_evac[(tile, c)] = w(CDVE)
                if TRUNCATE == f"ln{tile}c{c}":
                    return
            ln_evac[tile] = w(CDVE)
            if "d_stats3" in debug and tile == 0:
                dma("sync", dbg["d_stats3"].ap(), T["STATS"][:, :],
                    waits=(w(CACT), w(CDVE)))
            if "d_xlnfm" in debug and tile == 0:
                dma("sync", dbg["d_xlnfm"].ap(), T["XLNFM"][:, :],
                    waits=(ln_evac[tile],))
            g1w = [ln_evac[tile], consts_dma]
            if tile == 1:
                g1w.append(marks[("projevac", 0, NQ - 1)])
            for p in range(2):
                for dc in range(NDC):
                    op("tensor", "matmul", T["PP"][p][:, :],
                       T["WG1B"][:, dc * HD:(dc + 1) * HD],
                       T["XLNFM"][:, dc * 1024 + p * 512:dc * 1024 + (p + 1) * 512],
                       start=(dc == 0), stop=(dc == NDC - 1),
                       waits=tuple(g1w) if (dc == 0 and p == 0)
                       else WAITS_NONE)
                inc_last("tensor", CPE)
                op("vector", "tensor_copy",
                   T["G1"][:, tile * 1024 + p * 512:tile * 1024 + (p + 1) * 512],
                   T["PP"][p][:, :], waits=(w(CPE),), inc=CDVE)
            g1_done[tile] = w(CDVE)
            if "d_g1" in debug and tile == 1:
                dma("sync", dbg["d_g1"].ap(), T["G1"][:, :],
                    waits=(g1_done[tile],))

        sstate = {"lin": 0, "mask_m": None, "intra_m": [None, None],
                  "oevac": None, "dq": None, "sadd": None, "sb": None,
                  "fme_q": {}, "fme_k": {}, "vev": {}}
        pending_scan = [None]
        pstate = {}

        def scan_step(tile, g, c):
            st = sstate
            lin = st["lin"]
            st["lin"] += 1
            slot = lin % 2
            qfm3 = T["QFM"][:].rearrange("p (h t) -> p h t", h=QH)
            kfm3 = T["KFM"][:].rearrange("p (h t) -> p h t", h=QH)
            sw = [st["fme_q"].pop((g, c)), st["fme_k"].pop((g, c))]
            if st["mask_m"]:
                sw.append(st["mask_m"])
            for hl in range(QH):
                op("tensor", "matmul", T["PSC"][:, hl * 128:(hl + 1) * 128],
                   kfm3[:, hl, c * CH:(c + 1) * CH],
                   qfm3[:, hl, c * CH:(c + 1) * CH],
                   start=True, stop=True,
                   waits=tuple(sw) if hl == 0 else WAITS_NONE)
            inc_last("tensor", CPE)
            scd = w(CPE)
            mw = [scd]
            if st["intra_m"][slot]:
                mw.append(st["intra_m"][slot])
            op("vector", "tensor_tensor", T["SCT"][slot][:, :],
               T["PSC"][:, :], T["DMASK"][:, g * 512:(g + 1) * 512],
               op=ALU.mult, waits=tuple(mw), inc=CDVE)
            st["mask_m"] = w(CDVE)
            ow = [st["mask_m"], st["vev"].pop((g, c))]
            if st["oevac"]:
                ow.append(st["oevac"])
            for hl in range(QH):
                op("tensor", "matmul", T["POI"][:, hl * 128:(hl + 1) * 128],
                   T["SCT"][slot][:, hl * 128:(hl + 1) * 128],
                   T["V"][:, c * 512 + hl * 128:c * 512 + (hl + 1) * 128],
                   start=True, stop=True,
                   waits=tuple(ow) if hl == 0 else WAITS_NONE)
            iw = [x for x in (st["dq"], st["sb"]) if x]
            for hl in range(QH):
                h = g * QH + hl
                op("tensor", "matmul", T["POE"][:, hl * 128:(hl + 1) * 128],
                   qfm3[:, hl, c * CH:(c + 1) * CH],
                   T["SB"][:, h * HD:(h + 1) * HD],
                   start=True, stop=True,
                   waits=tuple(iw) if hl == 0 else WAITS_NONE)
            kw = [st["sadd"]] if st["sadd"] else []
            for hl in range(QH):
                op("tensor", "matmul", T["PKV"][:, hl * 128:(hl + 1) * 128],
                   T["K2"][:, c * 512 + hl * 128:c * 512 + (hl + 1) * 128],
                   T["V"][:, c * 512 + hl * 128:c * 512 + (hl + 1) * 128],
                   start=True, stop=True,
                   waits=tuple(kw) if hl == 0 else WAITS_NONE)
            inc_last("tensor", CPE)
            mm2 = w(CPE)
            st["intra_m"][slot] = mm2
            marks[("scanstep", tile, g, c)] = mm2
            for hl in range(QH):
                h = g * QH + hl
                op("scalar", "activation",
                   out=T["TMPOE"][:, hl * 128:(hl + 1) * 128],
                   in_=T["POE"][:, hl * 128:(hl + 1) * 128],
                   func=AF.Identity, scale=T["DQV"][:, h:h + 1],
                   waits=(mm2,) if hl == 0 else WAITS_NONE)
            st["dq"] = w(CACT)
            dw = [st["dq"]]
            if st["sadd"]:
                dw.append(st["sadd"])
            if st["sb"]:
                dw.append(st["sb"])
            if tile == 0 and g == 0 and c == 0:
                dw.append(consts_dma)
            for hl in range(QH):
                h = g * QH + hl
                op("scalar", "activation",
                   out=T["S"][:, h * HD:(h + 1) * HD],
                   in_=T["S"][:, h * HD:(h + 1) * HD], func=AF.Identity,
                   scale=T["LAMCT"][:, h:h + 1],
                   waits=tuple(dw) if hl == 0 else WAITS_NONE)
            decay_done = w(CACT)
            oqw = [st["dq"]]
            if c == 0:
                pq = (tile, g - 1) if g > 0 else (
                    (tile - 1, NQ - 1) if tile > 0 else None)
                pv = spill_o.get(pq) if pq else None
                if pv:
                    oqw.append(pv)
            op("vector", "tensor_tensor", T["OQ"][:, c * 512:(c + 1) * 512],
               T["POI"][:, :], T["TMPOE"][:, :], op=ALU.add,
               waits=tuple(oqw), inc=CDVE)
            st["oevac"] = w(CDVE)
            op("vector", "tensor_tensor", T["S"][:, g * 512:(g + 1) * 512],
               T["S"][:, g * 512:(g + 1) * 512], T["PKV"][:, :],
               op=ALU.add, waits=(decay_done, mm2), inc=CDVE)
            st["sadd"] = w(CDVE)
            op("vector", "tensor_copy", T["SB"][:, g * 512:(g + 1) * 512],
               T["S"][:, g * 512:(g + 1) * 512], inc=CDVE)
            st["sb"] = w(CDVE)

            if c == TPT - 1:
                marks[("scan", tile, g)] = mm2
                spill_o[(tile, g)] = dma(
                    "sync",
                    o_spill[tile * TPT:(tile + 1) * TPT, :,
                            g * 512:(g + 1) * 512].rearrange(
                        "tc t e -> t tc e"),
                    T["OQ"][:].rearrange("p (tc e) -> p tc e", tc=TPT),
                    waits=(st["oevac"],))
                spill_q[(tile, g)] = []
                for c_ in range(TPT):
                    spill_q[(tile, g)].append(dma(
                        "sync",
                        q_spill[tile * TPT + c_,
                                g * QH:(g + 1) * QH].rearrange(
                            "h hd t -> hd h t"),
                        T["QFM"][:].rearrange("p (h t) -> p h t", h=QH)[
                            :, :, c_ * CH:(c_ + 1) * CH],
                        waits=(mm2,) if c_ == 0 else WAITS_NONE))
                if tile == 1:
                    sdma = dma("gpsimd", cc_s_in[g].ap(),
                               T["S"][:, g * 512:(g + 1) * 512],
                               waits=(st["sadd"],))
                    op("gpsimd", "nop", waits=(sdma,))
                    if not SKIP_CC:
                        op("gpsimd", "collective_compute", "AllGather",
                           ALU.bypass,
                           replica_groups=[[0, 1], [2, 3], [4, 5], [6, 7]],
                           ins=[cc_s_in[g].ap().opt()],
                           outs=[cc_s_out[g].ap().opt()], inc=CCP[1 + g])
                    else:
                        op("gpsimd", "nop", inc=CCP[1 + g])
                    s16_sent[g] = (CCP[1 + g], CCP[1 + g].n)

        def quad_phase(tile, g):
            prev_quad = (tile, g - 1) if g > 0 else (
                (tile - 1, NQ - 1) if tile > 0 else None)

            wdmas = {}
            for wname, wt, pj in (("WQ", wq, "q"), ("WK", wk, "k"),
                                  ("WV", wv, "v")):
                pv = marks.get(("proj", pj) + prev_quad) if prev_quad else None
                wdmas[pj] = dma(
                    "gpsimd",
                    T[wname][:].rearrange("p (dc e) -> p dc e", dc=NDC),
                    wt[:, g * 512:(g + 1) * 512].rearrange(
                        "(dc p) e -> p dc e", p=128),
                    waits=(pv,) if pv else WAITS_NONE)

            ppfree = {}
            rotp = {}
            tpp = {}
            silup = {}
            mmk = {}

            def proj_chunk(which, wsb, c, wi):
                gi = c * 3 + wi
                slot = gi % 2
                pw = [ln_evac[tile], wdmas[which]]
                if ppfree.get(gi - 2):
                    pw.append(ppfree[gi - 2])
                elif gi < 2:
                    pw.append(g1_done[tile])
                    if prev_quad:
                        pv = marks.get(("projevac",) + prev_quad)
                        if pv:
                            pw.append(pv)
                for dc in range(NDC):
                    op("tensor", "matmul", T["PP"][slot][:, :],
                       T["XLNFM"][:, dc * 1024 + c * CH:dc * 1024 + (c + 1) * CH],
                       wsb[:, dc * 512:(dc + 1) * 512],
                       start=(dc == 0), stop=(dc == NDC - 1),
                       waits=tuple(pw) if dc == 0 else WAITS_NONE)
                inc_last("tensor", CPE)
                mmd = w(CPE)
                mmk[(which, c)] = mmd
                prevscan = marks.get(("scanstep",) + prev_quad + (c,)) \
                    if prev_quad else None

                if which == "v":
                    ew = [mmd]
                    if prevscan:
                        ew.append(prevscan)
                    op("scalar", "activation",
                       out=T["V"][:, c * 512:(c + 1) * 512],
                       in_=T["PP"][slot][:, :], func=AF.Identity,
                       waits=tuple(ew), inc=CACT)
                    ppfree[gi] = w(CACT)
                    sstate["vev"][(g, c)] = w(CACT)
                    return

                QSB = T["QSB"][0 if which == "q" else 1]
                QTM = T["QTM"][0 if which == "q" else 1]
                sw = [mmd]
                if rotp.get((which, c - 1)):
                    sw.append(rotp[(which, c - 1)])
                elif pstate.get(("rot", which)):
                    sw.append(pstate[("rot", which)])
                op("scalar", "activation", out=QSB[:, :],
                   in_=T["PP"][slot][:, :], func=AF.Silu, waits=tuple(sw),
                   inc=CACT)
                silup[(which, c)] = w(CACT)
                ppfree[gi] = w(CACT)
                tc = tile * TPT + c
                q3 = QSB[:].rearrange("p (h e) -> p h e", h=QH)
                o3 = QTM[:].rearrange("p (h e) -> p h e", h=QH)
                cosap = T["COS"][:, tc * 64:(tc + 1) * 64][:, None, :] \
                    .broadcast_to([128, QH, 64])
                sinap = T["SIN"][:, tc * 64:(tc + 1) * 64][:, None, :] \
                    .broadcast_to([128, QH, 64])
                t1 = T["TMP1"][:].rearrange("p (h e) -> p h e", h=QH)
                t2 = T["TMP2"][:].rearrange("p (h e) -> p h e", h=QH)
                rw = [silup[(which, c)]]
                if tpp.get((which, c - 1)):
                    rw.append(tpp[(which, c - 1)])
                elif pstate.get(("tp", which)):
                    rw.append(pstate[("tp", which)])
                op("vector", "tensor_tensor", t1, q3[:, :, 0:64], cosap,
                   op=ALU.mult, waits=tuple(rw))
                op("vector", "tensor_tensor", t2, q3[:, :, 64:128], sinap,
                   op=ALU.mult)
                op("vector", "tensor_tensor", o3[:, :, 0:64], t1, t2,
                   op=ALU.subtract)
                op("vector", "tensor_tensor", t1, q3[:, :, 0:64], sinap,
                   op=ALU.mult)
                op("vector", "tensor_tensor", t2, q3[:, :, 64:128], cosap,
                   op=ALU.mult)
                op("vector", "tensor_tensor", o3[:, :, 64:128], t1, t2,
                   op=ALU.add, inc=CDVE)
                rotp[(which, c)] = w(CDVE)
                if which == "k":
                    kw = [w(CDVE)]
                    if prevscan:
                        kw.append(prevscan)
                    op("vector", "tensor_tensor",
                       T["K2"][:, c * 512:(c + 1) * 512], QTM[:, :],
                       T["DKT"][:, g * 512:(g + 1) * 512], op=ALU.mult,
                       waits=tuple(kw), inc=CDVE)
                    rotp[(which, c)] = w(CDVE)

            def proj_transpose(which, c):
                QTM = T["QTM"][0 if which == "q" else 1]
                dst_fm = T["QFM"] if which == "q" else T["KFM"]
                off = 0 if which == "q" else 1024
                prevscan = marks.get(("scanstep",) + prev_quad + (c,)) \
                    if prev_quad else None
                tw = [rotp[(which, c)]]
                if tpp.get((which, c - 1)):
                    tw.append(marks.get(("fme", which, tile, g, c - 1)))
                elif marks.get(("lastPT", tile)):
                    tw.append(marks[("lastPT", tile)])
                for hl in range(QH):
                    op("tensor", "transpose",
                       T["PT"][:, off + hl * 128:off + (hl + 1) * 128],
                       QTM[:, hl * 128:(hl + 1) * 128], T["IDN"][:, :],
                       waits=tuple(tw) if hl == 0 else WAITS_NONE)
                inc_last("tensor", CPE)
                tpp[(which, c)] = w(CPE)
                ew = [tpp[(which, c)]]
                if prevscan:
                    ew.append(prevscan)
                pq = prev_quad
                if pq and spill_q.get(pq):
                    ew.append(spill_q[pq][c])
                dstap = dst_fm[:].rearrange("p (h t) -> p h t", h=QH)[
                    :, :, c * CH:(c + 1) * CH]
                srcap = T["PT"][:, off:off + 512].rearrange(
                    "p (h t) -> p h t", h=QH)
                op("vector", "tensor_copy", dstap, srcap, waits=tuple(ew),
                   inc=CDVE)
                fm = w(CDVE)
                marks[("fme", which, tile, g, c)] = fm
                if which == "q":
                    sstate["fme_q"][(g, c)] = fm
                else:
                    sstate["fme_k"][(g, c)] = fm

            for c in range(TPT):
                if c == 0 and pending_scan[0] is not None:
                    scan_step(*pending_scan[0])
                    pending_scan[0] = None
                proj_chunk("q", T["WQ"], c, 0)
                proj_chunk("k", T["WK"], c, 1)
                proj_chunk("v", T["WV"], c, 2)
                if c > 0:
                    scan_step(tile, g, c - 1)
                proj_transpose("q", c)
                proj_transpose("k", c)
            marks[("proj", "q", tile, g)] = mmk[("q", TPT - 1)]
            marks[("proj", "k", tile, g)] = mmk[("k", TPT - 1)]
            marks[("proj", "v", tile, g)] = mmk[("v", TPT - 1)]
            marks[("projevac", tile, g)] = w(CACT)
            marks[("fme", "k", tile, g)] = marks[("fme", "k", tile, g, TPT - 1)]
            marks[("lastPT", tile)] = marks[("fme", "k", tile, g, TPT - 1)]
            pstate[("rot", "q")] = rotp[("q", TPT - 1)]
            pstate[("rot", "k")] = rotp[("k", TPT - 1)]
            pstate[("tp", "q")] = tpp[("q", TPT - 1)]
            pstate[("tp", "k")] = tpp[("k", TPT - 1)]
            if g == NQ - 1:
                scan_step(tile, g, TPT - 1)
                pending_scan[0] = None
            else:
                pending_scan[0] = (tile, g, TPT - 1)

        def tail_phase(tile):
            S16ST = T[f"S16ST{tile}"]
            OCH = T[f"OCH{tile}"]
            QCH = T[f"QCH{tile}"]
            TMPC = T[f"TMPC{tile}"]
            GATS = T[f"GATS{tile}"]
            XG = T[f"XG{tile}"]
            SQB = T[f"SQB{tile}"]
            OGT = T[f"OGT{tile}"]
            PCORR = T[f"PCORR{tile}"]
            PGATE = T[f"PGATE{tile}"]
            PT2 = T[f"PT2{tile}"]

            s16marks = []
            for g in range(NQ):
                s16marks.append(dma(
                    "sync", S16ST[:, g * 512:(g + 1) * 512],
                    cc_s_out[g][0:HD, :], waits=(s16_sent[g],)))
            S16B = T[f"S16B{tile}"]
            op("vector", "tensor_copy", S16B[:, :], S16ST[:, :],
               waits=tuple(s16marks), inc=CDVE)
            s16b = w(CDVE)
            if "d_s16" in debug and tile == 0:
                dma("sync", dbg["d_s16"].ap(), S16ST[:, :], waits=(s16b,))

            sig_done = [None] * TPT
            corr_add = [None] * TPT
            qch_used = [None] * TPT
            prev_vadd = None
            for c in range(TPT):
                tc = tile * TPT + c
                slot = c % 2
                odma = dma("sync", OCH[:, c * D:(c + 1) * D], o_spill[tc],
                           waits=tuple(spill_o[(tile, gg)]
                                       for gg in range(NQ)) if c == 0
                           else WAITS_NONE)
                qw = []
                if c == 0:
                    for gg in range(NQ):
                        qw.extend(spill_q[(tile, gg)])
                if c >= 2 and qch_used[c - 2]:
                    qw.append(qch_used[c - 2])
                qdma = dma("sync",
                           QCH[slot][:].rearrange("p (h t) -> p h t", h=NH),
                           q_spill[tc].rearrange("h hd t -> hd h t"),
                           waits=tuple(qw))
                for g in range(NQ):
                    ps_ = PCORR[g % 2]
                    cw = [qdma, s16b]
                    if g >= 2:
                        cw.append(marks[("corrva", tile, c, g - 2)])
                    elif c >= 1:
                        cw.append(marks[("corrva", tile, c - 1, g + 2)])
                    for hl in range(QH):
                        h = g * QH + hl
                        op("tensor", "matmul", ps_[:, hl * 128:(hl + 1) * 128],
                           QCH[slot][:, h * 128:(h + 1) * 128],
                           S16B[:, h * HD:(h + 1) * HD],
                           start=True, stop=True,
                           waits=tuple(x for x in cw if x) if hl == 0
                           else WAITS_NONE)
                    inc_last("tensor", CPE)
                    cmm = w(CPE)
                    tm = TMPC[g % 2]
                    scw = [cmm]
                    if g >= 2:
                        scw.append(marks[("corrva", tile, c, g - 2)])
                    elif c >= 1:
                        scw.append(marks[("corrva", tile, c - 1, g + 2)])
                    for hl in range(QH):
                        h = g * QH + hl
                        op("scalar", "activation",
                           out=tm[:, hl * 128:(hl + 1) * 128],
                           in_=ps_[:, hl * 128:(hl + 1) * 128], func=AF.Identity,
                           scale=T["CORR"][:, tc * NH + h:tc * NH + h + 1],
                           waits=tuple(scw) if hl == 0 else WAITS_NONE)
                    inc_last("scalar", CACT)
                    csc = w(CACT)
                    op("vector", "tensor_tensor",
                       OCH[:, c * D + g * 512:c * D + (g + 1) * 512],
                       OCH[:, c * D + g * 512:c * D + (g + 1) * 512],
                       tm[:, :], op=ALU.add, waits=(csc, odma), inc=CDVE)
                    marks[("corrva", tile, c, g)] = w(CDVE)
                    prev_vadd = w(CDVE)
                corr_add[c] = w(CDVE)
                qch_used[c] = w(CPE)

                gw = [g1_done[tile], consts_dma]
                if c >= 1 and sig_done[c - 1]:
                    gw.append(sig_done[c - 1])
                for q4 in range(4):
                    op("tensor", "matmul", PGATE[:, q4 * 512:(q4 + 1) * 512],
                       T["G1"][:, tile * 1024 + c * CH:tile * 1024 + (c + 1) * CH],
                       T["WG2B"][:, q4 * 512:(q4 + 1) * 512],
                       start=True, stop=True,
                       waits=tuple(gw) if q4 == 0 else WAITS_NONE)
                inc_last("tensor", CPE)
                gmm = w(CPE)
                for q4 in range(4):
                    op("scalar", "activation",
                       out=GATS[:, c * D + q4 * 512:c * D + (q4 + 1) * 512],
                       in_=PGATE[:, q4 * 512:(q4 + 1) * 512], func=AF.Sigmoid,
                       waits=(gmm,) if q4 == 0 else WAITS_NONE)
                inc_last("scalar", CACT)
                sig_done[c] = w(CACT)
            if "d_gats" in debug and tile == 0:
                dma("sync", dbg["d_gats"].ap(), GATS[:, :],
                    waits=(w(CACT),))

            ogfm_e = [None] * TPT
            sq_marks = [None] * TPT
            for c in range(TPT):
                xw = [corr_add[c], sig_done[c]]
                if c >= 1 and ogfm_e[c - 1]:
                    xw.append(ogfm_e[c - 1])
                if c >= 1 and sq_marks[c - 1]:
                    xw.append(sq_marks[c - 1])
                op("vector", "tensor_tensor", XG[:, :], OCH[:, c * D:(c + 1) * D],
                   GATS[:, c * D:(c + 1) * D], op=ALU.mult, waits=tuple(xw),
                   inc=CDVE)
                xg_done = w(CDVE)
                op("scalar", "activation", out=SQB[:, :], in_=XG[:, :],
                   func=AF.Square, waits=(xg_done,), inc=CACT)
                sq_marks[c] = w(CACT)
                op("vector", "tensor_reduce", T["MS"][:, :],
                   SQB[:].rearrange("p (h e) -> p h e", h=NH), AX.X, ALU.add,
                   waits=(w(CACT),), inc=CDVE)
                op("scalar", "activation", out=T["SD2"][:, :], in_=T["MS"][:, :],
                   func=AF.Sqrt, scale=T["EPSC"][:, 2:3],
                   bias=T["EPSC"][:, 1:2], waits=(w(CDVE),), inc=CACT)
                op("vector", "reciprocal", T["R2"][:, :], T["SD2"][:, :],
                   waits=(w(CACT),))
                inc_last("vector", CDVE)
                r2d = w(CDVE)
                for h in range(NH):
                    op("vector", "tensor_scalar",
                       out=OGT[:, h * HD:(h + 1) * HD],
                       in0=XG[:, h * HD:(h + 1) * HD],
                       scalar1=T["R2"][:, h:h + 1], scalar2=None, op0=ALU.mult,
                       waits=(r2d,) if h == 0 else WAITS_NONE)
                inc_last("vector", CDVE)
                ogd = w(CDVE)
                tw = [ogd]
                if c >= 1 and ogfm_e[c - 1]:
                    tw.append(ogfm_e[c - 1])
                if c == 0 and tile == 1:
                    tw.append(marks[("ogfmall", 0)])
                for dc in range(NDC):
                    op("tensor", "transpose", PT2[:, dc * 128:(dc + 1) * 128],
                       OGT[:, dc * 128:(dc + 1) * 128], T["IDN"][:, :],
                       waits=tuple(tw) if dc == 0 else WAITS_NONE)
                inc_last("tensor", CPE)
                tp2 = w(CPE)
                for dc in range(NDC):
                    op("vector", "tensor_scalar",
                       out=T["XLNFM"][:, dc * 1024 + c * CH:dc * 1024 + (c + 1) * CH],
                       in0=PT2[:, dc * 128:(dc + 1) * 128],
                       scalar1=T["NW"][:, dc:dc + 1], scalar2=None, op0=ALU.mult,
                       waits=(tp2,) if dc == 0 else WAITS_NONE)
                inc_last("vector", CDVE)
                ogfm_e[c] = w(CDVE)
            marks[("ogfmall", tile)] = w(CDVE)
            og_all = w(CDVE)
            if "d_ogfm" in debug and tile == 0:
                dma("sync", dbg["d_ogfm"].ap(), T["XLNFM"][:, :],
                    waits=(og_all,))

            fence()
            WOH = T[f"WOH{tile}"]
            XRCH = T[f"XRCH{tile}"]
            X2H = T[f"X2H{tile}"]
            TRASH2 = T[f"TRASH2{tile}"]
            PB = T[f"PB{tile}"]
            x2w = [[None] * TPT, [None] * TPT]
            x2free = {}
            pbe_all = []
            for H in range(2):
                wodma = dma("gpsimd",
                            WOH[:].rearrange("p (dc e) -> p dc e", dc=NDC),
                            wo[:, H * 1024:(H + 1) * 1024].rearrange(
                                "(dc p) e -> p dc e", p=128),
                            waits=(w(CPE),) if H == 1 else WAITS_NONE)
                pbe = [None] * TPT
                for c in range(TPT):
                    tc = tile * TPT + c
                    slot = c % 2
                    gi = H * TPT + c
                    rw = [pbe_all[gi - 2]] if gi >= 2 else []
                    xrdma = dma("sync", XRCH[slot][:, :],
                                x_sl[tc * CH:(tc + 1) * CH,
                                     H * 1024:(H + 1) * 1024],
                                waits=tuple(x for x in rw if x))
                    mw = [og_all, wodma]
                    if gi >= 2 and pbe_all[gi - 2]:
                        mw.append(pbe_all[gi - 2])
                    for dc in range(NDC):
                        for n in range(2):
                            op("tensor", "matmul",
                               PB[slot][:, n * 512:(n + 1) * 512],
                               T["XLNFM"][:, dc * 1024 + c * CH:dc * 1024 + (c + 1) * CH],
                               WOH[:, dc * 1024 + n * 512:dc * 1024 + (n + 1) * 512],
                               start=(dc == 0), stop=(dc == NDC - 1),
                               waits=tuple(mw) if (dc == 0 and n == 0)
                               else WAITS_NONE)
                    inc_last("tensor", CPE)
                    bmm = w(CPE)
                    ew = [bmm, xrdma]
                    ew += list(x2free.get(gi - 2, ()))
                    op("vector", "tensor_tensor", X2H[slot][:, :],
                       PB[slot][:, :], XRCH[slot][:, :],
                       op=ALU.add, waits=tuple(ew), inc=CDVE)
                    ev = w(CDVE)
                    pbe[c] = ev
                    pbe_all.append(ev)
                    op("vector", "tensor_reduce",
                       T["S2S"][:, c * 2 + H:c * 2 + H + 1], X2H[slot][:, :],
                       AX.X, ALU.add, inc=CDVE)
                    red = w(CDVE)
                    op("scalar", "activation", out=TRASH2[:, :],
                       in_=X2H[slot][:, :], func=AF.Square,
                       accum_out=T["Q2S"][:, c * 2 + H:c * 2 + H + 1],
                       waits=(ev,))
                    op("scalar", "activation", out=T["DUMT"][:, :],
                       in_=T["EPSC"][:, 0:1], func=AF.Identity, inc=CACT)
                    sqd = w(CACT)
                    x2w[H][c] = dma(
                        "gpsimd",
                        out_sl[tc * CH:(tc + 1) * CH,
                               H * 1024:(H + 1) * 1024],
                        X2H[slot][:, :], waits=(ev,))
                    x2free[gi] = (x2w[H][c], sqd, red)
            bstats = (w(CDVE), w(CACT))

            fence()
            XB2 = T[f"XB2{tile}"]
            XLNT2 = T[f"XLNT2{tile}"]
            PT3 = T[f"PT3{tile}"]
            xl2e = [None] * TPT
            nrm_done = [None] * TPT
            if "d_stats" in debug and tile == 0:
                dma("sync", dbg["d_stats"][:, 0:16], T["S2S"][:, :],
                    waits=bstats)
                dma("sync", dbg["d_stats"][:, 16:32], T["Q2S"][:, :],
                    waits=bstats)
            for c in range(TPT):
                tc = tile * TPT + c
                slot = c % 2
                rw = [nrm_done[c - 2]] if c >= 2 else []
                rw += [x2w[0][c], x2w[1][c]]
                xbdma = dma("sync", XB2[slot][:, :],
                            out_sl[tc * CH:(tc + 1) * CH, :],
                            waits=tuple(x for x in rw if x))
                if "d_x2" in debug and tile == 0:
                    dma("sync", dbg["d_x2"][:, c * D:(c + 1) * D],
                        XB2[slot][:, :], waits=(xbdma,))
                mcol = T["STATS"][:, c:c + 1]
                vcol = T["STATS"][:, 8 + c:9 + c]
                rcol = T["STATS"][:, 16 + c:17 + c]
                bcol = T["STATS"][:, 24 + c:25 + c]
                tcol = T["STATS"][:, 32 + c:33 + c]
                scol = T["STATS"][:, 40 + c:41 + c]
                op("vector", "tensor_tensor", mcol,
                   T["S2S"][:, c * 2:c * 2 + 1],
                   T["S2S"][:, c * 2 + 1:c * 2 + 2], op=ALU.add,
                   waits=bstats if c == 0 else WAITS_NONE)
                op("vector", "tensor_scalar", out=mcol, in0=mcol,
                   scalar1=RECIP_D, scalar2=None, op0=ALU.mult)
                op("vector", "tensor_tensor", vcol,
                   T["Q2S"][:, c * 2:c * 2 + 1],
                   T["Q2S"][:, c * 2 + 1:c * 2 + 2], op=ALU.add)
                op("vector", "tensor_scalar", out=vcol, in0=vcol,
                   scalar1=RECIP_D, scalar2=None, op0=ALU.mult)
                op("vector", "tensor_tensor", tcol, mcol, mcol, op=ALU.mult)
                op("vector", "tensor_tensor", vcol, vcol, tcol,
                   op=ALU.subtract, inc=CDVE)
                op("scalar", "activation", out=scol, in_=vcol, func=AF.Sqrt,
                   bias=T["EPSC"][:, 0:1], waits=(w(CDVE),), inc=CACT)
                op("vector", "reciprocal", rcol, scol, waits=(w(CACT),))
                op("vector", "tensor_scalar", out=bcol, in0=mcol, scalar1=rcol,
                   scalar2=-1.0, op0=ALU.mult, op1=ALU.mult, inc=CDVE)
                nwt = [w(CDVE), xbdma]
                if c >= 2 and nrm_done[c - 2]:
                    nwt.append(nrm_done[c - 2])
                op("scalar", "activation", out=XLNT2[slot][:, :],
                   in_=XB2[slot][:, :], func=AF.Identity, scale=rcol, bias=bcol,
                   waits=tuple(nwt), inc=CACT)
                nrm_done[c] = w(CACT)
                tw = [nrm_done[c]]
                if c >= 1 and xl2e[c - 1]:
                    tw.append(xl2e[c - 1])
                for dc in range(NDC):
                    op("tensor", "transpose", PT3[:, dc * 128:(dc + 1) * 128],
                       XLNT2[slot][:, dc * 128:(dc + 1) * 128], T["IDN"][:, :],
                       waits=tuple(tw) if dc == 0 else WAITS_NONE)
                inc_last("tensor", CPE)
                tpd = w(CPE)
                dst = T[f"XLN2FM{tile}"][:].rearrange("p (dc t) -> p dc t", dc=NDC)[
                    :, :, c * CH:(c + 1) * CH]
                src = PT3[:].rearrange("p (dc t) -> p dc t", dc=NDC)
                op("vector", "tensor_copy", dst, src, waits=(tpd,), inc=CDVE)
                xl2e[c] = w(CDVE)
            xl2all = w(CDVE)
            if "d_stats2" in debug and tile == 0:
                dma("sync", dbg["d_stats2"].ap(), T["STATS"][:, :],
                    waits=(w(CACT), w(CDVE)))
            if "d_xln2fm" in debug and tile == 0:
                dma("sync", dbg["d_xln2fm"].ap(), T[f"XLN2FM{tile}"][:, :],
                    waits=(xl2all,))

            FSP = T[f"FSP{tile}"]
            OUTES = T[f"OUTES{tile}"]
            PF = T[f"PF{tile}"]
            fsp_used = [None] * 4
            prev_pf_evac = None
            prev_outes = [None, None]
            for es in range(4):
                slot = es % 2
                fw = [fsum_cc]
                if es >= 2 and fsp_used[es - 2]:
                    fw.append(fsp_used[es - 2])
                fsdma = []
                for rr in range(2):
                    dstp = FSP[slot][:].rearrange("p (dc e) -> p dc e", dc=NDC)[
                        :, :, rr * 256:(rr + 1) * 256]
                    fsdma.append(dma(
                        "sync", dstp,
                        cc_fs_out[(2 * es + rr) * D:(2 * es + rr + 1) * D]
                        .rearrange("(dc p) e -> p dc e", p=128),
                        waits=tuple(fw) if rr == 0 else WAITS_NONE))
                for half in range(2):
                    hw_ = [xl2all] + fsdma
                    if prev_pf_evac:
                        hw_.append(prev_pf_evac)
                    for tcq in range(4):
                        c = half * 4 + tcq
                        op("tensor", "matmul", PF[tcq][:, :], T["ONES"][:, :],
                           T["BIASB"][:, es * 512:(es + 1) * 512],
                           start=True, stop=False,
                           waits=tuple(x for x in hw_ if x) if tcq == 0
                           else WAITS_NONE)
                        for dc in range(NDC):
                            op("tensor", "matmul", PF[tcq][:, :],
                               T[f"XLN2FM{tile}"][:, dc * 1024 + c * CH:dc * 1024 + (c + 1) * CH],
                               FSP[slot][:, dc * 512:(dc + 1) * 512],
                               start=False, stop=(dc == NDC - 1))
                    inc_last("tensor", CPE)
                    fmm = w(CPE)
                    evw = [fmm]
                    if prev_outes[half]:
                        evw.append(prev_outes[half])
                    for tcq in range(4):
                        op("vector", "tensor_copy",
                           OUTES[half][:, tcq * 512:(tcq + 1) * 512],
                           PF[tcq][:, :], waits=tuple(evw) if tcq == 0
                           else WAITS_NONE)
                    inc_last("vector", CDVE)
                    evd = w(CDVE)
                    prev_pf_evac = evd
                    prev_outes[half] = dma("gpsimd",
                        out_sl[tile * 1024 + half * 512:
                               tile * 1024 + (half + 1) * 512,
                               es * 512:(es + 1) * 512].rearrange(
                            "(tc t) e -> t tc e", t=CH),
                        OUTES[half][:].rearrange("p (tc e) -> p tc e", tc=4),
                        waits=(evd,), accum_op=ALU.add)
                    _ = prev_outes
                fsp_used[es] = w(CPE)

        def trunc(name):
            return TRUNCATE == name

        while True:
            if trunc("prelude"):
                break
            ln_phase(0)
            if trunc("ln0") or (TRUNCATE or "").startswith("ln0c") \
                    or (TRUNCATE or "").startswith("ln0d") \
                    or (TRUNCATE or "").startswith("ln0r") \
                    or (TRUNCATE or "").startswith("ln0s") \
                    or (TRUNCATE or "").startswith("ln0n") \
                    or TRUNCATE == "ln0tp":
                break
            for g in range(NQ):
                quad_phase(0, g)
            marks[("allproj", 0)] = marks[("proj", "v", 0, NQ - 1)]
            if trunc("quads0"):
                break
            ln_phase(1)
            if trunc("ln1"):
                break
            for g in range(NQ):
                quad_phase(1, g)
            if trunc("quads1"):
                break
            fence()
            tail_phase(0)
            if trunc("tail0"):
                break
            fence()
            tail_phase(1)
            break

        for s in GPP:
            op("gpsimd", "nop", waits=((s, s.n),))
        for s in SXP:
            op("sync", "nop", waits=((s, s.n),))

    with nc.Block() as block:
        block.sync(lambda e: walk("sync", e))
        block.scalar(lambda e: walk("scalar", e))
        block.vector(lambda e: walk("vector", e))
        block.tensor(lambda e: walk("tensor", e))
        block.gpsimd(lambda e: walk("gpsimd", e))

    top.close()
    return nc


def _host_tables(core):
    half = core % 2
    p0 = half * NT
    ll = _LL.astype(np.float32)
    theta = (np.float32(10000.0) ** (np.float32(-2.0)
             * np.arange(64, dtype=np.float32) / np.float32(HD))).astype(np.float32)
    pos = np.arange(p0, p0 + NT, dtype=np.float32)
    ang = (pos[:, None] * theta[None, :]).astype(np.float32)
    cos_t = np.cos(ang.astype(np.float64)).astype(np.float32)
    sin_t = np.sin(ang.astype(np.float64)).astype(np.float32)
    i = np.arange(CH, dtype=np.float64)
    diff = i[None, :] - i[:, None]
    dm = np.where(diff >= 0, np.exp(ll[:, None, None].astype(np.float64)
                                    * diff[None]), 0.0)
    dmask = np.ascontiguousarray(
        dm.transpose(1, 0, 2).reshape(CH, NH * CH)).astype(np.float32)
    dk = np.exp(ll[None, :].astype(np.float64)
                * (CH - 1.0 - i[:, None])).astype(np.float32)
    dkt = np.repeat(dk[:, :, None], CH, axis=2).reshape(CH, NH * CH)
    dkt = np.ascontiguousarray(dkt).astype(np.float32)
    dqv = np.exp(ll[None, :].astype(np.float64) * (i[:, None] + 1.0)
                 ).astype(np.float32)
    oddf = 1.0 if (core % 2 == 1) else 0.0
    lamc = np.exp(ll.astype(np.float64) * CH)
    corr = np.zeros((CH, TCC * NH), np.float64)
    for c in range(TCC):
        corr[:, c * NH:(c + 1) * NH] = dqv.astype(np.float64) * (lamc ** c)[None, :]
    corr = (corr * oddf).astype(np.float32)
    lamct = np.tile(lamc.astype(np.float32)[None, :], (CH, 1))
    return cos_t, sin_t, dmask, dkt, dqv, corr, lamct


_NC_CACHE = {}


def _get_nc(debug=()):
    key = tuple(sorted(debug))
    if key not in _NC_CACHE:
        _NC_CACHE[key] = build_program(debug=key)
    return _NC_CACHE[key]


def kernel(L0_attn_x, L0_attn_Wq, L0_attn_Wk, L0_attn_Wv, L0_attn_Wo,
           L0_attn_Wg1, L0_attn_Wg2, L0_attn_norm_weight, L0_ffn_factors,
           L0_ffn_bias, _debug=(), _nc=None):
    x = np.ascontiguousarray(np.asarray(L0_attn_x, np.float32))
    weights = {
        "wq": np.ascontiguousarray(np.asarray(L0_attn_Wq, np.float32)),
        "wk": np.ascontiguousarray(np.asarray(L0_attn_Wk, np.float32)),
        "wv": np.ascontiguousarray(np.asarray(L0_attn_Wv, np.float32)),
        "wo": np.ascontiguousarray(np.asarray(L0_attn_Wo, np.float32)),
        "wg1": np.ascontiguousarray(np.asarray(L0_attn_Wg1, np.float32)),
        "wg2": np.ascontiguousarray(np.asarray(L0_attn_Wg2, np.float32)),
        "nw_in": np.ascontiguousarray(np.asarray(L0_attn_norm_weight, np.float32)),
        "bias_in": np.ascontiguousarray(
            np.asarray(L0_ffn_bias, np.float32).reshape(1, D)),
    }
    factors = np.asarray(L0_ffn_factors, np.float32)
    iden = np.eye(128, dtype=np.float32)

    nc = _nc if _nc is not None else _get_nc(_debug)
    in_maps = []
    for core in range(8):
        b, half = core // 2, core % 2
        cos_t, sin_t, dmask, dkt, dqv, corr, lamct = _host_tables(core)
        m = dict(weights)
        m["x_sl"] = np.ascontiguousarray(x[b, half * NT:(half + 1) * NT, :])
        m["fac_sl"] = np.ascontiguousarray(
            factors[:, :, core * 256:(core + 1) * 256])
        m["cos_in"] = cos_t
        m["sin_in"] = sin_t
        m["dmask_in"] = dmask
        m["dkt_in"] = dkt
        m["dqv_in"] = dqv
        m["corr_in"] = corr
        m["lamc_in"] = lamct
        m["idn_in"] = iden
        in_maps.append(m)

    res = run_bass_kernel_spmd(nc, in_maps, core_ids=list(range(8)))
    out = np.empty((4, 4096, D), np.float32)
    for core in range(8):
        b, half = core // 2, core % 2
        out[b, half * NT:(half + 1) * NT, :] = res.results[core]["out_sl"]
    if _debug:
        return out, res
    return out


# version 16
# speedup vs baseline: 1.0843x; 1.0064x over previous
import numpy as np
from contextlib import ExitStack

import concourse.bass as bass
import concourse.mybir as mybir
from concourse.bass_utils import run_bass_kernel_spmd

F32 = mybir.dt.float32
BF16 = mybir.dt.bfloat16
AF = mybir.ActivationFunctionType
ALU = mybir.AluOpType
AX = mybir.AxisListType

D = 2048
NT = 2048
CH = 128
TCC = 16
TPT = 8
NH = 16
HD = 128
QH = 4
NQ = 4
NDC = 16
EPS_LN = 1e-5
EPS_RMS = 1e-6
RECIP_D = 1.0 / 2048.0

ENGINES = ("sync", "scalar", "vector", "tensor", "gpsimd")
RACE_DETECT = False
PE_DENSE_INC = False
TRUNCATE = None
EVAC_MODE = 0
SKIP_CC = False

_LL = -(2.0 ** (-8.0 * (np.arange(NH) + 1.0) / NH))
_LAMC = np.exp(_LL * CH)


class Sem:
    __slots__ = ("h", "n")

    def __init__(self, h):
        self.h = h
        self.n = 0


def build_program(debug=()):
    nc = bass.Bass("TRN2", target_bir_lowering=False, debug=False,
                   detect_race_conditions=RACE_DETECT)

    x_sl = nc.dram_tensor("x_sl", [NT, D], F32, kind="ExternalInput")
    wq = nc.dram_tensor("wq", [D, D], F32, kind="ExternalInput")
    wk = nc.dram_tensor("wk", [D, D], F32, kind="ExternalInput")
    wv = nc.dram_tensor("wv", [D, D], F32, kind="ExternalInput")
    wo = nc.dram_tensor("wo", [D, D], F32, kind="ExternalInput")
    wg1 = nc.dram_tensor("wg1", [D, HD], F32, kind="ExternalInput")
    wg2 = nc.dram_tensor("wg2", [HD, D], F32, kind="ExternalInput")
    nw_in = nc.dram_tensor("nw_in", [D], F32, kind="ExternalInput")
    bias_in = nc.dram_tensor("bias_in", [1, D], F32, kind="ExternalInput")
    fac_sl = nc.dram_tensor("fac_sl", [8, D, 256], F32, kind="ExternalInput")
    cos_in = nc.dram_tensor("cos_in", [NT, 64], F32, kind="ExternalInput")
    sin_in = nc.dram_tensor("sin_in", [NT, 64], F32, kind="ExternalInput")
    dmask_in = nc.dram_tensor("dmask_in", [CH, D], F32, kind="ExternalInput")
    dkt_in = nc.dram_tensor("dkt_in", [CH, D], F32, kind="ExternalInput")
    dqv_in = nc.dram_tensor("dqv_in", [CH, NH], F32, kind="ExternalInput")
    corr_in = nc.dram_tensor("corr_in", [CH, TCC * NH], F32, kind="ExternalInput")
    idn_in = nc.dram_tensor("idn_in", [128, 128], F32, kind="ExternalInput")
    lamc_in = nc.dram_tensor("lamc_in", [CH, NH], F32, kind="ExternalInput")
    out_sl = nc.dram_tensor("out_sl", [NT, D], F32, kind="ExternalOutput")

    def dscratch(name, shape, dtype):
        kind = "ExternalOutput" if name in debug else "Internal"
        return nc.dram_tensor(name, shape, dtype, kind=kind)

    o_spill = dscratch("o_spill", [TCC, CH, D], BF16)
    q_spill = dscratch("q_spill", [TCC, NH, HD, CH], BF16)
    cc_fs_in = nc.dram_tensor("cc_fs_in", [D, 256], BF16)
    cc_fs_out = nc.dram_tensor("cc_fs_out", [8 * D, 256], BF16)
    cc_s_in = [nc.dram_tensor(f"cc_s_in{g}", [HD, QH * HD], F32) for g in range(NQ)]
    cc_s_out = [nc.dram_tensor(f"cc_s_out{g}", [2 * HD, QH * HD], F32)
                for g in range(NQ)]
    dbg = {}
    for name, shape, dt in (
        ("d_xlnfm", [128, NDC * 1024], BF16),
        ("d_g1", [128, NT], BF16),
        ("d_s16", [128, NH * HD], F32),
        ("d_gats", [128, TPT * D], BF16),
        ("d_ogfm", [128, NDC * 1024], BF16),
        ("d_xln2fm", [128, NDC * 1024], BF16),
        ("d_x2", [128, TPT * D], F32),
        ("d_stats", [128, 64], F32),
        ("d_stats2", [128, 64], F32),
        ("d_stats3", [128, 64], F32),
    ):
        if name in debug:
            dbg[name] = nc.dram_tensor(name, shape, dt, kind="ExternalOutput")

    top = ExitStack()

    def mk(stk, name, shape, dtype):
        return stk.enter_context(nc.sbuf_tensor(name, shape, dtype))

    def mkp(stk, name, shape, dtype):
        return stk.enter_context(nc.psum_tensor(name, shape, dtype))

    T = {}
    T["XLNFM"] = mk(top, "XLNFM", [128, NDC * 1024], BF16)
    T["G1"] = mk(top, "G1", [128, NT], BF16)
    T["WG2B"] = mk(top, "WG2B", [HD, D], BF16)
    T["WG1B"] = mk(top, "WG1B", [128, NDC * HD], BF16)
    T["NW"] = mk(top, "NW", [128, 16], F32)
    T["BIASB"] = mk(top, "BIASB", [1, D], BF16)
    T["ONES"] = mk(top, "ONES", [1, 128], BF16)
    T["IDN"] = mk(top, "IDN", [128, 128], BF16)
    T["DQV"] = mk(top, "DQV", [128, NH], F32)
    T["CORR"] = mk(top, "CORR", [128, TCC * NH], F32)
    T["STATS"] = mk(top, "STATS", [128, 64], F32)
    T["S2S"] = mk(top, "S2S", [128, 16], F32)
    T["Q2S"] = mk(top, "Q2S", [128, 16], F32)
    T["MS"] = mk(top, "MS", [128, 16], F32)
    T["SD2"] = mk(top, "SD2", [128, 16], F32)
    T["R2"] = mk(top, "R2", [128, 16], F32)
    T["EPSC"] = mk(top, "EPSC", [128, 4], F32)
    T["DUMT"] = mk(top, "DUMT", [128, 1], F32)
    T["LAMCT"] = mk(top, "LAMCT", [128, NH], F32)

    qs = ExitStack()
    T["COS"] = mk(qs, "COS", [128, TCC * 64], BF16)
    T["SIN"] = mk(qs, "SIN", [128, TCC * 64], BF16)
    T["DMASK"] = mk(qs, "DMASK", [CH, D], BF16)
    T["DKT"] = mk(qs, "DKT", [CH, D], BF16)
    T["S"] = mk(qs, "S", [128, NH * HD], F32)
    T["SB"] = mk(qs, "SB", [128, NH * HD], BF16)
    T["XCH"] = [mk(qs, f"XCH{i}", [128, D], F32) for i in range(2)]
    T["XLNT"] = [mk(qs, f"XLNT{i}", [128, D], BF16) for i in range(2)]
    T["TRASH"] = mk(qs, "TRASH", [128, D], BF16)
    T["WQ"] = mk(qs, "WQ", [128, NDC * 512], BF16)
    T["WK"] = mk(qs, "WK", [128, NDC * 512], BF16)
    T["WV"] = mk(qs, "WV", [128, NDC * 512], BF16)
    T["QSB"] = [mk(qs, f"QSB{i}", [128, 512], BF16) for i in range(2)]
    T["QTM"] = [mk(qs, f"QTM{i}", [128, 512], BF16) for i in range(2)]
    T["TMP1"] = mk(qs, "TMP1", [128, QH * 64], F32)
    T["TMP2"] = mk(qs, "TMP2", [128, QH * 64], F32)
    T["QFM"] = mk(qs, "QFM", [128, QH * 1024], BF16)
    T["KFM"] = mk(qs, "KFM", [128, QH * 1024], BF16)
    T["K2"] = mk(qs, "K2", [128, TPT * 512], BF16)
    T["V"] = mk(qs, "V", [128, TPT * 512], BF16)
    T["SCT"] = [mk(qs, f"SCT{i}", [128, 512], BF16) for i in range(2)]
    T["TMPOE"] = mk(qs, "TMPOE", [128, 512], F32)
    T["OQ"] = mk(qs, "OQ", [128, TPT * 512], BF16)
    qps = ExitStack()
    T["PP"] = [mkp(qps, f"PP{i}", [128, 512], F32) for i in range(2)]
    T["PT"] = mkp(qps, "PT", [128, D], BF16)
    T["PSC"] = mkp(qps, "PSC", [128, 512], F32)
    T["POI"] = mkp(qps, "POI", [128, 512], F32)
    T["POE"] = mkp(qps, "POE", [128, 512], F32)
    T["PKV"] = mkp(qps, "PKV", [128, 512], F32)
    qps.close()
    qs.close()

    for tile in range(2):
        ts = ExitStack()
        T[f"S16ST{tile}"] = mk(ts, f"S16ST{tile}", [128, NH * HD], F32)
        T[f"S16B{tile}"] = mk(ts, f"S16B{tile}", [128, NH * HD], BF16)
        T[f"OCH{tile}"] = mk(ts, f"OCH{tile}", [128, TPT * D], BF16)
        T[f"QCH{tile}"] = [mk(ts, f"QCH{tile}_{i}", [128, NH * CH], BF16)
                           for i in range(2)]
        T[f"TMPC{tile}"] = [mk(ts, f"TMPC{tile}_{i}", [128, 512], F32)
                            for i in range(2)]
        T[f"GATS{tile}"] = mk(ts, f"GATS{tile}", [128, TPT * D], BF16)
        T[f"XG{tile}"] = [mk(ts, f"XG{tile}_{i}", [128, D], F32)
                          for i in range(2)]
        T[f"SQB{tile}"] = mk(ts, f"SQB{tile}", [128, D], F32)
        T[f"OGT{tile}"] = mk(ts, f"OGT{tile}", [128, D], BF16)
        aps = ExitStack()
        T[f"PCORR{tile}"] = [mkp(aps, f"PCORR{tile}_{i}", [128, 512], F32)
                             for i in range(2)]
        T[f"PGATE{tile}"] = mkp(aps, f"PGATE{tile}", [128, D], F32)
        T[f"PT2{tile}"] = mkp(aps, f"PT2{tile}", [128, D], BF16)
        aps.close()
        ts.close()
        bs = ExitStack()
        T[f"XLN2FM{tile}"] = mk(bs, f"XLN2FM{tile}", [128, NDC * 1024], BF16)
        T[f"WOH{tile}"] = mk(bs, f"WOH{tile}", [128, NDC * 1024], BF16)
        T[f"XRCH{tile}"] = [mk(bs, f"XRCH{tile}_{i}", [128, 1024], F32)
                            for i in range(2)]
        T[f"X2H{tile}"] = [mk(bs, f"X2H{tile}_{i}", [128, 1024], F32)
                           for i in range(2)]
        T[f"TRASH2{tile}"] = mk(bs, f"TRASH2{tile}", [128, 1024], F32)
        bps = ExitStack()
        T[f"PB{tile}"] = [mkp(bps, f"PB{tile}_{i}", [128, 1024], F32)
                          for i in range(2)]
        bps.close()
        cs = ExitStack()
        T[f"XB2{tile}"] = [mk(cs, f"XB2{tile}_{i}", [128, D], F32)
                           for i in range(2)]
        T[f"XLNT2{tile}"] = [mk(cs, f"XLNT2{tile}_{i}", [128, D], BF16)
                             for i in range(2)]
        T[f"FSP{tile}"] = [mk(cs, f"FSP{tile}_{i}", [128, NDC * 512], BF16)
                           for i in range(2)]
        T[f"OUTES{tile}"] = [mk(cs, f"OUTES{tile}_{i}", [128, 4 * 512], F32)
                             for i in range(2)]
        cps = ExitStack()
        T[f"PT3{tile}"] = mkp(cps, f"PT3{tile}", [128, D], BF16)
        T[f"PF{tile}"] = [mkp(cps, f"PF{tile}_{i}", [128, 512], F32)
                          for i in range(6)]
        cps.close()
        cs.close()
        bs.close()

    NSX, NGP = 8, 8
    SXP = [Sem(top.enter_context(nc.semaphore(f"dsx{i}"))) for i in range(NSX)]
    GPP = [Sem(top.enter_context(nc.semaphore(f"dgp{i}"))) for i in range(NGP)]
    CCP = [Sem(top.enter_context(nc.semaphore(f"dcc{i}"))) for i in range(5)]
    CPE = Sem(top.enter_context(nc.semaphore("cpe")))

    class ChainSem:
        W = 4

        def __init__(self, name):
            self.sems = [Sem(top.enter_context(nc.semaphore(f"{name}{i}")))
                         for i in range(self.W)]
            self.n = 0

        def mark(self):
            if self.n == 0:
                return (self.sems[0], 0)
            i = self.n - 1
            return (self.sems[i % self.W], i // self.W + 1)

    CACT = ChainSem("cact")
    CDVE = ChainSem("cdve")
    ALLSEMS = tuple(SXP) + tuple(GPP) + tuple(CCP) + (CPE,)         + tuple(CACT.sems) + tuple(CDVE.sems)

    def walk(eng_name, E):
        for s in ALLSEMS:
            s.n = 0
        CACT.n = 0
        CDVE.n = 0

        SELF = {"vector": CDVE, "scalar": CACT}
        last_inst = {e: None for e in ENGINES}
        last_mark = {e: None for e in ENGINES}

        def op(engine, method, *a, inc=None, incn=1, waits=(), **k):
            if PE_DENSE_INC and engine == "tensor":
                assert inc is None or inc is CPE
                CPE.n += 1
                if engine == eng_name:
                    for s_, v_ in waits:
                        if s_ is not None:
                            E.wait_ge(s_.h, v_)
                    instr = getattr(E, method)(*a, **k)
                    instr.then_inc(CPE.h, 1)
                    last_inst[engine] = instr
                    last_mark[engine] = (CPE, CPE.n)
                return
            selfsem = SELF.get(engine)
            if selfsem is not None:
                assert inc is None or inc is selfsem
                prevmark = selfsem.mark()
                selfsem.n += 1
                mysem, myval = selfsem.mark()
                if engine == eng_name:
                    for s_, v_ in waits:
                        if s_ is not None:
                            E.wait_ge(s_.h, v_)
                    E.wait_ge(prevmark[0].h, prevmark[1])
                    instr = getattr(E, method)(*a, **k)
                    instr.then_inc(mysem.h, 1)
                    last_inst[engine] = instr
                return
            if inc is not None:
                inc.n += incn
            if engine == eng_name:
                for s_, v_ in waits:
                    if s_ is not None:
                        E.wait_ge(s_.h, v_)
                instr = getattr(E, method)(*a, **k)
                if instr is not None:
                    last_inst[engine] = instr
                    last_mark[engine] = None
                if inc is not None:
                    instr.then_inc(inc.h, incn)
                    last_mark[engine] = (inc, inc.n)

        def inc_last(engine, sem, n=1):
            if SELF.get(engine) is not None:
                assert sem is SELF[engine]
                return
            if PE_DENSE_INC and engine == "tensor":
                assert sem is CPE
                return
            prev = last_mark[engine]
            sem.n += n
            if engine == eng_name:
                if last_inst[engine] is None or prev is not None:
                    if prev is not None:
                        E.wait_ge(prev[0].h, prev[1])
                    E.nop().then_inc(sem.h, n)
                else:
                    last_inst[engine].then_inc(sem.h, n)
            last_mark[engine] = (sem, sem.n)

        qstate = {"sync": [SXP, 0], "gpsimd": [GPP, 0], "scalar": [SXP, 0]}

        def dma(engine, out_ap, in_ap, dsem=None, waits=(), **k):
            pool, idx = qstate[engine]
            s = pool[idx % len(pool)]
            qstate[engine][1] = idx + 1
            op(engine, "dma_start", inc=s, incn=16,
               waits=tuple(waits) + ((s, s.n),),
               out=out_ap, in_=in_ap, **k)
            return (s, s.n)

        def w(sem_):
            if isinstance(sem_, ChainSem):
                return sem_.mark()
            return (sem_, sem_.n)

        def fence():
            marks = []
            for en, cs in (("tensor", CPE), ("scalar", CACT), ("vector", CDVE)):
                if en == "tensor":
                    inc_last(en, cs)
                marks.append((en, w(cs)))
            dm = [(s, s.n) for s in SXP + GPP]
            for en in ENGINES:
                for other, mk_ in marks:
                    if other != en:
                        op(en, "nop", waits=(mk_,))
                for mk_ in dm:
                    op(en, "nop", waits=(mk_,))

        WAITS_NONE = ()

        dma("gpsimd", T["IDN"][:, :], idn_in.ap())
        dma("gpsimd", T["COS"][:].rearrange("p (tc f) -> p tc f", tc=TCC),
            cos_in.ap().rearrange("(tc p) f -> p tc f", p=128))
        dma("gpsimd", T["SIN"][:].rearrange("p (tc f) -> p tc f", tc=TCC),
            sin_in.ap().rearrange("(tc p) f -> p tc f", p=128))
        dma("gpsimd", T["DMASK"][:, :], dmask_in.ap())
        dma("gpsimd", T["DKT"][:, :], dkt_in.ap())
        dma("gpsimd", T["DQV"][:, :], dqv_in.ap())
        dma("gpsimd", T["CORR"][:, :], corr_in.ap())
        dma("gpsimd", T["NW"][:, :],
            nw_in.ap().rearrange("(dc p) -> p dc", p=128),
            allow_slow_non_contiguous=True)
        dma("gpsimd", T["BIASB"][:, :], bias_in.ap())
        dma("gpsimd", T["WG1B"][:].rearrange("p (dc r) -> p dc r", dc=NDC),
            wg1.ap().rearrange("(dc p) r -> p dc r", p=128))
        dma("gpsimd", T["LAMCT"][:, :], lamc_in.ap())
        consts_dma = dma("gpsimd", T["WG2B"][:, :], wg2.ap())
        op("vector", "memset", T["EPSC"][:, 0:1], EPS_LN)
        op("vector", "memset", T["EPSC"][:, 1:2], EPS_RMS)
        op("vector", "memset", T["EPSC"][:, 2:3], float(1.0 / HD))
        op("vector", "memset", T["ONES"][:, :], 1.0)
        op("vector", "memset", T["S"][:, :], 0.0)
        op("vector", "memset", T["SB"][:, :], 0.0, inc=CDVE)

        FACCv = T["WQ"][:].bitcast(F32)
        FBUFv = [T["WK"][:].bitcast(F32), T["WV"][:].bitcast(F32)]
        FBCv = T["OQ"][:]
        add_done = [None] * 8
        for r in range(8):
            wt = (add_done[r - 2],) if r >= 2 else WAITS_NONE
            fdma = dma(
                "gpsimd",
                FBUFv[r % 2].rearrange("p (dc e) -> p dc e", dc=NDC),
                fac_sl[r].rearrange("(dc p) e -> p dc e", p=128),
                waits=wt)
            if r == 0:
                op("vector", "tensor_copy", FACCv, FBUFv[0], waits=(fdma,),
                   inc=CDVE)
            else:
                op("vector", "tensor_tensor", FACCv, FACCv, FBUFv[r % 2],
                   op=ALU.add, waits=(fdma,), inc=CDVE)
            add_done[r] = w(CDVE)
        op("vector", "tensor_scalar", out=FBCv, in0=FACCv,
           scalar1=float(1.0 / np.sqrt(8.0)), scalar2=None, op0=ALU.mult,
           inc=CDVE)
        fbc = w(CDVE)
        fsin_dma = dma("gpsimd",
                       cc_fs_in.ap().rearrange("(dc p) e -> p dc e", p=128),
                       FBCv.rearrange("p (dc e) -> p dc e", dc=NDC),
                       waits=(fbc,))
        op("gpsimd", "nop", waits=(fsin_dma,))
        if not SKIP_CC:
            op("gpsimd", "collective_compute", "AllGather", ALU.bypass,
               replica_groups=[list(range(8))],
               ins=[cc_fs_in.ap().opt()], outs=[cc_fs_out.ap().opt()],
               inc=CCP[0])
        else:
            op("gpsimd", "nop", inc=CCP[0])
        fsum_cc = (CCP[0], CCP[0].n)
        fence()

        ln_evac = {}
        ln_tp = {}
        ln_chunk_evac = {}
        g1_done = {}
        marks = {}
        s16_sent = [None] * NQ
        spill_o = {}
        spill_q = {}

        def ln_phase(tile):
            xch_free = {}
            for c in range(TPT):
                tc = tile * TPT + c
                slot = c % 2
                XCH = T["XCH"][slot]
                rw = list(xch_free.get(c - 2, ()))
                if tile == 1 and c < 2:
                    rw.append(marks.get(("lnfree", 0, TPT - 2 + c)))
                xdma = dma("sync", XCH[:, :], x_sl[tc * CH:(tc + 1) * CH, :],
                           waits=tuple(x for x in rw if x))
                if TRUNCATE == "ln0dma" and tile == 0 and c == 0:
                    return
                op("vector", "tensor_reduce", T["STATS"][:, c:c + 1], XCH[:, :],
                   AX.X, ALU.add, waits=(xdma,), inc=CDVE)
                sum_done = w(CDVE)
                if TRUNCATE == "ln0red" and tile == 0 and c == 0:
                    return
                op("scalar", "activation", out=T["TRASH"][:, :], in_=XCH[:, :],
                   func=AF.Square, waits=(xdma,), inc=CACT)
                sq_sq = w(CACT)
                op("vector", "tensor_reduce", T["STATS"][:, 8 + c:9 + c],
                   T["TRASH"][:, :], AX.X, ALU.add, waits=(sq_sq,), inc=CDVE)
                sq_done = w(CDVE)
                if TRUNCATE == "ln0sq" and tile == 0 and c == 0:
                    return
                mcol = T["STATS"][:, 16 + c:17 + c]
                vcol = T["STATS"][:, 24 + c:25 + c]
                rcol = T["STATS"][:, 32 + c:33 + c]
                bcol = T["STATS"][:, 40 + c:41 + c]
                tcol = T["STATS"][:, 48 + c:49 + c]
                scol = T["STATS"][:, 56 + c:57 + c]
                op("vector", "tensor_scalar", out=mcol, in0=T["STATS"][:, c:c + 1],
                   scalar1=RECIP_D, scalar2=None, op0=ALU.mult, waits=(sum_done,))
                op("vector", "tensor_scalar", out=vcol,
                   in0=T["STATS"][:, 8 + c:9 + c], scalar1=RECIP_D, scalar2=None,
                   op0=ALU.mult, waits=(sq_done,))
                op("vector", "tensor_tensor", tcol, mcol, mcol, op=ALU.mult)
                op("vector", "tensor_tensor", vcol, vcol, tcol,
                   op=ALU.subtract, inc=CDVE)
                op("scalar", "activation", out=scol, in_=vcol, func=AF.Sqrt,
                   bias=T["EPSC"][:, 0:1], waits=(w(CDVE),), inc=CACT)
                op("vector", "reciprocal", rcol, scol, waits=(w(CACT),))
                op("vector", "tensor_scalar", out=bcol, in0=mcol, scalar1=rcol,
                   scalar2=-1.0, op0=ALU.mult, op1=ALU.mult, inc=CDVE)
                rb = w(CDVE)
                XLNT = T["XLNT"][slot]
                nwt = [rb]
                pv = ln_tp.get((tile, c - 2)) or (
                    ln_tp.get((0, TPT - 2 + c)) if tile == 1 else None)
                if pv:
                    nwt.append(pv)
                op("scalar", "activation", out=XLNT[:, :], in_=XCH[:, :],
                   func=AF.Identity, scale=rcol, bias=bcol, waits=tuple(nwt),
                   inc=CACT)
                nrm = w(CACT)
                if TRUNCATE == "ln0nrm" and tile == 0 and c == 0:
                    return
                marks[("lnfree", tile, c)] = nrm
                xch_free[c] = (nrm, sq_done, sum_done)
                tw = [nrm]
                if (tile, c - 1) in ln_chunk_evac:
                    tw.append(ln_chunk_evac[(tile, c - 1)])
                elif tile == 1 and c == 0:
                    tw.append(ln_chunk_evac[(0, TPT - 1)])
                    tw.append(marks[("fme", "k", 0, NQ - 1)])
                for dc in range(NDC):
                    op("tensor", "transpose",
                       T["PT"][:, dc * 128:(dc + 1) * 128],
                       XLNT[:, dc * 128:(dc + 1) * 128], T["IDN"][:, :],
                       waits=tuple(tw) if dc == 0 else WAITS_NONE)
                inc_last("tensor", CPE)
                ln_tp[(tile, c)] = w(CPE)
                if TRUNCATE == "ln0tp" and tile == 0 and c == 0:
                    return
                if EVAC_MODE == 1:
                    dst = T["XLNFM"][:, c * D:(c + 1) * D]
                    src = T["PT"][:, :]
                elif EVAC_MODE == 2:
                    dst = T["XLNFM"][:, c * D:(c + 1) * D].rearrange(
                        "p (dc t) -> p dc t", dc=NDC)
                    src = T["PT"][:].rearrange("p (dc t) -> p dc t", dc=NDC)
                elif EVAC_MODE == 3:
                    dst = T["XLNFM"][:].rearrange("p (dc t) -> p dc t", dc=NDC)[
                        :, :, c * CH:(c + 1) * CH]
                    src = T["PT"][:, :].rearrange("p (t dc) -> p dc t", dc=NDC)
                else:
                    dst = T["XLNFM"][:].rearrange("p (dc t) -> p dc t", dc=NDC)[
                        :, :, c * CH:(c + 1) * CH]
                    src = T["PT"][:].rearrange("p (dc t) -> p dc t", dc=NDC)
                ecw = [ln_tp[(tile, c)]]
                if tile == 1 and c == 0:
                    ecw.append(marks[("allproj", 0)])
                op("vector", "tensor_copy", dst, src, waits=tuple(ecw),
                   inc=CDVE)
                ln_chunk_evac[(tile, c)] = w(CDVE)
                if TRUNCATE == f"ln{tile}c{c}":
                    return
            ln_evac[tile] = w(CDVE)
            if "d_stats3" in debug and tile == 0:
                dma("sync", dbg["d_stats3"].ap(), T["STATS"][:, :],
                    waits=(w(CACT), w(CDVE)))
            if "d_xlnfm" in debug and tile == 0:
                dma("sync", dbg["d_xlnfm"].ap(), T["XLNFM"][:, :],
                    waits=(ln_evac[tile],))
            g1w = [ln_evac[tile], consts_dma]
            if tile == 1:
                g1w.append(marks[("projevac", 0, NQ - 1)])
            for p in range(2):
                for dc in range(NDC):
                    op("tensor", "matmul", T["PP"][p][:, :],
                       T["WG1B"][:, dc * HD:(dc + 1) * HD],
                       T["XLNFM"][:, dc * 1024 + p * 512:dc * 1024 + (p + 1) * 512],
                       start=(dc == 0), stop=(dc == NDC - 1),
                       waits=tuple(g1w) if (dc == 0 and p == 0)
                       else WAITS_NONE)
                inc_last("tensor", CPE)
                op("vector", "tensor_copy",
                   T["G1"][:, tile * 1024 + p * 512:tile * 1024 + (p + 1) * 512],
                   T["PP"][p][:, :], waits=(w(CPE),), inc=CDVE)
            g1_done[tile] = w(CDVE)
            if "d_g1" in debug and tile == 1:
                dma("sync", dbg["d_g1"].ap(), T["G1"][:, :],
                    waits=(g1_done[tile],))

        sstate = {"lin": 0, "mask_m": None, "intra_m": [None, None],
                  "oevac": None, "dq": None, "sadd": None, "sb": None,
                  "fme_q": {}, "fme_k": {}, "vev": {}}
        pending_scan = [None]
        pstate = {}

        def scan_step(tile, g, c):
            st = sstate
            lin = st["lin"]
            st["lin"] += 1
            slot = lin % 2
            qfm3 = T["QFM"][:].rearrange("p (h t) -> p h t", h=QH)
            kfm3 = T["KFM"][:].rearrange("p (h t) -> p h t", h=QH)
            sw = [st["fme_q"].pop((g, c)), st["fme_k"].pop((g, c))]
            if st["mask_m"]:
                sw.append(st["mask_m"])
            for hl in range(QH):
                op("tensor", "matmul", T["PSC"][:, hl * 128:(hl + 1) * 128],
                   kfm3[:, hl, c * CH:(c + 1) * CH],
                   qfm3[:, hl, c * CH:(c + 1) * CH],
                   start=True, stop=True,
                   waits=tuple(sw) if hl == 0 else WAITS_NONE)
            inc_last("tensor", CPE)
            scd = w(CPE)
            mw = [scd]
            if st["intra_m"][slot]:
                mw.append(st["intra_m"][slot])
            op("vector", "tensor_tensor", T["SCT"][slot][:, :],
               T["PSC"][:, :], T["DMASK"][:, g * 512:(g + 1) * 512],
               op=ALU.mult, waits=tuple(mw), inc=CDVE)
            st["mask_m"] = w(CDVE)
            ow = [st["mask_m"], st["vev"].pop((g, c))]
            if st["oevac"]:
                ow.append(st["oevac"])
            for hl in range(QH):
                op("tensor", "matmul", T["POI"][:, hl * 128:(hl + 1) * 128],
                   T["SCT"][slot][:, hl * 128:(hl + 1) * 128],
                   T["V"][:, c * 512 + hl * 128:c * 512 + (hl + 1) * 128],
                   start=True, stop=True,
                   waits=tuple(ow) if hl == 0 else WAITS_NONE)
            iw = [x for x in (st["dq"], st["sb"]) if x]
            for hl in range(QH):
                h = g * QH + hl
                op("tensor", "matmul", T["POE"][:, hl * 128:(hl + 1) * 128],
                   qfm3[:, hl, c * CH:(c + 1) * CH],
                   T["SB"][:, h * HD:(h + 1) * HD],
                   start=True, stop=True,
                   waits=tuple(iw) if hl == 0 else WAITS_NONE)
            kw = [st["sadd"]] if st["sadd"] else []
            for hl in range(QH):
                op("tensor", "matmul", T["PKV"][:, hl * 128:(hl + 1) * 128],
                   T["K2"][:, c * 512 + hl * 128:c * 512 + (hl + 1) * 128],
                   T["V"][:, c * 512 + hl * 128:c * 512 + (hl + 1) * 128],
                   start=True, stop=True,
                   waits=tuple(kw) if hl == 0 else WAITS_NONE)
            inc_last("tensor", CPE)
            mm2 = w(CPE)
            st["intra_m"][slot] = mm2
            marks[("scanstep", tile, g, c)] = mm2
            for hl in range(QH):
                h = g * QH + hl
                op("scalar", "activation",
                   out=T["TMPOE"][:, hl * 128:(hl + 1) * 128],
                   in_=T["POE"][:, hl * 128:(hl + 1) * 128],
                   func=AF.Identity, scale=T["DQV"][:, h:h + 1],
                   waits=(mm2,) if hl == 0 else WAITS_NONE)
            st["dq"] = w(CACT)
            dw = [st["dq"]]
            if st["sadd"]:
                dw.append(st["sadd"])
            if st["sb"]:
                dw.append(st["sb"])
            if tile == 0 and g == 0 and c == 0:
                dw.append(consts_dma)
            for hl in range(QH):
                h = g * QH + hl
                op("scalar", "activation",
                   out=T["S"][:, h * HD:(h + 1) * HD],
                   in_=T["S"][:, h * HD:(h + 1) * HD], func=AF.Identity,
                   scale=T["LAMCT"][:, h:h + 1],
                   waits=tuple(dw) if hl == 0 else WAITS_NONE)
            decay_done = w(CACT)
            oqw = [st["dq"]]
            if c == 0:
                pq = (tile, g - 1) if g > 0 else (
                    (tile - 1, NQ - 1) if tile > 0 else None)
                pv = spill_o.get(pq) if pq else None
                if pv:
                    oqw.append(pv)
            op("vector", "tensor_tensor", T["OQ"][:, c * 512:(c + 1) * 512],
               T["POI"][:, :], T["TMPOE"][:, :], op=ALU.add,
               waits=tuple(oqw), inc=CDVE)
            st["oevac"] = w(CDVE)
            op("vector", "tensor_tensor", T["S"][:, g * 512:(g + 1) * 512],
               T["S"][:, g * 512:(g + 1) * 512], T["PKV"][:, :],
               op=ALU.add, waits=(decay_done, mm2), inc=CDVE)
            st["sadd"] = w(CDVE)
            op("vector", "tensor_copy", T["SB"][:, g * 512:(g + 1) * 512],
               T["S"][:, g * 512:(g + 1) * 512], inc=CDVE)
            st["sb"] = w(CDVE)

            if c == TPT - 1:
                marks[("scan", tile, g)] = mm2
                spill_o[(tile, g)] = dma(
                    "sync",
                    o_spill[tile * TPT:(tile + 1) * TPT, :,
                            g * 512:(g + 1) * 512].rearrange(
                        "tc t e -> t tc e"),
                    T["OQ"][:].rearrange("p (tc e) -> p tc e", tc=TPT),
                    waits=(st["oevac"],))
                spill_q[(tile, g)] = []
                for c_ in range(TPT):
                    spill_q[(tile, g)].append(dma(
                        "sync",
                        q_spill[tile * TPT + c_,
                                g * QH:(g + 1) * QH].rearrange(
                            "h hd t -> hd h t"),
                        T["QFM"][:].rearrange("p (h t) -> p h t", h=QH)[
                            :, :, c_ * CH:(c_ + 1) * CH],
                        waits=(mm2,) if c_ == 0 else WAITS_NONE))
                if tile == 1:
                    sdma = dma("gpsimd", cc_s_in[g].ap(),
                               T["S"][:, g * 512:(g + 1) * 512],
                               waits=(st["sadd"],))
                    op("gpsimd", "nop", waits=(sdma,))
                    if not SKIP_CC:
                        op("gpsimd", "collective_compute", "AllGather",
                           ALU.bypass,
                           replica_groups=[[0, 1], [2, 3], [4, 5], [6, 7]],
                           ins=[cc_s_in[g].ap().opt()],
                           outs=[cc_s_out[g].ap().opt()], inc=CCP[1 + g])
                    else:
                        op("gpsimd", "nop", inc=CCP[1 + g])
                    s16_sent[g] = (CCP[1 + g], CCP[1 + g].n)

        def quad_phase(tile, g):
            prev_quad = (tile, g - 1) if g > 0 else (
                (tile - 1, NQ - 1) if tile > 0 else None)

            wdmas = {}
            for wname, wt, pj in (("WQ", wq, "q"), ("WK", wk, "k"),
                                  ("WV", wv, "v")):
                pv = marks.get(("proj", pj) + prev_quad) if prev_quad else None
                wdmas[pj] = dma(
                    "gpsimd",
                    T[wname][:].rearrange("p (dc e) -> p dc e", dc=NDC),
                    wt[:, g * 512:(g + 1) * 512].rearrange(
                        "(dc p) e -> p dc e", p=128),
                    waits=(pv,) if pv else WAITS_NONE)

            ppfree = {}
            rotp = {}
            tpp = {}
            silup = {}
            mmk = {}

            def proj_chunk(which, wsb, c, wi):
                gi = c * 3 + wi
                slot = gi % 2
                pw = [ln_evac[tile], wdmas[which]]
                if ppfree.get(gi - 2):
                    pw.append(ppfree[gi - 2])
                elif gi < 2:
                    pw.append(g1_done[tile])
                    if prev_quad:
                        pv = marks.get(("projevac",) + prev_quad)
                        if pv:
                            pw.append(pv)
                for dc in range(NDC):
                    op("tensor", "matmul", T["PP"][slot][:, :],
                       T["XLNFM"][:, dc * 1024 + c * CH:dc * 1024 + (c + 1) * CH],
                       wsb[:, dc * 512:(dc + 1) * 512],
                       start=(dc == 0), stop=(dc == NDC - 1),
                       waits=tuple(pw) if dc == 0 else WAITS_NONE)
                inc_last("tensor", CPE)
                mmd = w(CPE)
                mmk[(which, c)] = mmd
                prevscan = marks.get(("scanstep",) + prev_quad + (c,)) \
                    if prev_quad else None

                if which == "v":
                    ew = [mmd]
                    if prevscan:
                        ew.append(prevscan)
                    op("scalar", "activation",
                       out=T["V"][:, c * 512:(c + 1) * 512],
                       in_=T["PP"][slot][:, :], func=AF.Identity,
                       waits=tuple(ew), inc=CACT)
                    ppfree[gi] = w(CACT)
                    sstate["vev"][(g, c)] = w(CACT)
                    return

                QSB = T["QSB"][0 if which == "q" else 1]
                QTM = T["QTM"][0 if which == "q" else 1]
                sw = [mmd]
                if rotp.get((which, c - 1)):
                    sw.append(rotp[(which, c - 1)])
                elif pstate.get(("rot", which)):
                    sw.append(pstate[("rot", which)])
                op("scalar", "activation", out=QSB[:, :],
                   in_=T["PP"][slot][:, :], func=AF.Silu, waits=tuple(sw),
                   inc=CACT)
                silup[(which, c)] = w(CACT)
                ppfree[gi] = w(CACT)
                tc = tile * TPT + c
                q3 = QSB[:].rearrange("p (h e) -> p h e", h=QH)
                o3 = QTM[:].rearrange("p (h e) -> p h e", h=QH)
                cosap = T["COS"][:, tc * 64:(tc + 1) * 64][:, None, :] \
                    .broadcast_to([128, QH, 64])
                sinap = T["SIN"][:, tc * 64:(tc + 1) * 64][:, None, :] \
                    .broadcast_to([128, QH, 64])
                t1 = T["TMP1"][:].rearrange("p (h e) -> p h e", h=QH)
                t2 = T["TMP2"][:].rearrange("p (h e) -> p h e", h=QH)
                rw = [silup[(which, c)]]
                if tpp.get((which, c - 1)):
                    rw.append(tpp[(which, c - 1)])
                elif pstate.get(("tp", which)):
                    rw.append(pstate[("tp", which)])
                op("vector", "tensor_tensor", t1, q3[:, :, 0:64], cosap,
                   op=ALU.mult, waits=tuple(rw))
                op("vector", "tensor_tensor", t2, q3[:, :, 64:128], sinap,
                   op=ALU.mult)
                op("vector", "tensor_tensor", o3[:, :, 0:64], t1, t2,
                   op=ALU.subtract)
                op("vector", "tensor_tensor", t1, q3[:, :, 0:64], sinap,
                   op=ALU.mult)
                op("vector", "tensor_tensor", t2, q3[:, :, 64:128], cosap,
                   op=ALU.mult)
                op("vector", "tensor_tensor", o3[:, :, 64:128], t1, t2,
                   op=ALU.add, inc=CDVE)
                rotp[(which, c)] = w(CDVE)
                if which == "k":
                    kw = [w(CDVE)]
                    if prevscan:
                        kw.append(prevscan)
                    op("vector", "tensor_tensor",
                       T["K2"][:, c * 512:(c + 1) * 512], QTM[:, :],
                       T["DKT"][:, g * 512:(g + 1) * 512], op=ALU.mult,
                       waits=tuple(kw), inc=CDVE)
                    rotp[(which, c)] = w(CDVE)

            def proj_transpose(which, c):
                QTM = T["QTM"][0 if which == "q" else 1]
                dst_fm = T["QFM"] if which == "q" else T["KFM"]
                off = 0 if which == "q" else 1024
                prevscan = marks.get(("scanstep",) + prev_quad + (c,)) \
                    if prev_quad else None
                tw = [rotp[(which, c)]]
                if tpp.get((which, c - 1)):
                    tw.append(marks.get(("fme", which, tile, g, c - 1)))
                elif marks.get(("lastPT", tile)):
                    tw.append(marks[("lastPT", tile)])
                for hl in range(QH):
                    op("tensor", "transpose",
                       T["PT"][:, off + hl * 128:off + (hl + 1) * 128],
                       QTM[:, hl * 128:(hl + 1) * 128], T["IDN"][:, :],
                       waits=tuple(tw) if hl == 0 else WAITS_NONE)
                inc_last("tensor", CPE)
                tpp[(which, c)] = w(CPE)
                ew = [tpp[(which, c)]]
                if prevscan:
                    ew.append(prevscan)
                pq = prev_quad
                if pq and spill_q.get(pq):
                    ew.append(spill_q[pq][c])
                dstap = dst_fm[:].rearrange("p (h t) -> p h t", h=QH)[
                    :, :, c * CH:(c + 1) * CH]
                srcap = T["PT"][:, off:off + 512].rearrange(
                    "p (h t) -> p h t", h=QH)
                op("vector", "tensor_copy", dstap, srcap, waits=tuple(ew),
                   inc=CDVE)
                fm = w(CDVE)
                marks[("fme", which, tile, g, c)] = fm
                if which == "q":
                    sstate["fme_q"][(g, c)] = fm
                else:
                    sstate["fme_k"][(g, c)] = fm

            for c in range(TPT):
                if c == 0 and pending_scan[0] is not None:
                    scan_step(*pending_scan[0])
                    pending_scan[0] = None
                proj_chunk("q", T["WQ"], c, 0)
                proj_chunk("k", T["WK"], c, 1)
                proj_chunk("v", T["WV"], c, 2)
                if c > 0:
                    scan_step(tile, g, c - 1)
                proj_transpose("q", c)
                proj_transpose("k", c)
            marks[("proj", "q", tile, g)] = mmk[("q", TPT - 1)]
            marks[("proj", "k", tile, g)] = mmk[("k", TPT - 1)]
            marks[("proj", "v", tile, g)] = mmk[("v", TPT - 1)]
            marks[("projevac", tile, g)] = w(CACT)
            marks[("fme", "k", tile, g)] = marks[("fme", "k", tile, g, TPT - 1)]
            marks[("lastPT", tile)] = marks[("fme", "k", tile, g, TPT - 1)]
            pstate[("rot", "q")] = rotp[("q", TPT - 1)]
            pstate[("rot", "k")] = rotp[("k", TPT - 1)]
            pstate[("tp", "q")] = tpp[("q", TPT - 1)]
            pstate[("tp", "k")] = tpp[("k", TPT - 1)]
            if g == NQ - 1:
                scan_step(tile, g, TPT - 1)
                pending_scan[0] = None
            else:
                pending_scan[0] = (tile, g, TPT - 1)

        def tail_phase(tile):
            S16ST = T[f"S16ST{tile}"]
            OCH = T[f"OCH{tile}"]
            QCH = T[f"QCH{tile}"]
            TMPC = T[f"TMPC{tile}"]
            GATS = T[f"GATS{tile}"]
            XG = T[f"XG{tile}"]
            SQB = T[f"SQB{tile}"]
            OGT = T[f"OGT{tile}"]
            PCORR = T[f"PCORR{tile}"]
            PGATE = T[f"PGATE{tile}"]
            PT2 = T[f"PT2{tile}"]

            s16marks = []
            for g in range(NQ):
                s16marks.append(dma(
                    "sync", S16ST[:, g * 512:(g + 1) * 512],
                    cc_s_out[g][0:HD, :], waits=(s16_sent[g],)))
            S16B = T[f"S16B{tile}"]
            op("vector", "tensor_copy", S16B[:, :], S16ST[:, :],
               waits=tuple(s16marks), inc=CDVE)
            s16b = w(CDVE)
            if "d_s16" in debug and tile == 0:
                dma("sync", dbg["d_s16"].ap(), S16ST[:, :], waits=(s16b,))

            sig_done = [None] * TPT
            corr_add = [None] * TPT
            qch_used = [None] * TPT
            prev_vadd = None
            for c in range(TPT):
                tc = tile * TPT + c
                slot = c % 2
                odma = dma("sync", OCH[:, c * D:(c + 1) * D], o_spill[tc],
                           waits=tuple(spill_o[(tile, gg)]
                                       for gg in range(NQ)) if c == 0
                           else WAITS_NONE)
                qw = []
                if c == 0:
                    for gg in range(NQ):
                        qw.extend(spill_q[(tile, gg)])
                if c >= 2 and qch_used[c - 2]:
                    qw.append(qch_used[c - 2])
                qdma = dma("sync",
                           QCH[slot][:].rearrange("p (h t) -> p h t", h=NH),
                           q_spill[tc].rearrange("h hd t -> hd h t"),
                           waits=tuple(qw))
                for g in range(NQ):
                    ps_ = PCORR[g % 2]
                    cw = [qdma, s16b]
                    if g >= 2:
                        cw.append(marks[("corrva", tile, c, g - 2)])
                    elif c >= 1:
                        cw.append(marks[("corrva", tile, c - 1, g + 2)])
                    for hl in range(QH):
                        h = g * QH + hl
                        op("tensor", "matmul", ps_[:, hl * 128:(hl + 1) * 128],
                           QCH[slot][:, h * 128:(h + 1) * 128],
                           S16B[:, h * HD:(h + 1) * HD],
                           start=True, stop=True,
                           waits=tuple(x for x in cw if x) if hl == 0
                           else WAITS_NONE)
                    inc_last("tensor", CPE)
                    cmm = w(CPE)
                    tm = TMPC[g % 2]
                    scw = [cmm]
                    if g >= 2:
                        scw.append(marks[("corrva", tile, c, g - 2)])
                    elif c >= 1:
                        scw.append(marks[("corrva", tile, c - 1, g + 2)])
                    for hl in range(QH):
                        h = g * QH + hl
                        op("scalar", "activation",
                           out=tm[:, hl * 128:(hl + 1) * 128],
                           in_=ps_[:, hl * 128:(hl + 1) * 128], func=AF.Identity,
                           scale=T["CORR"][:, tc * NH + h:tc * NH + h + 1],
                           waits=tuple(scw) if hl == 0 else WAITS_NONE)
                    inc_last("scalar", CACT)
                    csc = w(CACT)
                    op("vector", "tensor_tensor",
                       OCH[:, c * D + g * 512:c * D + (g + 1) * 512],
                       OCH[:, c * D + g * 512:c * D + (g + 1) * 512],
                       tm[:, :], op=ALU.add, waits=(csc, odma), inc=CDVE)
                    marks[("corrva", tile, c, g)] = w(CDVE)
                    prev_vadd = w(CDVE)
                corr_add[c] = w(CDVE)
                qch_used[c] = w(CPE)

                gw = [g1_done[tile], consts_dma]
                if c >= 1 and sig_done[c - 1]:
                    gw.append(sig_done[c - 1])
                for q4 in range(4):
                    op("tensor", "matmul", PGATE[:, q4 * 512:(q4 + 1) * 512],
                       T["G1"][:, tile * 1024 + c * CH:tile * 1024 + (c + 1) * CH],
                       T["WG2B"][:, q4 * 512:(q4 + 1) * 512],
                       start=True, stop=True,
                       waits=tuple(gw) if q4 == 0 else WAITS_NONE)
                inc_last("tensor", CPE)
                gmm = w(CPE)
                for q4 in range(4):
                    op("scalar", "activation",
                       out=GATS[:, c * D + q4 * 512:c * D + (q4 + 1) * 512],
                       in_=PGATE[:, q4 * 512:(q4 + 1) * 512], func=AF.Sigmoid,
                       waits=(gmm,) if q4 == 0 else WAITS_NONE)
                inc_last("scalar", CACT)
                sig_done[c] = w(CACT)
            if "d_gats" in debug and tile == 0:
                dma("sync", dbg["d_gats"].ap(), GATS[:, :],
                    waits=(w(CACT),))

            ogfm_e = [None] * TPT
            sq_marks = [None] * TPT
            for c in range(TPT):
                xg = XG[c % 2]
                xw = [corr_add[c], sig_done[c]]
                if c >= 2 and ogfm_e[c - 2]:
                    xw.append(ogfm_e[c - 2])
                if c >= 2 and sq_marks[c - 2]:
                    xw.append(sq_marks[c - 2])
                op("vector", "tensor_tensor", xg[:, :], OCH[:, c * D:(c + 1) * D],
                   GATS[:, c * D:(c + 1) * D], op=ALU.mult, waits=tuple(xw),
                   inc=CDVE)
                xg_done = w(CDVE)
                op("scalar", "activation", out=SQB[:, :], in_=xg[:, :],
                   func=AF.Square, waits=(xg_done,), inc=CACT)
                sq_marks[c] = w(CACT)
                op("vector", "tensor_reduce", T["MS"][:, :],
                   SQB[:].rearrange("p (h e) -> p h e", h=NH), AX.X, ALU.add,
                   waits=(w(CACT),), inc=CDVE)
                op("scalar", "activation", out=T["SD2"][:, :], in_=T["MS"][:, :],
                   func=AF.Sqrt, scale=T["EPSC"][:, 2:3],
                   bias=T["EPSC"][:, 1:2], waits=(w(CDVE),), inc=CACT)
                op("vector", "reciprocal", T["R2"][:, :], T["SD2"][:, :],
                   waits=(w(CACT),))
                inc_last("vector", CDVE)
                r2d = w(CDVE)
                for h in range(NH):
                    op("vector", "tensor_scalar",
                       out=OGT[:, h * HD:(h + 1) * HD],
                       in0=xg[:, h * HD:(h + 1) * HD],
                       scalar1=T["R2"][:, h:h + 1], scalar2=None, op0=ALU.mult,
                       waits=(r2d,) if h == 0 else WAITS_NONE)
                inc_last("vector", CDVE)
                ogd = w(CDVE)
                tw = [ogd]
                if c >= 1 and ogfm_e[c - 1]:
                    tw.append(ogfm_e[c - 1])
                if c == 0 and tile == 1:
                    tw.append(marks[("ogfmall", 0)])
                for dc in range(NDC):
                    op("tensor", "transpose", PT2[:, dc * 128:(dc + 1) * 128],
                       OGT[:, dc * 128:(dc + 1) * 128], T["IDN"][:, :],
                       waits=tuple(tw) if dc == 0 else WAITS_NONE)
                inc_last("tensor", CPE)
                tp2 = w(CPE)
                for dc in range(NDC):
                    op("vector", "tensor_scalar",
                       out=T["XLNFM"][:, dc * 1024 + c * CH:dc * 1024 + (c + 1) * CH],
                       in0=PT2[:, dc * 128:(dc + 1) * 128],
                       scalar1=T["NW"][:, dc:dc + 1], scalar2=None, op0=ALU.mult,
                       waits=(tp2,) if dc == 0 else WAITS_NONE)
                inc_last("vector", CDVE)
                ogfm_e[c] = w(CDVE)
            marks[("ogfmall", tile)] = w(CDVE)
            og_all = w(CDVE)
            if "d_ogfm" in debug and tile == 0:
                dma("sync", dbg["d_ogfm"].ap(), T["XLNFM"][:, :],
                    waits=(og_all,))

            fence()
            WOH = T[f"WOH{tile}"]
            XRCH = T[f"XRCH{tile}"]
            X2H = T[f"X2H{tile}"]
            TRASH2 = T[f"TRASH2{tile}"]
            PB = T[f"PB{tile}"]
            x2w = [[None] * TPT, [None] * TPT]
            x2free = {}
            pbe_all = []
            for H in range(2):
                wodma = dma("gpsimd",
                            WOH[:].rearrange("p (dc e) -> p dc e", dc=NDC),
                            wo[:, H * 1024:(H + 1) * 1024].rearrange(
                                "(dc p) e -> p dc e", p=128),
                            waits=(w(CPE),) if H == 1 else WAITS_NONE)
                pbe = [None] * TPT
                for c in range(TPT):
                    tc = tile * TPT + c
                    slot = c % 2
                    gi = H * TPT + c
                    rw = [pbe_all[gi - 2]] if gi >= 2 else []
                    xrdma = dma("sync", XRCH[slot][:, :],
                                x_sl[tc * CH:(tc + 1) * CH,
                                     H * 1024:(H + 1) * 1024],
                                waits=tuple(x for x in rw if x))
                    mw = [og_all, wodma]
                    if gi >= 2 and pbe_all[gi - 2]:
                        mw.append(pbe_all[gi - 2])
                    for dc in range(NDC):
                        for n in range(2):
                            op("tensor", "matmul",
                               PB[slot][:, n * 512:(n + 1) * 512],
                               T["XLNFM"][:, dc * 1024 + c * CH:dc * 1024 + (c + 1) * CH],
                               WOH[:, dc * 1024 + n * 512:dc * 1024 + (n + 1) * 512],
                               start=(dc == 0), stop=(dc == NDC - 1),
                               waits=tuple(mw) if (dc == 0 and n == 0)
                               else WAITS_NONE)
                    inc_last("tensor", CPE)
                    bmm = w(CPE)
                    ew = [bmm, xrdma]
                    ew += list(x2free.get(gi - 2, ()))
                    op("vector", "tensor_tensor", X2H[slot][:, :],
                       PB[slot][:, :], XRCH[slot][:, :],
                       op=ALU.add, waits=tuple(ew), inc=CDVE)
                    ev = w(CDVE)
                    pbe[c] = ev
                    pbe_all.append(ev)
                    op("vector", "tensor_reduce",
                       T["S2S"][:, c * 2 + H:c * 2 + H + 1], X2H[slot][:, :],
                       AX.X, ALU.add, inc=CDVE)
                    red = w(CDVE)
                    op("scalar", "activation", out=TRASH2[:, :],
                       in_=X2H[slot][:, :], func=AF.Square,
                       accum_out=T["Q2S"][:, c * 2 + H:c * 2 + H + 1],
                       waits=(ev,))
                    op("scalar", "activation", out=T["DUMT"][:, :],
                       in_=T["EPSC"][:, 0:1], func=AF.Identity, inc=CACT)
                    sqd = w(CACT)
                    x2w[H][c] = dma(
                        "gpsimd",
                        out_sl[tc * CH:(tc + 1) * CH,
                               H * 1024:(H + 1) * 1024],
                        X2H[slot][:, :], waits=(ev,))
                    x2free[gi] = (x2w[H][c], sqd, red)
            bstats = (w(CDVE), w(CACT))

            fence()
            XB2 = T[f"XB2{tile}"]
            XLNT2 = T[f"XLNT2{tile}"]
            PT3 = T[f"PT3{tile}"]
            xl2e = [None] * TPT
            nrm_done = [None] * TPT
            if "d_stats" in debug and tile == 0:
                dma("sync", dbg["d_stats"][:, 0:16], T["S2S"][:, :],
                    waits=bstats)
                dma("sync", dbg["d_stats"][:, 16:32], T["Q2S"][:, :],
                    waits=bstats)
            for c in range(TPT):
                tc = tile * TPT + c
                slot = c % 2
                rw = [nrm_done[c - 2]] if c >= 2 else []
                rw += [x2w[0][c], x2w[1][c]]
                xbdma = dma("sync", XB2[slot][:, :],
                            out_sl[tc * CH:(tc + 1) * CH, :],
                            waits=tuple(x for x in rw if x))
                if "d_x2" in debug and tile == 0:
                    dma("sync", dbg["d_x2"][:, c * D:(c + 1) * D],
                        XB2[slot][:, :], waits=(xbdma,))
                mcol = T["STATS"][:, c:c + 1]
                vcol = T["STATS"][:, 8 + c:9 + c]
                rcol = T["STATS"][:, 16 + c:17 + c]
                bcol = T["STATS"][:, 24 + c:25 + c]
                tcol = T["STATS"][:, 32 + c:33 + c]
                scol = T["STATS"][:, 40 + c:41 + c]
                op("vector", "tensor_tensor", mcol,
                   T["S2S"][:, c * 2:c * 2 + 1],
                   T["S2S"][:, c * 2 + 1:c * 2 + 2], op=ALU.add,
                   waits=bstats if c == 0 else WAITS_NONE)
                op("vector", "tensor_scalar", out=mcol, in0=mcol,
                   scalar1=RECIP_D, scalar2=None, op0=ALU.mult)
                op("vector", "tensor_tensor", vcol,
                   T["Q2S"][:, c * 2:c * 2 + 1],
                   T["Q2S"][:, c * 2 + 1:c * 2 + 2], op=ALU.add)
                op("vector", "tensor_scalar", out=vcol, in0=vcol,
                   scalar1=RECIP_D, scalar2=None, op0=ALU.mult)
                op("vector", "tensor_tensor", tcol, mcol, mcol, op=ALU.mult)
                op("vector", "tensor_tensor", vcol, vcol, tcol,
                   op=ALU.subtract, inc=CDVE)
                op("scalar", "activation", out=scol, in_=vcol, func=AF.Sqrt,
                   bias=T["EPSC"][:, 0:1], waits=(w(CDVE),), inc=CACT)
                op("vector", "reciprocal", rcol, scol, waits=(w(CACT),))
                op("vector", "tensor_scalar", out=bcol, in0=mcol, scalar1=rcol,
                   scalar2=-1.0, op0=ALU.mult, op1=ALU.mult, inc=CDVE)
                nwt = [w(CDVE), xbdma]
                if c >= 2 and nrm_done[c - 2]:
                    nwt.append(nrm_done[c - 2])
                op("scalar", "activation", out=XLNT2[slot][:, :],
                   in_=XB2[slot][:, :], func=AF.Identity, scale=rcol, bias=bcol,
                   waits=tuple(nwt), inc=CACT)
                nrm_done[c] = w(CACT)
                tw = [nrm_done[c]]
                if c >= 1 and xl2e[c - 1]:
                    tw.append(xl2e[c - 1])
                for dc in range(NDC):
                    op("tensor", "transpose", PT3[:, dc * 128:(dc + 1) * 128],
                       XLNT2[slot][:, dc * 128:(dc + 1) * 128], T["IDN"][:, :],
                       waits=tuple(tw) if dc == 0 else WAITS_NONE)
                inc_last("tensor", CPE)
                tpd = w(CPE)
                dst = T[f"XLN2FM{tile}"][:].rearrange("p (dc t) -> p dc t", dc=NDC)[
                    :, :, c * CH:(c + 1) * CH]
                src = PT3[:].rearrange("p (dc t) -> p dc t", dc=NDC)
                op("vector", "tensor_copy", dst, src, waits=(tpd,), inc=CDVE)
                xl2e[c] = w(CDVE)
            xl2all = w(CDVE)
            if "d_stats2" in debug and tile == 0:
                dma("sync", dbg["d_stats2"].ap(), T["STATS"][:, :],
                    waits=(w(CACT), w(CDVE)))
            if "d_xln2fm" in debug and tile == 0:
                dma("sync", dbg["d_xln2fm"].ap(), T[f"XLN2FM{tile}"][:, :],
                    waits=(xl2all,))

            FSP = T[f"FSP{tile}"]
            OUTES = T[f"OUTES{tile}"]
            PF = T[f"PF{tile}"]
            fsp_used = [None] * 4
            prev_pf_evac = None
            prev_outes = [None, None]
            for es in range(4):
                slot = es % 2
                fw = [fsum_cc]
                if es >= 2 and fsp_used[es - 2]:
                    fw.append(fsp_used[es - 2])
                fsdma = []
                for rr in range(2):
                    dstp = FSP[slot][:].rearrange("p (dc e) -> p dc e", dc=NDC)[
                        :, :, rr * 256:(rr + 1) * 256]
                    fsdma.append(dma(
                        "sync", dstp,
                        cc_fs_out[(2 * es + rr) * D:(2 * es + rr + 1) * D]
                        .rearrange("(dc p) e -> p dc e", p=128),
                        waits=tuple(fw) if rr == 0 else WAITS_NONE))
                for half in range(2):
                    hw_ = [xl2all] + fsdma
                    if prev_pf_evac:
                        hw_.append(prev_pf_evac)
                    for tcq in range(4):
                        c = half * 4 + tcq
                        op("tensor", "matmul", PF[tcq][:, :], T["ONES"][:, :],
                           T["BIASB"][:, es * 512:(es + 1) * 512],
                           start=True, stop=False,
                           waits=tuple(x for x in hw_ if x) if tcq == 0
                           else WAITS_NONE)
                        for dc in range(NDC):
                            op("tensor", "matmul", PF[tcq][:, :],
                               T[f"XLN2FM{tile}"][:, dc * 1024 + c * CH:dc * 1024 + (c + 1) * CH],
                               FSP[slot][:, dc * 512:(dc + 1) * 512],
                               start=False, stop=(dc == NDC - 1))
                    inc_last("tensor", CPE)
                    fmm = w(CPE)
                    evw = [fmm]
                    if prev_outes[half]:
                        evw.append(prev_outes[half])
                    for tcq in range(4):
                        op("vector", "tensor_copy",
                           OUTES[half][:, tcq * 512:(tcq + 1) * 512],
                           PF[tcq][:, :], waits=tuple(evw) if tcq == 0
                           else WAITS_NONE)
                    inc_last("vector", CDVE)
                    evd = w(CDVE)
                    prev_pf_evac = evd
                    prev_outes[half] = dma("gpsimd",
                        out_sl[tile * 1024 + half * 512:
                               tile * 1024 + (half + 1) * 512,
                               es * 512:(es + 1) * 512].rearrange(
                            "(tc t) e -> t tc e", t=CH),
                        OUTES[half][:].rearrange("p (tc e) -> p tc e", tc=4),
                        waits=(evd,), accum_op=ALU.add)
                    _ = prev_outes
                fsp_used[es] = w(CPE)

        def trunc(name):
            return TRUNCATE == name

        while True:
            if trunc("prelude"):
                break
            ln_phase(0)
            if trunc("ln0") or (TRUNCATE or "").startswith("ln0c") \
                    or (TRUNCATE or "").startswith("ln0d") \
                    or (TRUNCATE or "").startswith("ln0r") \
                    or (TRUNCATE or "").startswith("ln0s") \
                    or (TRUNCATE or "").startswith("ln0n") \
                    or TRUNCATE == "ln0tp":
                break
            for g in range(NQ):
                quad_phase(0, g)
            marks[("allproj", 0)] = marks[("proj", "v", 0, NQ - 1)]
            if trunc("quads0"):
                break
            ln_phase(1)
            if trunc("ln1"):
                break
            for g in range(NQ):
                quad_phase(1, g)
            if trunc("quads1"):
                break
            fence()
            tail_phase(0)
            if trunc("tail0"):
                break
            fence()
            tail_phase(1)
            break

        for s in GPP:
            op("gpsimd", "nop", waits=((s, s.n),))
        for s in SXP:
            op("sync", "nop", waits=((s, s.n),))

    with nc.Block() as block:
        block.sync(lambda e: walk("sync", e))
        block.scalar(lambda e: walk("scalar", e))
        block.vector(lambda e: walk("vector", e))
        block.tensor(lambda e: walk("tensor", e))
        block.gpsimd(lambda e: walk("gpsimd", e))

    top.close()
    return nc


def _host_tables(core):
    half = core % 2
    p0 = half * NT
    ll = _LL.astype(np.float32)
    theta = (np.float32(10000.0) ** (np.float32(-2.0)
             * np.arange(64, dtype=np.float32) / np.float32(HD))).astype(np.float32)
    pos = np.arange(p0, p0 + NT, dtype=np.float32)
    ang = (pos[:, None] * theta[None, :]).astype(np.float32)
    cos_t = np.cos(ang.astype(np.float64)).astype(np.float32)
    sin_t = np.sin(ang.astype(np.float64)).astype(np.float32)
    i = np.arange(CH, dtype=np.float64)
    diff = i[None, :] - i[:, None]
    dm = np.where(diff >= 0, np.exp(ll[:, None, None].astype(np.float64)
                                    * diff[None]), 0.0)
    dmask = np.ascontiguousarray(
        dm.transpose(1, 0, 2).reshape(CH, NH * CH)).astype(np.float32)
    dk = np.exp(ll[None, :].astype(np.float64)
                * (CH - 1.0 - i[:, None])).astype(np.float32)
    dkt = np.repeat(dk[:, :, None], CH, axis=2).reshape(CH, NH * CH)
    dkt = np.ascontiguousarray(dkt).astype(np.float32)
    dqv = np.exp(ll[None, :].astype(np.float64) * (i[:, None] + 1.0)
                 ).astype(np.float32)
    oddf = 1.0 if (core % 2 == 1) else 0.0
    lamc = np.exp(ll.astype(np.float64) * CH)
    corr = np.zeros((CH, TCC * NH), np.float64)
    for c in range(TCC):
        corr[:, c * NH:(c + 1) * NH] = dqv.astype(np.float64) * (lamc ** c)[None, :]
    corr = (corr * oddf).astype(np.float32)
    lamct = np.tile(lamc.astype(np.float32)[None, :], (CH, 1))
    return cos_t, sin_t, dmask, dkt, dqv, corr, lamct


_NC_CACHE = {}


def _get_nc(debug=()):
    key = tuple(sorted(debug))
    if key not in _NC_CACHE:
        _NC_CACHE[key] = build_program(debug=key)
    return _NC_CACHE[key]


def kernel(L0_attn_x, L0_attn_Wq, L0_attn_Wk, L0_attn_Wv, L0_attn_Wo,
           L0_attn_Wg1, L0_attn_Wg2, L0_attn_norm_weight, L0_ffn_factors,
           L0_ffn_bias, _debug=(), _nc=None):
    x = np.ascontiguousarray(np.asarray(L0_attn_x, np.float32))
    weights = {
        "wq": np.ascontiguousarray(np.asarray(L0_attn_Wq, np.float32)),
        "wk": np.ascontiguousarray(np.asarray(L0_attn_Wk, np.float32)),
        "wv": np.ascontiguousarray(np.asarray(L0_attn_Wv, np.float32)),
        "wo": np.ascontiguousarray(np.asarray(L0_attn_Wo, np.float32)),
        "wg1": np.ascontiguousarray(np.asarray(L0_attn_Wg1, np.float32)),
        "wg2": np.ascontiguousarray(np.asarray(L0_attn_Wg2, np.float32)),
        "nw_in": np.ascontiguousarray(np.asarray(L0_attn_norm_weight, np.float32)),
        "bias_in": np.ascontiguousarray(
            np.asarray(L0_ffn_bias, np.float32).reshape(1, D)),
    }
    factors = np.asarray(L0_ffn_factors, np.float32)
    iden = np.eye(128, dtype=np.float32)

    nc = _nc if _nc is not None else _get_nc(_debug)
    in_maps = []
    for core in range(8):
        b, half = core // 2, core % 2
        cos_t, sin_t, dmask, dkt, dqv, corr, lamct = _host_tables(core)
        m = dict(weights)
        m["x_sl"] = np.ascontiguousarray(x[b, half * NT:(half + 1) * NT, :])
        m["fac_sl"] = np.ascontiguousarray(
            factors[:, :, core * 256:(core + 1) * 256])
        m["cos_in"] = cos_t
        m["sin_in"] = sin_t
        m["dmask_in"] = dmask
        m["dkt_in"] = dkt
        m["dqv_in"] = dqv
        m["corr_in"] = corr
        m["lamc_in"] = lamct
        m["idn_in"] = iden
        in_maps.append(m)

    res = run_bass_kernel_spmd(nc, in_maps, core_ids=list(range(8)))
    out = np.empty((4, 4096, D), np.float32)
    for core in range(8):
        b, half = core // 2, core % 2
        out[b, half * NT:(half + 1) * NT, :] = res.results[core]["out_sl"]
    if _debug:
        return out, res
    return out


# version 17 (final)
# speedup vs baseline: 1.1086x; 1.0224x over previous
import numpy as np
from contextlib import ExitStack

import concourse.bass as bass
import concourse.mybir as mybir
from concourse.bass_utils import run_bass_kernel_spmd

F32 = mybir.dt.float32
BF16 = mybir.dt.bfloat16
AF = mybir.ActivationFunctionType
ALU = mybir.AluOpType
AX = mybir.AxisListType

D = 2048
NT = 2048
CH = 128
TCC = 16
TPT = 8
NH = 16
HD = 128
QH = 4
NQ = 4
NDC = 16
EPS_LN = 1e-5
EPS_RMS = 1e-6
RECIP_D = 1.0 / 2048.0

ENGINES = ("sync", "scalar", "vector", "tensor", "gpsimd")
RACE_DETECT = False
PE_DENSE_INC = False
TRUNCATE = None
EVAC_MODE = 0
SKIP_CC = False

_LL = -(2.0 ** (-8.0 * (np.arange(NH) + 1.0) / NH))
_LAMC = np.exp(_LL * CH)


class Sem:
    __slots__ = ("h", "n")

    def __init__(self, h):
        self.h = h
        self.n = 0


def build_program(debug=()):
    nc = bass.Bass("TRN2", target_bir_lowering=False, debug=False,
                   detect_race_conditions=RACE_DETECT)

    x_sl = nc.dram_tensor("x_sl", [NT, D], F32, kind="ExternalInput")
    wq = nc.dram_tensor("wq", [D, D], F32, kind="ExternalInput")
    wk = nc.dram_tensor("wk", [D, D], F32, kind="ExternalInput")
    wv = nc.dram_tensor("wv", [D, D], F32, kind="ExternalInput")
    wo = nc.dram_tensor("wo", [D, D], F32, kind="ExternalInput")
    wg1 = nc.dram_tensor("wg1", [D, HD], F32, kind="ExternalInput")
    wg2 = nc.dram_tensor("wg2", [HD, D], F32, kind="ExternalInput")
    nw_in = nc.dram_tensor("nw_in", [D], F32, kind="ExternalInput")
    bias_in = nc.dram_tensor("bias_in", [1, D], F32, kind="ExternalInput")
    fac_sl = nc.dram_tensor("fac_sl", [8, D, 256], F32, kind="ExternalInput")
    cos_in = nc.dram_tensor("cos_in", [NT, 64], F32, kind="ExternalInput")
    sin_in = nc.dram_tensor("sin_in", [NT, 64], F32, kind="ExternalInput")
    dmask_in = nc.dram_tensor("dmask_in", [CH, D], F32, kind="ExternalInput")
    dkt_in = nc.dram_tensor("dkt_in", [CH, D], F32, kind="ExternalInput")
    dqv_in = nc.dram_tensor("dqv_in", [CH, NH], F32, kind="ExternalInput")
    corr_in = nc.dram_tensor("corr_in", [CH, TCC * NH], F32, kind="ExternalInput")
    idn_in = nc.dram_tensor("idn_in", [128, 128], F32, kind="ExternalInput")
    lamc_in = nc.dram_tensor("lamc_in", [CH, NH], F32, kind="ExternalInput")
    out_sl = nc.dram_tensor("out_sl", [NT, D], F32, kind="ExternalOutput")

    def dscratch(name, shape, dtype):
        kind = "ExternalOutput" if name in debug else "Internal"
        return nc.dram_tensor(name, shape, dtype, kind=kind)

    o_spill = dscratch("o_spill", [TCC, CH, D], BF16)
    q_spill = dscratch("q_spill", [TCC, NH, HD, CH], BF16)
    cc_fs_in = nc.dram_tensor("cc_fs_in", [D, 256], BF16)
    cc_fs_out = nc.dram_tensor("cc_fs_out", [8 * D, 256], BF16)
    cc_s_in = [nc.dram_tensor(f"cc_s_in{g}", [HD, QH * HD], F32) for g in range(NQ)]
    cc_s_out = [nc.dram_tensor(f"cc_s_out{g}", [2 * HD, QH * HD], F32)
                for g in range(NQ)]
    dbg = {}
    for name, shape, dt in (
        ("d_xlnfm", [128, NDC * 1024], BF16),
        ("d_g1", [128, NT], BF16),
        ("d_s16", [128, NH * HD], F32),
        ("d_gats", [128, TPT * D], BF16),
        ("d_ogfm", [128, NDC * 1024], BF16),
        ("d_xln2fm", [128, NDC * 1024], BF16),
        ("d_x2", [128, TPT * D], F32),
        ("d_stats", [128, 64], F32),
        ("d_stats2", [128, 64], F32),
        ("d_stats3", [128, 64], F32),
    ):
        if name in debug:
            dbg[name] = nc.dram_tensor(name, shape, dt, kind="ExternalOutput")

    top = ExitStack()

    def mk(stk, name, shape, dtype):
        return stk.enter_context(nc.sbuf_tensor(name, shape, dtype))

    def mkp(stk, name, shape, dtype):
        return stk.enter_context(nc.psum_tensor(name, shape, dtype))

    T = {}
    T["XLNFM"] = mk(top, "XLNFM", [128, NDC * 1024], BF16)
    T["G1"] = mk(top, "G1", [128, NT], BF16)
    T["WG2B"] = mk(top, "WG2B", [HD, D], BF16)
    T["WG1B"] = mk(top, "WG1B", [128, NDC * HD], BF16)
    T["NW"] = mk(top, "NW", [128, 16], F32)
    T["BIASB"] = mk(top, "BIASB", [1, D], BF16)
    T["ONES"] = mk(top, "ONES", [1, 128], BF16)
    T["IDN"] = mk(top, "IDN", [128, 128], BF16)
    T["DQV"] = mk(top, "DQV", [128, NH], F32)
    T["CORR"] = mk(top, "CORR", [128, TCC * NH], F32)
    T["STATS"] = mk(top, "STATS", [128, 64], F32)
    T["S2S"] = mk(top, "S2S", [128, 16], F32)
    T["Q2S"] = mk(top, "Q2S", [128, 16], F32)
    T["MS"] = mk(top, "MS", [128, 16], F32)
    T["SD2"] = mk(top, "SD2", [128, 16], F32)
    T["R2"] = mk(top, "R2", [128, 16], F32)
    T["EPSC"] = mk(top, "EPSC", [128, 4], F32)
    T["DUMT"] = mk(top, "DUMT", [128, 1], F32)
    T["LAMCT"] = mk(top, "LAMCT", [128, NH], F32)

    qs = ExitStack()
    T["COS"] = mk(qs, "COS", [128, TCC * 64], BF16)
    T["SIN"] = mk(qs, "SIN", [128, TCC * 64], BF16)
    T["DMASK"] = mk(qs, "DMASK", [CH, D], BF16)
    T["DKT"] = mk(qs, "DKT", [CH, D], BF16)
    T["S"] = mk(qs, "S", [128, NH * HD], F32)
    T["SB"] = mk(qs, "SB", [128, NH * HD], BF16)
    T["XCH"] = [mk(qs, f"XCH{i}", [128, D], F32) for i in range(2)]
    T["XLNT"] = [mk(qs, f"XLNT{i}", [128, D], BF16) for i in range(2)]
    T["TRASH"] = mk(qs, "TRASH", [128, D], BF16)
    T["WQ"] = mk(qs, "WQ", [128, NDC * 512], BF16)
    T["WK"] = mk(qs, "WK", [128, NDC * 512], BF16)
    T["WV"] = mk(qs, "WV", [128, NDC * 512], BF16)
    T["QSB"] = [mk(qs, f"QSB{i}", [128, 512], BF16) for i in range(2)]
    T["QTM"] = [mk(qs, f"QTM{i}", [128, 512], BF16) for i in range(2)]
    T["TMP1"] = mk(qs, "TMP1", [128, QH * 64], F32)
    T["TMP2"] = mk(qs, "TMP2", [128, QH * 64], F32)
    T["QFM"] = mk(qs, "QFM", [128, QH * 1024], BF16)
    T["KFM"] = mk(qs, "KFM", [128, QH * 1024], BF16)
    T["K2"] = mk(qs, "K2", [128, TPT * 512], BF16)
    T["V"] = mk(qs, "V", [128, TPT * 512], BF16)
    T["SCT"] = [mk(qs, f"SCT{i}", [128, 512], BF16) for i in range(2)]
    T["TMPOE"] = mk(qs, "TMPOE", [128, 512], F32)
    T["OQ"] = mk(qs, "OQ", [128, TPT * 512], BF16)
    qps = ExitStack()
    T["PP"] = [mkp(qps, f"PP{i}", [128, 512], F32) for i in range(2)]
    T["PT"] = mkp(qps, "PT", [128, D], BF16)
    T["PSC"] = mkp(qps, "PSC", [128, 512], F32)
    T["POI"] = mkp(qps, "POI", [128, 512], F32)
    T["POE"] = mkp(qps, "POE", [128, 512], F32)
    T["PKV"] = mkp(qps, "PKV", [128, 512], F32)
    qps.close()
    qs.close()

    for tile in range(2):
        ts = ExitStack()
        T[f"S16ST{tile}"] = mk(ts, f"S16ST{tile}", [128, NH * HD], F32)
        T[f"S16B{tile}"] = mk(ts, f"S16B{tile}", [128, NH * HD], BF16)
        T[f"OCH{tile}"] = mk(ts, f"OCH{tile}", [128, TPT * D], BF16)
        T[f"QCH{tile}"] = [mk(ts, f"QCH{tile}_{i}", [128, NH * CH], BF16)
                           for i in range(2)]
        T[f"TMPC{tile}"] = [mk(ts, f"TMPC{tile}_{i}", [128, 512], F32)
                            for i in range(2)]
        T[f"GATS{tile}"] = mk(ts, f"GATS{tile}", [128, TPT * D], BF16)
        T[f"XG{tile}"] = [mk(ts, f"XG{tile}_{i}", [128, D], F32)
                          for i in range(2)]
        T[f"SQB{tile}"] = mk(ts, f"SQB{tile}", [128, D], F32)
        T[f"OGT{tile}"] = mk(ts, f"OGT{tile}", [128, D], BF16)
        aps = ExitStack()
        T[f"PCORR{tile}"] = [mkp(aps, f"PCORR{tile}_{i}", [128, 512], F32)
                             for i in range(2)]
        T[f"PGATE{tile}"] = mkp(aps, f"PGATE{tile}", [128, D], F32)
        T[f"PT2{tile}"] = mkp(aps, f"PT2{tile}", [128, D], BF16)
        aps.close()
        ts.close()
        bs = ExitStack()
        T[f"XLN2FM{tile}"] = mk(bs, f"XLN2FM{tile}", [128, NDC * 1024], BF16)
        T[f"WOH{tile}"] = mk(bs, f"WOH{tile}", [128, NDC * 1024], BF16)
        T[f"XRCH{tile}"] = [mk(bs, f"XRCH{tile}_{i}", [128, 1024], F32)
                            for i in range(2)]
        T[f"X2H{tile}"] = [mk(bs, f"X2H{tile}_{i}", [128, 1024], F32)
                           for i in range(2)]
        T[f"TRASH2{tile}"] = mk(bs, f"TRASH2{tile}", [128, 1024], F32)
        bps = ExitStack()
        T[f"PB{tile}"] = [mkp(bps, f"PB{tile}_{i}", [128, 1024], F32)
                          for i in range(2)]
        bps.close()
        cs = ExitStack()
        T[f"XB2{tile}"] = [mk(cs, f"XB2{tile}_{i}", [128, D], F32)
                           for i in range(2)]
        T[f"XLNT2{tile}"] = [mk(cs, f"XLNT2{tile}_{i}", [128, D], BF16)
                             for i in range(2)]
        T[f"FSP{tile}"] = [mk(cs, f"FSP{tile}_{i}", [128, NDC * 512], BF16)
                           for i in range(2)]
        T[f"OUTES{tile}"] = [mk(cs, f"OUTES{tile}_{i}", [128, 4 * 512], F32)
                             for i in range(2)]
        cps = ExitStack()
        T[f"PT3{tile}"] = mkp(cps, f"PT3{tile}", [128, D], BF16)
        T[f"PF{tile}"] = [mkp(cps, f"PF{tile}_{i}", [128, 512], F32)
                          for i in range(6)]
        cps.close()
        cs.close()
        bs.close()

    NSX, NGP = 8, 8
    SXP = [Sem(top.enter_context(nc.semaphore(f"dsx{i}"))) for i in range(NSX)]
    GPP = [Sem(top.enter_context(nc.semaphore(f"dgp{i}"))) for i in range(NGP)]
    CCP = [Sem(top.enter_context(nc.semaphore(f"dcc{i}"))) for i in range(5)]
    CPE = Sem(top.enter_context(nc.semaphore("cpe")))

    class ChainSem:
        W = 4

        def __init__(self, name):
            self.sems = [Sem(top.enter_context(nc.semaphore(f"{name}{i}")))
                         for i in range(self.W)]
            self.n = 0

        def mark(self):
            if self.n == 0:
                return (self.sems[0], 0)
            i = self.n - 1
            return (self.sems[i % self.W], i // self.W + 1)

    CACT = ChainSem("cact")
    CDVE = ChainSem("cdve")
    ALLSEMS = tuple(SXP) + tuple(GPP) + tuple(CCP) + (CPE,)         + tuple(CACT.sems) + tuple(CDVE.sems)

    def walk(eng_name, E):
        for s in ALLSEMS:
            s.n = 0
        CACT.n = 0
        CDVE.n = 0

        SELF = {"vector": CDVE, "scalar": CACT}
        last_inst = {e: None for e in ENGINES}
        last_mark = {e: None for e in ENGINES}

        def op(engine, method, *a, inc=None, incn=1, waits=(), **k):
            if PE_DENSE_INC and engine == "tensor":
                assert inc is None or inc is CPE
                CPE.n += 1
                if engine == eng_name:
                    for s_, v_ in waits:
                        if s_ is not None:
                            E.wait_ge(s_.h, v_)
                    instr = getattr(E, method)(*a, **k)
                    instr.then_inc(CPE.h, 1)
                    last_inst[engine] = instr
                    last_mark[engine] = (CPE, CPE.n)
                return
            selfsem = SELF.get(engine)
            if selfsem is not None:
                assert inc is None or inc is selfsem
                prevmark = selfsem.mark()
                selfsem.n += 1
                mysem, myval = selfsem.mark()
                if engine == eng_name:
                    for s_, v_ in waits:
                        if s_ is not None:
                            E.wait_ge(s_.h, v_)
                    E.wait_ge(prevmark[0].h, prevmark[1])
                    instr = getattr(E, method)(*a, **k)
                    instr.then_inc(mysem.h, 1)
                    last_inst[engine] = instr
                return
            if inc is not None:
                inc.n += incn
            if engine == eng_name:
                for s_, v_ in waits:
                    if s_ is not None:
                        E.wait_ge(s_.h, v_)
                instr = getattr(E, method)(*a, **k)
                if instr is not None:
                    last_inst[engine] = instr
                    last_mark[engine] = None
                if inc is not None:
                    instr.then_inc(inc.h, incn)
                    last_mark[engine] = (inc, inc.n)

        def inc_last(engine, sem, n=1):
            if SELF.get(engine) is not None:
                assert sem is SELF[engine]
                return
            if PE_DENSE_INC and engine == "tensor":
                assert sem is CPE
                return
            prev = last_mark[engine]
            sem.n += n
            if engine == eng_name:
                if last_inst[engine] is None or prev is not None:
                    if prev is not None:
                        E.wait_ge(prev[0].h, prev[1])
                    E.nop().then_inc(sem.h, n)
                else:
                    last_inst[engine].then_inc(sem.h, n)
            last_mark[engine] = (sem, sem.n)

        qstate = {"sync": [SXP, 0], "gpsimd": [GPP, 0], "scalar": [SXP, 0]}

        def dma(engine, out_ap, in_ap, dsem=None, waits=(), **k):
            pool, idx = qstate[engine]
            s = pool[idx % len(pool)]
            qstate[engine][1] = idx + 1
            op(engine, "dma_start", inc=s, incn=16,
               waits=tuple(waits) + ((s, s.n),),
               out=out_ap, in_=in_ap, **k)
            return (s, s.n)

        def w(sem_):
            if isinstance(sem_, ChainSem):
                return sem_.mark()
            return (sem_, sem_.n)

        def fence():
            marks = []
            for en, cs in (("tensor", CPE), ("scalar", CACT), ("vector", CDVE)):
                if en == "tensor":
                    inc_last(en, cs)
                marks.append((en, w(cs)))
            dm = [(s, s.n) for s in SXP + GPP]
            for en in ENGINES:
                for other, mk_ in marks:
                    if other != en:
                        op(en, "nop", waits=(mk_,))
                for mk_ in dm:
                    op(en, "nop", waits=(mk_,))

        WAITS_NONE = ()

        dma("gpsimd", T["IDN"][:, :], idn_in.ap())
        dma("gpsimd", T["COS"][:].rearrange("p (tc f) -> p tc f", tc=TCC),
            cos_in.ap().rearrange("(tc p) f -> p tc f", p=128))
        dma("gpsimd", T["SIN"][:].rearrange("p (tc f) -> p tc f", tc=TCC),
            sin_in.ap().rearrange("(tc p) f -> p tc f", p=128))
        dma("gpsimd", T["DMASK"][:, :], dmask_in.ap())
        dma("gpsimd", T["DKT"][:, :], dkt_in.ap())
        dma("gpsimd", T["DQV"][:, :], dqv_in.ap())
        dma("gpsimd", T["CORR"][:, :], corr_in.ap())
        dma("gpsimd", T["NW"][:, :],
            nw_in.ap().rearrange("(dc p) -> p dc", p=128),
            allow_slow_non_contiguous=True)
        dma("gpsimd", T["BIASB"][:, :], bias_in.ap())
        dma("gpsimd", T["WG1B"][:].rearrange("p (dc r) -> p dc r", dc=NDC),
            wg1.ap().rearrange("(dc p) r -> p dc r", p=128))
        dma("gpsimd", T["LAMCT"][:, :], lamc_in.ap())
        consts_dma = dma("gpsimd", T["WG2B"][:, :], wg2.ap())
        op("vector", "memset", T["EPSC"][:, 0:1], EPS_LN)
        op("vector", "memset", T["EPSC"][:, 1:2], EPS_RMS)
        op("vector", "memset", T["EPSC"][:, 2:3], float(1.0 / HD))
        op("vector", "memset", T["ONES"][:, :], 1.0)
        op("vector", "memset", T["S"][:, :], 0.0)
        op("vector", "memset", T["SB"][:, :], 0.0, inc=CDVE)

        FACCv = T["WQ"][:].bitcast(F32)
        FBUFv = [T["WK"][:].bitcast(F32), T["WV"][:].bitcast(F32)]
        FBCv = T["OQ"][:]
        add_done = [None] * 8
        for r in range(8):
            wt = (add_done[r - 2],) if r >= 2 else WAITS_NONE
            fdma = dma(
                "gpsimd",
                FBUFv[r % 2].rearrange("p (dc e) -> p dc e", dc=NDC),
                fac_sl[r].rearrange("(dc p) e -> p dc e", p=128),
                waits=wt)
            if r == 0:
                op("vector", "tensor_copy", FACCv, FBUFv[0], waits=(fdma,),
                   inc=CDVE)
            else:
                op("vector", "tensor_tensor", FACCv, FACCv, FBUFv[r % 2],
                   op=ALU.add, waits=(fdma,), inc=CDVE)
            add_done[r] = w(CDVE)
        op("vector", "tensor_scalar", out=FBCv, in0=FACCv,
           scalar1=float(1.0 / np.sqrt(8.0)), scalar2=None, op0=ALU.mult,
           inc=CDVE)
        fbc = w(CDVE)
        fsin_dma = dma("gpsimd",
                       cc_fs_in.ap().rearrange("(dc p) e -> p dc e", p=128),
                       FBCv.rearrange("p (dc e) -> p dc e", dc=NDC),
                       waits=(fbc,))
        op("gpsimd", "nop", waits=(fsin_dma,))
        if not SKIP_CC:
            op("gpsimd", "collective_compute", "AllGather", ALU.bypass,
               replica_groups=[list(range(8))],
               ins=[cc_fs_in.ap().opt()], outs=[cc_fs_out.ap().opt()],
               inc=CCP[0])
        else:
            op("gpsimd", "nop", inc=CCP[0])
        fsum_cc = (CCP[0], CCP[0].n)
        fence()

        ln_evac = {}
        ln_tp = {}
        ln_chunk_evac = {}
        g1_done = {}
        marks = {}
        s16_sent = [None] * NQ
        spill_o = {}
        spill_q = {}

        def ln_phase(tile):
            xch_free = {}
            for c in range(TPT):
                tc = tile * TPT + c
                slot = c % 2
                XCH = T["XCH"][slot]
                rw = list(xch_free.get(c - 2, ()))
                if tile == 1 and c < 2:
                    rw.append(marks.get(("lnfree", 0, TPT - 2 + c)))
                xdma = dma("sync", XCH[:, :], x_sl[tc * CH:(tc + 1) * CH, :],
                           waits=tuple(x for x in rw if x))
                if TRUNCATE == "ln0dma" and tile == 0 and c == 0:
                    return
                op("vector", "tensor_reduce", T["STATS"][:, c:c + 1], XCH[:, :],
                   AX.X, ALU.add, waits=(xdma,), inc=CDVE)
                sum_done = w(CDVE)
                if TRUNCATE == "ln0red" and tile == 0 and c == 0:
                    return
                op("scalar", "activation", out=T["TRASH"][:, :], in_=XCH[:, :],
                   func=AF.Square, waits=(xdma,), inc=CACT)
                sq_sq = w(CACT)
                op("vector", "tensor_reduce", T["STATS"][:, 8 + c:9 + c],
                   T["TRASH"][:, :], AX.X, ALU.add, waits=(sq_sq,), inc=CDVE)
                sq_done = w(CDVE)
                if TRUNCATE == "ln0sq" and tile == 0 and c == 0:
                    return
                mcol = T["STATS"][:, 16 + c:17 + c]
                vcol = T["STATS"][:, 24 + c:25 + c]
                rcol = T["STATS"][:, 32 + c:33 + c]
                bcol = T["STATS"][:, 40 + c:41 + c]
                tcol = T["STATS"][:, 48 + c:49 + c]
                scol = T["STATS"][:, 56 + c:57 + c]
                op("vector", "tensor_scalar", out=mcol, in0=T["STATS"][:, c:c + 1],
                   scalar1=RECIP_D, scalar2=None, op0=ALU.mult, waits=(sum_done,))
                op("vector", "tensor_scalar", out=vcol,
                   in0=T["STATS"][:, 8 + c:9 + c], scalar1=RECIP_D, scalar2=None,
                   op0=ALU.mult, waits=(sq_done,))
                op("vector", "tensor_tensor", tcol, mcol, mcol, op=ALU.mult)
                op("vector", "tensor_tensor", vcol, vcol, tcol,
                   op=ALU.subtract, inc=CDVE)
                op("scalar", "activation", out=scol, in_=vcol, func=AF.Sqrt,
                   bias=T["EPSC"][:, 0:1], waits=(w(CDVE),), inc=CACT)
                op("vector", "reciprocal", rcol, scol, waits=(w(CACT),))
                op("vector", "tensor_scalar", out=bcol, in0=mcol, scalar1=rcol,
                   scalar2=-1.0, op0=ALU.mult, op1=ALU.mult, inc=CDVE)
                rb = w(CDVE)
                XLNT = T["XLNT"][slot]
                nwt = [rb]
                pv = ln_tp.get((tile, c - 2)) or (
                    ln_tp.get((0, TPT - 2 + c)) if tile == 1 else None)
                if pv:
                    nwt.append(pv)
                op("scalar", "activation", out=XLNT[:, :], in_=XCH[:, :],
                   func=AF.Identity, scale=rcol, bias=bcol, waits=tuple(nwt),
                   inc=CACT)
                nrm = w(CACT)
                if TRUNCATE == "ln0nrm" and tile == 0 and c == 0:
                    return
                marks[("lnfree", tile, c)] = nrm
                xch_free[c] = (nrm, sq_done, sum_done)
                tw = [nrm]
                if (tile, c - 1) in ln_chunk_evac:
                    tw.append(ln_chunk_evac[(tile, c - 1)])
                elif tile == 1 and c == 0:
                    tw.append(ln_chunk_evac[(0, TPT - 1)])
                    tw.append(marks[("fme", "k", 0, NQ - 1)])
                for dc in range(NDC):
                    op("tensor", "transpose",
                       T["PT"][:, dc * 128:(dc + 1) * 128],
                       XLNT[:, dc * 128:(dc + 1) * 128], T["IDN"][:, :],
                       waits=tuple(tw) if dc == 0 else WAITS_NONE)
                inc_last("tensor", CPE)
                ln_tp[(tile, c)] = w(CPE)
                if TRUNCATE == "ln0tp" and tile == 0 and c == 0:
                    return
                if EVAC_MODE == 1:
                    dst = T["XLNFM"][:, c * D:(c + 1) * D]
                    src = T["PT"][:, :]
                elif EVAC_MODE == 2:
                    dst = T["XLNFM"][:, c * D:(c + 1) * D].rearrange(
                        "p (dc t) -> p dc t", dc=NDC)
                    src = T["PT"][:].rearrange("p (dc t) -> p dc t", dc=NDC)
                elif EVAC_MODE == 3:
                    dst = T["XLNFM"][:].rearrange("p (dc t) -> p dc t", dc=NDC)[
                        :, :, c * CH:(c + 1) * CH]
                    src = T["PT"][:, :].rearrange("p (t dc) -> p dc t", dc=NDC)
                else:
                    dst = T["XLNFM"][:].rearrange("p (dc t) -> p dc t", dc=NDC)[
                        :, :, c * CH:(c + 1) * CH]
                    src = T["PT"][:].rearrange("p (dc t) -> p dc t", dc=NDC)
                ecw = [ln_tp[(tile, c)]]
                if tile == 1 and c == 0:
                    ecw.append(marks[("allproj", 0)])
                op("vector", "tensor_copy", dst, src, waits=tuple(ecw),
                   inc=CDVE)
                ln_chunk_evac[(tile, c)] = w(CDVE)
                if TRUNCATE == f"ln{tile}c{c}":
                    return
            ln_evac[tile] = w(CDVE)
            if "d_stats3" in debug and tile == 0:
                dma("sync", dbg["d_stats3"].ap(), T["STATS"][:, :],
                    waits=(w(CACT), w(CDVE)))
            if "d_xlnfm" in debug and tile == 0:
                dma("sync", dbg["d_xlnfm"].ap(), T["XLNFM"][:, :],
                    waits=(ln_evac[tile],))
            g1w = [ln_evac[tile], consts_dma]
            if tile == 1:
                g1w.append(marks[("projevac", 0, NQ - 1)])
            for p in range(2):
                for dc in range(NDC):
                    op("tensor", "matmul", T["PP"][p][:, :],
                       T["WG1B"][:, dc * HD:(dc + 1) * HD],
                       T["XLNFM"][:, dc * 1024 + p * 512:dc * 1024 + (p + 1) * 512],
                       start=(dc == 0), stop=(dc == NDC - 1),
                       waits=tuple(g1w) if (dc == 0 and p == 0)
                       else WAITS_NONE)
                inc_last("tensor", CPE)
                op("vector", "tensor_copy",
                   T["G1"][:, tile * 1024 + p * 512:tile * 1024 + (p + 1) * 512],
                   T["PP"][p][:, :], waits=(w(CPE),), inc=CDVE)
            g1_done[tile] = w(CDVE)
            if "d_g1" in debug and tile == 1:
                dma("sync", dbg["d_g1"].ap(), T["G1"][:, :],
                    waits=(g1_done[tile],))

        sstate = {"lin": 0, "mask_m": None, "intra_m": [None, None],
                  "oevac": None, "dq": None, "sadd": None, "sb": None,
                  "fme_q": {}, "fme_k": {}, "vev": {}}
        pending_scan = [None]
        pstate = {}

        def scan_step(tile, g, c):
            st = sstate
            lin = st["lin"]
            st["lin"] += 1
            slot = lin % 2
            qfm3 = T["QFM"][:].rearrange("p (h t) -> p h t", h=QH)
            kfm3 = T["KFM"][:].rearrange("p (h t) -> p h t", h=QH)
            sw = [st["fme_q"].pop((g, c)), st["fme_k"].pop((g, c))]
            if st["mask_m"]:
                sw.append(st["mask_m"])
            for hl in range(QH):
                op("tensor", "matmul", T["PSC"][:, hl * 128:(hl + 1) * 128],
                   kfm3[:, hl, c * CH:(c + 1) * CH],
                   qfm3[:, hl, c * CH:(c + 1) * CH],
                   start=True, stop=True,
                   waits=tuple(sw) if hl == 0 else WAITS_NONE)
            inc_last("tensor", CPE)
            scd = w(CPE)
            mw = [scd]
            if st["intra_m"][slot]:
                mw.append(st["intra_m"][slot])
            op("vector", "tensor_tensor", T["SCT"][slot][:, :],
               T["PSC"][:, :], T["DMASK"][:, g * 512:(g + 1) * 512],
               op=ALU.mult, waits=tuple(mw), inc=CDVE)
            st["mask_m"] = w(CDVE)
            ow = [st["mask_m"], st["vev"].pop((g, c))]
            if st["oevac"]:
                ow.append(st["oevac"])
            for hl in range(QH):
                op("tensor", "matmul", T["POI"][:, hl * 128:(hl + 1) * 128],
                   T["SCT"][slot][:, hl * 128:(hl + 1) * 128],
                   T["V"][:, c * 512 + hl * 128:c * 512 + (hl + 1) * 128],
                   start=True, stop=True,
                   waits=tuple(ow) if hl == 0 else WAITS_NONE)
            iw = [x for x in (st["dq"], st["sb"]) if x]
            for hl in range(QH):
                h = g * QH + hl
                op("tensor", "matmul", T["POE"][:, hl * 128:(hl + 1) * 128],
                   qfm3[:, hl, c * CH:(c + 1) * CH],
                   T["SB"][:, h * HD:(h + 1) * HD],
                   start=True, stop=True,
                   waits=tuple(iw) if hl == 0 else WAITS_NONE)
            kw = [st["sadd"]] if st["sadd"] else []
            for hl in range(QH):
                op("tensor", "matmul", T["PKV"][:, hl * 128:(hl + 1) * 128],
                   T["K2"][:, c * 512 + hl * 128:c * 512 + (hl + 1) * 128],
                   T["V"][:, c * 512 + hl * 128:c * 512 + (hl + 1) * 128],
                   start=True, stop=True,
                   waits=tuple(kw) if hl == 0 else WAITS_NONE)
            inc_last("tensor", CPE)
            mm2 = w(CPE)
            st["intra_m"][slot] = mm2
            marks[("scanstep", tile, g, c)] = mm2
            for hl in range(QH):
                h = g * QH + hl
                op("scalar", "activation",
                   out=T["TMPOE"][:, hl * 128:(hl + 1) * 128],
                   in_=T["POE"][:, hl * 128:(hl + 1) * 128],
                   func=AF.Identity, scale=T["DQV"][:, h:h + 1],
                   waits=(mm2,) if hl == 0 else WAITS_NONE)
            st["dq"] = w(CACT)
            dw = [st["dq"]]
            if st["sadd"]:
                dw.append(st["sadd"])
            if st["sb"]:
                dw.append(st["sb"])
            if tile == 0 and g == 0 and c == 0:
                dw.append(consts_dma)
            for hl in range(QH):
                h = g * QH + hl
                op("scalar", "activation",
                   out=T["S"][:, h * HD:(h + 1) * HD],
                   in_=T["S"][:, h * HD:(h + 1) * HD], func=AF.Identity,
                   scale=T["LAMCT"][:, h:h + 1],
                   waits=tuple(dw) if hl == 0 else WAITS_NONE)
            decay_done = w(CACT)
            oqw = [st["dq"]]
            if c == 0:
                pq = (tile, g - 1) if g > 0 else (
                    (tile - 1, NQ - 1) if tile > 0 else None)
                pv = spill_o.get(pq) if pq else None
                if pv:
                    oqw.append(pv)
            op("vector", "tensor_tensor", T["OQ"][:, c * 512:(c + 1) * 512],
               T["POI"][:, :], T["TMPOE"][:, :], op=ALU.add,
               waits=tuple(oqw), inc=CDVE)
            st["oevac"] = w(CDVE)
            op("vector", "tensor_tensor", T["S"][:, g * 512:(g + 1) * 512],
               T["S"][:, g * 512:(g + 1) * 512], T["PKV"][:, :],
               op=ALU.add, waits=(decay_done, mm2), inc=CDVE)
            st["sadd"] = w(CDVE)
            op("vector", "tensor_copy", T["SB"][:, g * 512:(g + 1) * 512],
               T["S"][:, g * 512:(g + 1) * 512], inc=CDVE)
            st["sb"] = w(CDVE)

            if c == TPT - 1:
                marks[("scan", tile, g)] = mm2
                spill_o[(tile, g)] = dma(
                    "sync",
                    o_spill[tile * TPT:(tile + 1) * TPT, :,
                            g * 512:(g + 1) * 512].rearrange(
                        "tc t e -> t tc e"),
                    T["OQ"][:].rearrange("p (tc e) -> p tc e", tc=TPT),
                    waits=(st["oevac"],))
                spill_q[(tile, g)] = []
                for c_ in range(TPT):
                    spill_q[(tile, g)].append(dma(
                        "sync",
                        q_spill[tile * TPT + c_,
                                g * QH:(g + 1) * QH].rearrange(
                            "h hd t -> hd h t"),
                        T["QFM"][:].rearrange("p (h t) -> p h t", h=QH)[
                            :, :, c_ * CH:(c_ + 1) * CH],
                        waits=(mm2,) if c_ == 0 else WAITS_NONE))
                if tile == 1:
                    sdma = dma("gpsimd", cc_s_in[g].ap(),
                               T["S"][:, g * 512:(g + 1) * 512],
                               waits=(st["sadd"],))
                    op("gpsimd", "nop", waits=(sdma,))
                    if not SKIP_CC:
                        op("gpsimd", "collective_compute", "AllGather",
                           ALU.bypass,
                           replica_groups=[[0, 1], [2, 3], [4, 5], [6, 7]],
                           ins=[cc_s_in[g].ap().opt()],
                           outs=[cc_s_out[g].ap().opt()], inc=CCP[1 + g])
                    else:
                        op("gpsimd", "nop", inc=CCP[1 + g])
                    s16_sent[g] = (CCP[1 + g], CCP[1 + g].n)

        def quad_phase(tile, g):
            prev_quad = (tile, g - 1) if g > 0 else (
                (tile - 1, NQ - 1) if tile > 0 else None)

            wdmas = {}
            for wname, wt, pj in (("WQ", wq, "q"), ("WK", wk, "k"),
                                  ("WV", wv, "v")):
                pv = marks.get(("proj", pj) + prev_quad) if prev_quad else None
                wdmas[pj] = dma(
                    "gpsimd",
                    T[wname][:].rearrange("p (dc e) -> p dc e", dc=NDC),
                    wt[:, g * 512:(g + 1) * 512].rearrange(
                        "(dc p) e -> p dc e", p=128),
                    waits=(pv,) if pv else WAITS_NONE)

            ppfree = {}
            rotp = {}
            tpp = {}
            silup = {}
            mmk = {}

            def proj_chunk(which, wsb, c, wi):
                gi = c * 3 + wi
                slot = gi % 2
                pw = [ln_evac[tile], wdmas[which]]
                if ppfree.get(gi - 2):
                    pw.append(ppfree[gi - 2])
                elif gi < 2:
                    pw.append(g1_done[tile])
                    if prev_quad:
                        pv = marks.get(("projevac",) + prev_quad)
                        if pv:
                            pw.append(pv)
                for dc in range(NDC):
                    op("tensor", "matmul", T["PP"][slot][:, :],
                       T["XLNFM"][:, dc * 1024 + c * CH:dc * 1024 + (c + 1) * CH],
                       wsb[:, dc * 512:(dc + 1) * 512],
                       start=(dc == 0), stop=(dc == NDC - 1),
                       waits=tuple(pw) if dc == 0 else WAITS_NONE)
                inc_last("tensor", CPE)
                mmd = w(CPE)
                mmk[(which, c)] = mmd
                prevscan = marks.get(("scanstep",) + prev_quad + (c,)) \
                    if prev_quad else None

                if which == "v":
                    ew = [mmd]
                    if prevscan:
                        ew.append(prevscan)
                    op("scalar", "activation",
                       out=T["V"][:, c * 512:(c + 1) * 512],
                       in_=T["PP"][slot][:, :], func=AF.Identity,
                       waits=tuple(ew), inc=CACT)
                    ppfree[gi] = w(CACT)
                    sstate["vev"][(g, c)] = w(CACT)
                    return

                QSB = T["QSB"][0 if which == "q" else 1]
                QTM = T["QTM"][0 if which == "q" else 1]
                sw = [mmd]
                if rotp.get((which, c - 1)):
                    sw.append(rotp[(which, c - 1)])
                elif pstate.get(("rot", which)):
                    sw.append(pstate[("rot", which)])
                op("scalar", "activation", out=QSB[:, :],
                   in_=T["PP"][slot][:, :], func=AF.Silu, waits=tuple(sw),
                   inc=CACT)
                silup[(which, c)] = w(CACT)
                ppfree[gi] = w(CACT)
                tc = tile * TPT + c
                q3 = QSB[:].rearrange("p (h e) -> p h e", h=QH)
                o3 = QTM[:].rearrange("p (h e) -> p h e", h=QH)
                cosap = T["COS"][:, tc * 64:(tc + 1) * 64][:, None, :] \
                    .broadcast_to([128, QH, 64])
                sinap = T["SIN"][:, tc * 64:(tc + 1) * 64][:, None, :] \
                    .broadcast_to([128, QH, 64])
                t1 = T["TMP1"][:].rearrange("p (h e) -> p h e", h=QH)
                t2 = T["TMP2"][:].rearrange("p (h e) -> p h e", h=QH)
                rw = [silup[(which, c)]]
                if tpp.get((which, c - 1)):
                    rw.append(tpp[(which, c - 1)])
                elif pstate.get(("tp", which)):
                    rw.append(pstate[("tp", which)])
                op("vector", "tensor_tensor", t1, q3[:, :, 0:64], cosap,
                   op=ALU.mult, waits=tuple(rw))
                op("vector", "tensor_tensor", t2, q3[:, :, 64:128], sinap,
                   op=ALU.mult)
                op("vector", "tensor_tensor", o3[:, :, 0:64], t1, t2,
                   op=ALU.subtract)
                op("vector", "tensor_tensor", t1, q3[:, :, 0:64], sinap,
                   op=ALU.mult)
                op("vector", "tensor_tensor", t2, q3[:, :, 64:128], cosap,
                   op=ALU.mult)
                op("vector", "tensor_tensor", o3[:, :, 64:128], t1, t2,
                   op=ALU.add, inc=CDVE)
                rotp[(which, c)] = w(CDVE)
                if which == "k":
                    kw = [w(CDVE)]
                    if prevscan:
                        kw.append(prevscan)
                    op("vector", "tensor_tensor",
                       T["K2"][:, c * 512:(c + 1) * 512], QTM[:, :],
                       T["DKT"][:, g * 512:(g + 1) * 512], op=ALU.mult,
                       waits=tuple(kw), inc=CDVE)
                    rotp[(which, c)] = w(CDVE)

            def proj_transpose(which, c):
                QTM = T["QTM"][0 if which == "q" else 1]
                dst_fm = T["QFM"] if which == "q" else T["KFM"]
                off = 0 if which == "q" else 1024
                prevscan = marks.get(("scanstep",) + prev_quad + (c,)) \
                    if prev_quad else None
                tw = [rotp[(which, c)]]
                if tpp.get((which, c - 1)):
                    tw.append(marks.get(("fme", which, tile, g, c - 1)))
                elif marks.get(("lastPT", tile)):
                    tw.append(marks[("lastPT", tile)])
                for hl in range(QH):
                    op("tensor", "transpose",
                       T["PT"][:, off + hl * 128:off + (hl + 1) * 128],
                       QTM[:, hl * 128:(hl + 1) * 128], T["IDN"][:, :],
                       waits=tuple(tw) if hl == 0 else WAITS_NONE)
                inc_last("tensor", CPE)
                tpp[(which, c)] = w(CPE)
                ew = [tpp[(which, c)]]
                if prevscan:
                    ew.append(prevscan)
                pq = prev_quad
                if pq and spill_q.get(pq):
                    ew.append(spill_q[pq][c])
                dstap = dst_fm[:].rearrange("p (h t) -> p h t", h=QH)[
                    :, :, c * CH:(c + 1) * CH]
                srcap = T["PT"][:, off:off + 512].rearrange(
                    "p (h t) -> p h t", h=QH)
                op("vector", "tensor_copy", dstap, srcap, waits=tuple(ew),
                   inc=CDVE)
                fm = w(CDVE)
                marks[("fme", which, tile, g, c)] = fm
                if which == "q":
                    sstate["fme_q"][(g, c)] = fm
                else:
                    sstate["fme_k"][(g, c)] = fm

            for c in range(TPT):
                if c == 0 and pending_scan[0] is not None:
                    scan_step(*pending_scan[0])
                    pending_scan[0] = None
                proj_chunk("q", T["WQ"], c, 0)
                proj_chunk("k", T["WK"], c, 1)
                proj_chunk("v", T["WV"], c, 2)
                if c > 0:
                    scan_step(tile, g, c - 1)
                proj_transpose("q", c)
                proj_transpose("k", c)
            marks[("proj", "q", tile, g)] = mmk[("q", TPT - 1)]
            marks[("proj", "k", tile, g)] = mmk[("k", TPT - 1)]
            marks[("proj", "v", tile, g)] = mmk[("v", TPT - 1)]
            marks[("projevac", tile, g)] = w(CACT)
            marks[("fme", "k", tile, g)] = marks[("fme", "k", tile, g, TPT - 1)]
            marks[("lastPT", tile)] = marks[("fme", "k", tile, g, TPT - 1)]
            pstate[("rot", "q")] = rotp[("q", TPT - 1)]
            pstate[("rot", "k")] = rotp[("k", TPT - 1)]
            pstate[("tp", "q")] = tpp[("q", TPT - 1)]
            pstate[("tp", "k")] = tpp[("k", TPT - 1)]
            if g == NQ - 1:
                scan_step(tile, g, TPT - 1)
                pending_scan[0] = None
            else:
                pending_scan[0] = (tile, g, TPT - 1)

        def tail_phase(tile):
            S16ST = T[f"S16ST{tile}"]
            OCH = T[f"OCH{tile}"]
            QCH = T[f"QCH{tile}"]
            TMPC = T[f"TMPC{tile}"]
            GATS = T[f"GATS{tile}"]
            XG = T[f"XG{tile}"]
            SQB = T[f"SQB{tile}"]
            OGT = T[f"OGT{tile}"]
            PCORR = T[f"PCORR{tile}"]
            PGATE = T[f"PGATE{tile}"]
            PT2 = T[f"PT2{tile}"]

            s16marks = []
            for g in range(NQ):
                s16marks.append(dma(
                    "sync", S16ST[:, g * 512:(g + 1) * 512],
                    cc_s_out[g][0:HD, :], waits=(s16_sent[g],)))
            S16B = T[f"S16B{tile}"]
            op("vector", "tensor_copy", S16B[:, :], S16ST[:, :],
               waits=tuple(s16marks), inc=CDVE)
            s16b = w(CDVE)
            if "d_s16" in debug and tile == 0:
                dma("sync", dbg["d_s16"].ap(), S16ST[:, :], waits=(s16b,))

            sig_done = [None] * TPT
            corr_add = [None] * TPT
            qch_used = [None] * TPT
            prev_vadd = None
            for c in range(TPT):
                tc = tile * TPT + c
                slot = c % 2
                odma = dma("sync", OCH[:, c * D:(c + 1) * D], o_spill[tc],
                           waits=tuple(spill_o[(tile, gg)]
                                       for gg in range(NQ)) if c == 0
                           else WAITS_NONE)
                qw = []
                if c == 0:
                    for gg in range(NQ):
                        qw.extend(spill_q[(tile, gg)])
                if c >= 2 and qch_used[c - 2]:
                    qw.append(qch_used[c - 2])
                qdma = dma("sync",
                           QCH[slot][:].rearrange("p (h t) -> p h t", h=NH),
                           q_spill[tc].rearrange("h hd t -> hd h t"),
                           waits=tuple(qw))
                for g in range(NQ):
                    ps_ = PCORR[g % 2]
                    cw = [qdma, s16b]
                    if g >= 2:
                        cw.append(marks[("corrva", tile, c, g - 2)])
                    elif c >= 1:
                        cw.append(marks[("corrva", tile, c - 1, g + 2)])
                    for hl in range(QH):
                        h = g * QH + hl
                        op("tensor", "matmul", ps_[:, hl * 128:(hl + 1) * 128],
                           QCH[slot][:, h * 128:(h + 1) * 128],
                           S16B[:, h * HD:(h + 1) * HD],
                           start=True, stop=True,
                           waits=tuple(x for x in cw if x) if hl == 0
                           else WAITS_NONE)
                    inc_last("tensor", CPE)
                    cmm = w(CPE)
                    tm = TMPC[g % 2]
                    scw = [cmm]
                    if g >= 2:
                        scw.append(marks[("corrva", tile, c, g - 2)])
                    elif c >= 1:
                        scw.append(marks[("corrva", tile, c - 1, g + 2)])
                    for hl in range(QH):
                        h = g * QH + hl
                        op("scalar", "activation",
                           out=tm[:, hl * 128:(hl + 1) * 128],
                           in_=ps_[:, hl * 128:(hl + 1) * 128], func=AF.Identity,
                           scale=T["CORR"][:, tc * NH + h:tc * NH + h + 1],
                           waits=tuple(scw) if hl == 0 else WAITS_NONE)
                    inc_last("scalar", CACT)
                    csc = w(CACT)
                    op("vector", "tensor_tensor",
                       OCH[:, c * D + g * 512:c * D + (g + 1) * 512],
                       OCH[:, c * D + g * 512:c * D + (g + 1) * 512],
                       tm[:, :], op=ALU.add, waits=(csc, odma), inc=CDVE)
                    marks[("corrva", tile, c, g)] = w(CDVE)
                    prev_vadd = w(CDVE)
                corr_add[c] = w(CDVE)
                qch_used[c] = w(CPE)

                gw = [g1_done[tile], consts_dma]
                if c >= 1 and sig_done[c - 1]:
                    gw.append(sig_done[c - 1])
                for q4 in range(4):
                    op("tensor", "matmul", PGATE[:, q4 * 512:(q4 + 1) * 512],
                       T["G1"][:, tile * 1024 + c * CH:tile * 1024 + (c + 1) * CH],
                       T["WG2B"][:, q4 * 512:(q4 + 1) * 512],
                       start=True, stop=True,
                       waits=tuple(gw) if q4 == 0 else WAITS_NONE)
                inc_last("tensor", CPE)
                gmm = w(CPE)
                for q4 in range(4):
                    op("scalar", "activation",
                       out=GATS[:, c * D + q4 * 512:c * D + (q4 + 1) * 512],
                       in_=PGATE[:, q4 * 512:(q4 + 1) * 512], func=AF.Sigmoid,
                       waits=(gmm,) if q4 == 0 else WAITS_NONE)
                inc_last("scalar", CACT)
                sig_done[c] = w(CACT)
            if "d_gats" in debug and tile == 0:
                dma("sync", dbg["d_gats"].ap(), GATS[:, :],
                    waits=(w(CACT),))

            ogfm_e = [None] * TPT
            sq_marks = [None] * TPT
            for c in range(TPT):
                xg = XG[c % 2]
                xw = [corr_add[c], sig_done[c]]
                if c >= 2 and ogfm_e[c - 2]:
                    xw.append(ogfm_e[c - 2])
                if c >= 2 and sq_marks[c - 2]:
                    xw.append(sq_marks[c - 2])
                op("vector", "tensor_tensor", xg[:, :], OCH[:, c * D:(c + 1) * D],
                   GATS[:, c * D:(c + 1) * D], op=ALU.mult, waits=tuple(xw),
                   inc=CDVE)
                xg_done = w(CDVE)
                op("scalar", "activation", out=SQB[:, :], in_=xg[:, :],
                   func=AF.Square, waits=(xg_done,), inc=CACT)
                sq_marks[c] = w(CACT)
                op("vector", "tensor_reduce", T["MS"][:, :],
                   SQB[:].rearrange("p (h e) -> p h e", h=NH), AX.X, ALU.add,
                   waits=(w(CACT),), inc=CDVE)
                op("scalar", "activation", out=T["SD2"][:, :], in_=T["MS"][:, :],
                   func=AF.Sqrt, scale=T["EPSC"][:, 2:3],
                   bias=T["EPSC"][:, 1:2], waits=(w(CDVE),), inc=CACT)
                op("vector", "reciprocal", T["R2"][:, :], T["SD2"][:, :],
                   waits=(w(CACT),))
                inc_last("vector", CDVE)
                r2d = w(CDVE)
                for h in range(NH):
                    op("vector", "tensor_scalar",
                       out=OGT[:, h * HD:(h + 1) * HD],
                       in0=xg[:, h * HD:(h + 1) * HD],
                       scalar1=T["R2"][:, h:h + 1], scalar2=None, op0=ALU.mult,
                       waits=(r2d,) if h == 0 else WAITS_NONE)
                inc_last("vector", CDVE)
                ogd = w(CDVE)
                tw = [ogd]
                if c >= 1 and ogfm_e[c - 1]:
                    tw.append(ogfm_e[c - 1])
                if c == 0 and tile == 1:
                    tw.append(marks[("ogfmall", 0)])
                for dc in range(NDC):
                    op("tensor", "transpose", PT2[:, dc * 128:(dc + 1) * 128],
                       OGT[:, dc * 128:(dc + 1) * 128], T["IDN"][:, :],
                       waits=tuple(tw) if dc == 0 else WAITS_NONE)
                inc_last("tensor", CPE)
                tp2 = w(CPE)
                for dc in range(NDC):
                    op("vector", "tensor_scalar",
                       out=T["XLNFM"][:, dc * 1024 + c * CH:dc * 1024 + (c + 1) * CH],
                       in0=PT2[:, dc * 128:(dc + 1) * 128],
                       scalar1=T["NW"][:, dc:dc + 1], scalar2=None, op0=ALU.mult,
                       waits=(tp2,) if dc == 0 else WAITS_NONE)
                inc_last("vector", CDVE)
                ogfm_e[c] = w(CDVE)
            marks[("ogfmall", tile)] = w(CDVE)
            og_all = w(CDVE)
            if "d_ogfm" in debug and tile == 0:
                dma("sync", dbg["d_ogfm"].ap(), T["XLNFM"][:, :],
                    waits=(og_all,))

            fence()
            WOH = T[f"WOH{tile}"]
            XRCH = T[f"XRCH{tile}"]
            X2H = T[f"X2H{tile}"]
            TRASH2 = T[f"TRASH2{tile}"]
            PB = T[f"PB{tile}"]
            x2w = [[None] * TPT, [None] * TPT]
            x2free = {}
            pbe_all = []
            for H in range(2):
                wodma = dma("gpsimd",
                            WOH[:].rearrange("p (dc e) -> p dc e", dc=NDC),
                            wo[:, H * 1024:(H + 1) * 1024].rearrange(
                                "(dc p) e -> p dc e", p=128),
                            waits=(w(CPE),) if H == 1 else WAITS_NONE)
                pbe = [None] * TPT
                for c in range(TPT):
                    tc = tile * TPT + c
                    slot = c % 2
                    gi = H * TPT + c
                    rw = [pbe_all[gi - 2]] if gi >= 2 else []
                    xrdma = dma("sync", XRCH[slot][:, :],
                                x_sl[tc * CH:(tc + 1) * CH,
                                     H * 1024:(H + 1) * 1024],
                                waits=tuple(x for x in rw if x))
                    mw = [og_all, wodma]
                    if gi >= 2 and pbe_all[gi - 2]:
                        mw.append(pbe_all[gi - 2])
                    for dc in range(NDC):
                        for n in range(2):
                            op("tensor", "matmul",
                               PB[slot][:, n * 512:(n + 1) * 512],
                               T["XLNFM"][:, dc * 1024 + c * CH:dc * 1024 + (c + 1) * CH],
                               WOH[:, dc * 1024 + n * 512:dc * 1024 + (n + 1) * 512],
                               start=(dc == 0), stop=(dc == NDC - 1),
                               waits=tuple(mw) if (dc == 0 and n == 0)
                               else WAITS_NONE)
                    inc_last("tensor", CPE)
                    bmm = w(CPE)
                    ew = [bmm, xrdma]
                    ew += list(x2free.get(gi - 2, ()))
                    op("vector", "tensor_tensor", X2H[slot][:, :],
                       PB[slot][:, :], XRCH[slot][:, :],
                       op=ALU.add, waits=tuple(ew), inc=CDVE)
                    ev = w(CDVE)
                    pbe[c] = ev
                    pbe_all.append(ev)
                    op("vector", "tensor_reduce",
                       T["S2S"][:, c * 2 + H:c * 2 + H + 1], X2H[slot][:, :],
                       AX.X, ALU.add, inc=CDVE)
                    red = w(CDVE)
                    op("scalar", "activation", out=TRASH2[:, :],
                       in_=X2H[slot][:, :], func=AF.Square,
                       accum_out=T["Q2S"][:, c * 2 + H:c * 2 + H + 1],
                       waits=(ev,))
                    op("scalar", "activation", out=T["DUMT"][:, :],
                       in_=T["EPSC"][:, 0:1], func=AF.Identity, inc=CACT)
                    sqd = w(CACT)
                    x2w[H][c] = dma(
                        "gpsimd",
                        out_sl[tc * CH:(tc + 1) * CH,
                               H * 1024:(H + 1) * 1024],
                        X2H[slot][:, :], waits=(ev,))
                    x2free[gi] = (x2w[H][c], sqd, red)
            bstats = (w(CDVE), w(CACT))

            fence()
            XB2 = T[f"XB2{tile}"]
            XLNT2 = T[f"XLNT2{tile}"]
            PT3 = T[f"PT3{tile}"]
            xl2e = [None] * TPT
            nrm_done = [None] * TPT
            if "d_stats" in debug and tile == 0:
                dma("sync", dbg["d_stats"][:, 0:16], T["S2S"][:, :],
                    waits=bstats)
                dma("sync", dbg["d_stats"][:, 16:32], T["Q2S"][:, :],
                    waits=bstats)
            for c in range(TPT):
                tc = tile * TPT + c
                slot = c % 2
                rw = [nrm_done[c - 2]] if c >= 2 else []
                rw += [x2w[0][c], x2w[1][c]]
                xbdma = dma("sync", XB2[slot][:, :],
                            out_sl[tc * CH:(tc + 1) * CH, :],
                            waits=tuple(x for x in rw if x))
                if "d_x2" in debug and tile == 0:
                    dma("sync", dbg["d_x2"][:, c * D:(c + 1) * D],
                        XB2[slot][:, :], waits=(xbdma,))
                mcol = T["STATS"][:, c:c + 1]
                vcol = T["STATS"][:, 8 + c:9 + c]
                rcol = T["STATS"][:, 16 + c:17 + c]
                bcol = T["STATS"][:, 24 + c:25 + c]
                tcol = T["STATS"][:, 32 + c:33 + c]
                scol = T["STATS"][:, 40 + c:41 + c]
                op("vector", "tensor_tensor", mcol,
                   T["S2S"][:, c * 2:c * 2 + 1],
                   T["S2S"][:, c * 2 + 1:c * 2 + 2], op=ALU.add,
                   waits=bstats if c == 0 else WAITS_NONE)
                op("vector", "tensor_scalar", out=mcol, in0=mcol,
                   scalar1=RECIP_D, scalar2=None, op0=ALU.mult)
                op("vector", "tensor_tensor", vcol,
                   T["Q2S"][:, c * 2:c * 2 + 1],
                   T["Q2S"][:, c * 2 + 1:c * 2 + 2], op=ALU.add)
                op("vector", "tensor_scalar", out=vcol, in0=vcol,
                   scalar1=RECIP_D, scalar2=None, op0=ALU.mult)
                op("vector", "tensor_tensor", tcol, mcol, mcol, op=ALU.mult)
                op("vector", "tensor_tensor", vcol, vcol, tcol,
                   op=ALU.subtract, inc=CDVE)
                op("scalar", "activation", out=scol, in_=vcol, func=AF.Sqrt,
                   bias=T["EPSC"][:, 0:1], waits=(w(CDVE),), inc=CACT)
                op("vector", "reciprocal", rcol, scol, waits=(w(CACT),))
                op("vector", "tensor_scalar", out=bcol, in0=mcol, scalar1=rcol,
                   scalar2=-1.0, op0=ALU.mult, op1=ALU.mult, inc=CDVE)
                nwt = [w(CDVE), xbdma]
                if c >= 2 and nrm_done[c - 2]:
                    nwt.append(nrm_done[c - 2])
                op("scalar", "activation", out=XLNT2[slot][:, :],
                   in_=XB2[slot][:, :], func=AF.Identity, scale=rcol, bias=bcol,
                   waits=tuple(nwt), inc=CACT)
                nrm_done[c] = w(CACT)
                tw = [nrm_done[c]]
                if c >= 1 and xl2e[c - 1]:
                    tw.append(xl2e[c - 1])
                for dc in range(NDC):
                    op("tensor", "transpose", PT3[:, dc * 128:(dc + 1) * 128],
                       XLNT2[slot][:, dc * 128:(dc + 1) * 128], T["IDN"][:, :],
                       waits=tuple(tw) if dc == 0 else WAITS_NONE)
                inc_last("tensor", CPE)
                tpd = w(CPE)
                dst = T[f"XLN2FM{tile}"][:].rearrange("p (dc t) -> p dc t", dc=NDC)[
                    :, :, c * CH:(c + 1) * CH]
                src = PT3[:].rearrange("p (dc t) -> p dc t", dc=NDC)
                op("vector", "tensor_copy", dst, src, waits=(tpd,), inc=CDVE)
                xl2e[c] = w(CDVE)
            xl2all = w(CDVE)
            if "d_stats2" in debug and tile == 0:
                dma("sync", dbg["d_stats2"].ap(), T["STATS"][:, :],
                    waits=(w(CACT), w(CDVE)))
            if "d_xln2fm" in debug and tile == 0:
                dma("sync", dbg["d_xln2fm"].ap(), T[f"XLN2FM{tile}"][:, :],
                    waits=(xl2all,))

            FSP = T[f"FSP{tile}"]
            OUTES = T[f"OUTES{tile}"]
            PF = T[f"PF{tile}"]
            fsp_used = [None] * 4
            pfi = [0]
            pf_free = {}
            prev_outes = [None, None]
            for es in range(4):
                slot = es % 2
                fw = [fsum_cc]
                if es >= 2 and fsp_used[es - 2]:
                    fw.append(fsp_used[es - 2])
                fsdma = []
                for rr in range(2):
                    dstp = FSP[slot][:].rearrange("p (dc e) -> p dc e", dc=NDC)[
                        :, :, rr * 256:(rr + 1) * 256]
                    fsdma.append(dma(
                        "sync", dstp,
                        cc_fs_out[(2 * es + rr) * D:(2 * es + rr + 1) * D]
                        .rearrange("(dc p) e -> p dc e", p=128),
                        waits=tuple(fw) if rr == 0 else WAITS_NONE))
                for half in range(2):
                    hw_ = [xl2all] + fsdma
                    gbanks = []
                    for tcq in range(4):
                        c = half * 4 + tcq
                        b = pfi[0] % 6
                        pfi[0] += 1
                        gbanks.append(b)
                        bw = list(hw_) if tcq == 0 else []
                        if pf_free.get(b):
                            bw.append(pf_free[b])
                        op("tensor", "matmul", PF[b][:, :], T["ONES"][:, :],
                           T["BIASB"][:, es * 512:(es + 1) * 512],
                           start=True, stop=False,
                           waits=tuple(x for x in bw if x))
                        for dc in range(NDC):
                            op("tensor", "matmul", PF[b][:, :],
                               T[f"XLN2FM{tile}"][:, dc * 1024 + c * CH:dc * 1024 + (c + 1) * CH],
                               FSP[slot][:, dc * 512:(dc + 1) * 512],
                               start=False, stop=(dc == NDC - 1))
                    inc_last("tensor", CPE)
                    fmm = w(CPE)
                    evw = [fmm]
                    if prev_outes[half]:
                        evw.append(prev_outes[half])
                    for tcq in range(4):
                        op("vector", "tensor_copy",
                           OUTES[half][:, tcq * 512:(tcq + 1) * 512],
                           PF[gbanks[tcq]][:, :],
                           waits=tuple(evw) if tcq == 0 else WAITS_NONE)
                        pf_free[gbanks[tcq]] = w(CDVE)
                    evd = w(CDVE)
                    prev_outes[half] = dma("gpsimd",
                        out_sl[tile * 1024 + half * 512:
                               tile * 1024 + (half + 1) * 512,
                               es * 512:(es + 1) * 512].rearrange(
                            "(tc t) e -> t tc e", t=CH),
                        OUTES[half][:].rearrange("p (tc e) -> p tc e", tc=4),
                        waits=(evd,), accum_op=ALU.add)
                    _ = prev_outes
                fsp_used[es] = w(CPE)

        def trunc(name):
            return TRUNCATE == name

        while True:
            if trunc("prelude"):
                break
            ln_phase(0)
            if trunc("ln0") or (TRUNCATE or "").startswith("ln0c") \
                    or (TRUNCATE or "").startswith("ln0d") \
                    or (TRUNCATE or "").startswith("ln0r") \
                    or (TRUNCATE or "").startswith("ln0s") \
                    or (TRUNCATE or "").startswith("ln0n") \
                    or TRUNCATE == "ln0tp":
                break
            for g in range(NQ):
                quad_phase(0, g)
            marks[("allproj", 0)] = marks[("proj", "v", 0, NQ - 1)]
            if trunc("quads0"):
                break
            ln_phase(1)
            if trunc("ln1"):
                break
            for g in range(NQ):
                quad_phase(1, g)
            if trunc("quads1"):
                break
            fence()
            tail_phase(0)
            if trunc("tail0"):
                break
            fence()
            tail_phase(1)
            break

        for s in GPP:
            op("gpsimd", "nop", waits=((s, s.n),))
        for s in SXP:
            op("sync", "nop", waits=((s, s.n),))

    with nc.Block() as block:
        block.sync(lambda e: walk("sync", e))
        block.scalar(lambda e: walk("scalar", e))
        block.vector(lambda e: walk("vector", e))
        block.tensor(lambda e: walk("tensor", e))
        block.gpsimd(lambda e: walk("gpsimd", e))

    top.close()
    return nc


def _host_tables(core):
    half = core % 2
    p0 = half * NT
    ll = _LL.astype(np.float32)
    theta = (np.float32(10000.0) ** (np.float32(-2.0)
             * np.arange(64, dtype=np.float32) / np.float32(HD))).astype(np.float32)
    pos = np.arange(p0, p0 + NT, dtype=np.float32)
    ang = (pos[:, None] * theta[None, :]).astype(np.float32)
    cos_t = np.cos(ang.astype(np.float64)).astype(np.float32)
    sin_t = np.sin(ang.astype(np.float64)).astype(np.float32)
    i = np.arange(CH, dtype=np.float64)
    diff = i[None, :] - i[:, None]
    dm = np.where(diff >= 0, np.exp(ll[:, None, None].astype(np.float64)
                                    * diff[None]), 0.0)
    dmask = np.ascontiguousarray(
        dm.transpose(1, 0, 2).reshape(CH, NH * CH)).astype(np.float32)
    dk = np.exp(ll[None, :].astype(np.float64)
                * (CH - 1.0 - i[:, None])).astype(np.float32)
    dkt = np.repeat(dk[:, :, None], CH, axis=2).reshape(CH, NH * CH)
    dkt = np.ascontiguousarray(dkt).astype(np.float32)
    dqv = np.exp(ll[None, :].astype(np.float64) * (i[:, None] + 1.0)
                 ).astype(np.float32)
    oddf = 1.0 if (core % 2 == 1) else 0.0
    lamc = np.exp(ll.astype(np.float64) * CH)
    corr = np.zeros((CH, TCC * NH), np.float64)
    for c in range(TCC):
        corr[:, c * NH:(c + 1) * NH] = dqv.astype(np.float64) * (lamc ** c)[None, :]
    corr = (corr * oddf).astype(np.float32)
    lamct = np.tile(lamc.astype(np.float32)[None, :], (CH, 1))
    return cos_t, sin_t, dmask, dkt, dqv, corr, lamct


_NC_CACHE = {}


def _get_nc(debug=()):
    key = tuple(sorted(debug))
    if key not in _NC_CACHE:
        _NC_CACHE[key] = build_program(debug=key)
    return _NC_CACHE[key]


def kernel(L0_attn_x, L0_attn_Wq, L0_attn_Wk, L0_attn_Wv, L0_attn_Wo,
           L0_attn_Wg1, L0_attn_Wg2, L0_attn_norm_weight, L0_ffn_factors,
           L0_ffn_bias, _debug=(), _nc=None):
    x = np.ascontiguousarray(np.asarray(L0_attn_x, np.float32))
    weights = {
        "wq": np.ascontiguousarray(np.asarray(L0_attn_Wq, np.float32)),
        "wk": np.ascontiguousarray(np.asarray(L0_attn_Wk, np.float32)),
        "wv": np.ascontiguousarray(np.asarray(L0_attn_Wv, np.float32)),
        "wo": np.ascontiguousarray(np.asarray(L0_attn_Wo, np.float32)),
        "wg1": np.ascontiguousarray(np.asarray(L0_attn_Wg1, np.float32)),
        "wg2": np.ascontiguousarray(np.asarray(L0_attn_Wg2, np.float32)),
        "nw_in": np.ascontiguousarray(np.asarray(L0_attn_norm_weight, np.float32)),
        "bias_in": np.ascontiguousarray(
            np.asarray(L0_ffn_bias, np.float32).reshape(1, D)),
    }
    factors = np.asarray(L0_ffn_factors, np.float32)
    iden = np.eye(128, dtype=np.float32)

    nc = _nc if _nc is not None else _get_nc(_debug)
    in_maps = []
    for core in range(8):
        b, half = core // 2, core % 2
        cos_t, sin_t, dmask, dkt, dqv, corr, lamct = _host_tables(core)
        m = dict(weights)
        m["x_sl"] = np.ascontiguousarray(x[b, half * NT:(half + 1) * NT, :])
        m["fac_sl"] = np.ascontiguousarray(
            factors[:, :, core * 256:(core + 1) * 256])
        m["cos_in"] = cos_t
        m["sin_in"] = sin_t
        m["dmask_in"] = dmask
        m["dkt_in"] = dkt
        m["dqv_in"] = dqv
        m["corr_in"] = corr
        m["lamc_in"] = lamct
        m["idn_in"] = iden
        in_maps.append(m)

    res = run_bass_kernel_spmd(nc, in_maps, core_ids=list(range(8)))
    out = np.empty((4, 4096, D), np.float32)
    for core in range(8):
        b, half = core // 2, core % 2
        out[b, half * NT:(half + 1) * NT, :] = res.results[core]["out_sl"]
    if _debug:
        return out, res
    return out
